# version 14
# baseline (speedup 1.0000x reference)
import contextlib
import numpy as np
import ml_dtypes
import concourse.bass as bass
import concourse.mybir as mybir
from concourse.bass_utils import run_bass_kernel_spmd

F32 = mybir.dt.float32
BF16 = mybir.dt.bfloat16
I32 = mybir.dt.int32
AF = mybir.ActivationFunctionType
ALU = mybir.AluOpType
AX = mybir.AxisListType.X
BF = ml_dtypes.bfloat16

T = 4096
D = 1024
NTT = T // 128
NDC = D // 128


ENGINES = ("sync", "tensor", "vector", "scalar", "gpsimd")


class Env:

    def __init__(self, nc):
        self.nc = nc
        self.es = contextlib.ExitStack()
        self.scopes = [self.es]
        self.steps = []
        self.sems = {}
        self.cnt = {}
        self.uid = 0
        self.final_tokens = []

    @contextlib.contextmanager
    def scope(self):
        s = contextlib.ExitStack()
        self.scopes.append(s)
        try:
            yield
        finally:
            self.scopes.pop()
            s.close()

    def fence(self):
        self.steps.append(("__fence__", None, tuple(self.join_all()), None))

    def name(self, base):
        self.uid += 1
        return f"{base}_{self.uid}"

    def sem(self, name):
        if name not in self.sems:
            self.sems[name] = self.es.enter_context(self.nc.semaphore(name))
            self.cnt[name] = 0
        return name

    def sb(self, base, shape, dt):
        return self.scopes[-1].enter_context(self.nc.sbuf_tensor(self.name(base), shape, dt))

    def psum(self, base, shape, dt=F32):
        return self.scopes[-1].enter_context(self.nc.psum_tensor(self.name(base), shape, dt))

    def step(self, eng, fn, waits=(), inc=None):
        tok = None
        if inc is not None:
            if isinstance(inc, str):
                inc = (inc, 1)
            s, n = inc
            self.sem(s)
            self.cnt[s] += n
            tok = (s, self.cnt[s])
        ws = tuple((s, v) for (s, v) in waits if v > 0)
        for s, _v in ws:
            self.sem(s)
        self.steps.append((eng, fn, ws, inc))
        return tok

    def dma(self, queue, fn, waits=(), sem=None):
        if sem is None:
            sem = {"sync": "d_sy", "scalar": "d_sc", "gpsimd": "d_gp"}[queue]
        return self.step(queue, fn, waits=waits, inc=(sem, 16))

    def emit(self):
        block = self.es.enter_context(self.nc.Block())
        env = self

        def mk2(eng_name):
            def body(eng):
                waited = {}
                pending = {}
                for (e, fn, waits, inc) in env.steps:
                    if e == "__fence__":
                        for (s, v) in waits:
                            if pending.get(s, 0) < v:
                                pending[s] = v
                        continue
                    if e != eng_name:
                        continue
                    if pending:
                        for s, v in pending.items():
                            if waited.get(s, 0) < v:
                                eng.wait_ge(env.sems[s], v)
                                waited[s] = v
                        pending = {}
                    for (s, v) in waits:
                        if waited.get(s, 0) < v:
                            eng.wait_ge(env.sems[s], v)
                            waited[s] = v
                    if fn is None:
                        continue
                    ins = fn(eng)
                    if inc is not None:
                        s, n = inc
                        ins.then_inc(env.sems[s], n)
            return body

        block.sync(mk2("sync"))
        block.tensor(mk2("tensor"))
        block.vector(mk2("vector"))
        block.scalar(mk2("scalar"))
        block.gpsimd(mk2("gpsimd"))

    def join_all(self):
        return [(s, v) for s, v in self.cnt.items()]

    def wait_only(self, eng, waits):
        self.steps.append((eng, None, tuple((s, v) for (s, v) in waits if v > 0), None))
        for s, _v in waits:
            self.sem(s)



def emit_consts(env):
    nc = env.nc
    idf = env.sb("idf", [128, 128], F32)
    idb = env.sb("idb", [128, 128], BF16)
    ones_f = env.sb("ones_f", [1, 128], F32)
    ones_b = env.sb("ones_b", [1, 128], BF16)
    t_io = env.step("gpsimd", lambda e: e.iota(idf[:], [[1, 128]], channel_multiplier=-1,
                                              allow_small_or_imprecise_dtypes=True),
                    inc="s_po")
    t_id = env.step("vector", lambda e: e.tensor_scalar(out=idf[:], in0=idf[:],
                    scalar1=0.0, scalar2=1.0, op0=ALU.is_equal, op1=ALU.mult),
                    waits=[t_io], inc="s_ve")
    t_idb = env.step("vector", lambda e: e.tensor_copy(out=idb[:], in_=idf[:]),
                     waits=[t_id], inc="s_ve")
    t_o1 = env.step("vector", lambda e: e.memset(ones_f[:], 1.0), inc="s_ve")
    t_o2 = env.step("vector", lambda e: e.memset(ones_b[:], 1.0), inc="s_ve")
    ocb = env.sb("ocb", [128, 1], BF16)
    t_o3 = env.step("vector", lambda e: e.memset(ocb[:], 1.0), inc="s_ve")
    return dict(idf=idf, idb=idb, ones_f=ones_f, ones_b=ones_b, ocb=ocb,
                tok=[t_id, t_idb, t_o1, t_o2, t_o3])



def emit_conv(env, pairs):
    toks = []
    for src, dst in pairs:
        R, C = src.shape
        rows_per = max(1, (4 << 20) // (C * 4))
        r = 0
        while r < R:
            r1 = min(R, r + rows_per)
            toks.append(env.dma("gpsimd",
                lambda e, src=src, dst=dst, r=r, r1=r1:
                    e.dma_start(out=dst[r:r1, :], in_=src[r:r1, :])))
            r = r1
    flush = env.sb("convflush", [1, 64], BF16)
    src0, dst0 = pairs[0]
    nfl = min(64, dst0.shape[1])
    toks.append(env.dma("gpsimd", lambda e, nfl=nfl: e.dma_start(
        out=flush[0:1, 0:nfl], in_=dst0[0:1, 0:nfl])))
    return toks



def emit_lnT(env, x_in, h_fm, waits=(), apply_fn=None, post_fn=None):
    nc = env.nc
    xt = [env.sb("ln_xt", [128, D], F32) for _ in range(2)]
    dump = env.sb("ln_dump", [128, D], F32)
    col = [env.sb("ln_col", [128, 16], F32) for _ in range(2)]
    lnb = [env.sb("ln_lnb", [128, D], BF16) for _ in range(2)]
    pst = env.psum("ln_pst", [128, 2 * D], BF16)

    done = []
    prev_evac = [None, None]
    prev_col = [None, None]
    prev_sq = [None, None]
    for tt in range(NTT):
        par = tt % 2
        w = list(waits) if tt < 2 else []
        if prev_evac[par] is not None:
            w.append(prev_evac[par])
        t_x = env.dma("sync", lambda e, tt=tt, par=par: e.dma_start(
            out=xt[par][:], in_=x_in[tt * 128:(tt + 1) * 128, :]), waits=w,
            sem=f"d_ln{par}")
        cw = [t_x] + ([prev_col[par]] if prev_col[par] else [])
        c = col[par]
        sx2 = c[:, 0:1]; sx = c[:, 1:2]; m = c[:, 2:3]; m2 = c[:, 3:4]
        v = c[:, 4:5]; sdev = c[:, 5:6]; r = c[:, 6:7]
        sqw = cw + ([prev_sq[par]] if prev_sq[par] else [])
        t_sq = env.step("scalar", lambda e, par=par, sx2=sx2: e.activation(
            out=dump[:], in_=xt[par][:], func=AF.Square, accum_out=sx2), waits=sqw, inc="s_se")
        prev_sq[par] = t_sq
        t_sm = env.step("vector", lambda e, par=par, sx=sx: e.reduce_sum(
            out=sx, in_=xt[par][:], axis=AX), waits=cw, inc="s_ve")
        t_m = env.step("vector", lambda e, sx=sx, m=m: e.tensor_scalar(
            out=m, in0=sx, scalar1=1.0 / D, scalar2=1.0, op0=ALU.mult, op1=ALU.mult),
            waits=[t_sm], inc="s_ve")
        t_m2 = env.step("vector", lambda e, m=m, m2=m2: e.tensor_scalar(
            out=m2, in0=m, scalar1=m, scalar2=1e-5, op0=ALU.mult, op1=ALU.subtract),
            waits=[t_m], inc="s_ve")
        t_v = env.step("vector", lambda e, sx2=sx2, m2=m2, v=v: e.scalar_tensor_tensor(
            out=v, in0=sx2, scalar=1.0 / D, in1=m2, op0=ALU.mult, op1=ALU.subtract),
            waits=[t_sq, t_m2], inc="s_ve")
        t_sd = env.step("scalar", lambda e, v=v, sdev=sdev: e.activation(
            out=sdev, in_=v, func=AF.Sqrt), waits=[t_v], inc="s_se")
        t_r = env.step("vector", lambda e, sdev=sdev, r=r: e.reciprocal(
            out=r, in_=sdev), waits=[t_sd], inc="s_ve")
        if apply_fn is None:
            t_ln = env.step("vector", lambda e, par=par, m=m, r=r: e.tensor_scalar(
                out=lnb[par][:], in0=xt[par][:], scalar1=m, scalar2=r,
                op0=ALU.subtract, op1=ALU.mult), waits=[t_r], inc="s_ve")
        else:
            t_ln = apply_fn(tt, par, xt[par], lnb[par], c, dict(t_r=t_r, t_x=t_x))
        prev_col[par] = t_ln
        trw = [t_ln] + ([prev_evac[par]] if prev_evac[par] else [])
        t_tr = None
        for cch in range(NDC):
            t_tr = env.step("tensor", lambda e, par=par, cch=cch: e.transpose(
                pst[:, par * D + cch * 128: par * D + (cch + 1) * 128],
                lnb[par][:, cch * 128:(cch + 1) * 128], env.consts["idb"][:]),
                waits=trw if cch == 0 else [], inc="s_te" if cch == NDC - 1 else None)
        hv = h_fm.rearrange("p (c t) -> p c t", c=NDC)
        pv = pst.rearrange("p (b c f) -> p b c f", b=2, c=NDC)
        t_ev = env.step("vector", lambda e, par=par, tt=tt, hv=hv, pv=pv: e.tensor_copy(
            out=hv[:, :, tt * 128:(tt + 1) * 128], in_=pv[:, par]),
            waits=[t_tr], inc="s_ve")
        prev_evac[par] = t_ev
        if post_fn is not None:
            post_fn(tt, par, t_ev)
        done.append(t_ev)
    return done[-2:]



def emit_ffn_core(env, h_fm, x_in, x_out, E, wg_fn, wu_fn, wd_fn, HID, gate_scale,
                  res_scale, waits=(), cbc_src=None):
    nc = env.nc
    NHC = HID // 128
    NB = T // 512
    NG = NB * E
    NWB = 4
    assert NHC % 2 == 0
    wg_buf = [env.sb("f_wg", [128, NDC * 256], BF16) for _ in range(NWB)]
    wu_buf = [env.sb("f_wu", [128, NDC * 256], BF16) for _ in range(NWB)]
    wdq = [env.sb("f_wdq", [128, 2 * NHC * 128], BF16) for _ in range(2)]
    hid = [env.sb("f_hid", [128, NHC * 512], BF16) for _ in range(2)]
    sg = [env.sb("f_sg", [128, 512], F32) for _ in range(2)]
    tmp16 = [env.sb("f_tmp16", [128, 512], BF16) for _ in range(2)] if cbc_src is not None else None
    cbc = [env.sb("f_cbc", [128, 512], BF16) for _ in range(2)] if cbc_src is not None else None
    crow = env.sb("f_crow", [1, 512], F32) if cbc_src is not None else None
    acc = env.sb("f_acc", [128, 4096], F32)
    x_blk = env.sb("f_xblk", [128, 2048], F32)
    ps = env.psum("f_ps", [128, 4096], F32)

    def psb(i):
        return ps[:, i * 512:(i + 1) * 512]

    hfv = h_fm.rearrange("p (c t) -> p c t", c=NDC)
    wdqv = [w.rearrange("p (d hc j) -> p d hc j", d=2, hc=NHC) for w in wdq]

    fin = []
    st = {
        "hid_user": [None, None],
        "gate_bank_user": [None] * 4,
        "pair_user": [None, None],
        "x_wb": None,
        "sg_user": [None, None],
        "tmp_user": [None, None],
        "cbc_user": [None, None],
        "cbc_tok": [None, None],
        "crow_user": None,
        "wg_used": [None] * NWB,
        "wdq_mm": [None, None],
        "wdq_tok": [None] * 4,
        "hid_toks": [None, None],
        "acc_toks": {},
        "wpar": 0,
    }

    def emit_cbc(g, extra_waits=()):
        blk_, e_ = divmod(g, E)
        idx = g % 2
        ctv = cbc_src.rearrange("p (tt ex) -> p tt ex", ex=8)
        cw = [x for x in (st["pair_user"][1],) if x] + list(extra_waits)
        t_ctr = None
        for j in range(4):
            t_ctr = env.step("tensor", lambda e2, j=j, e_=e_, blk_=blk_: e2.matmul(
                ps[0:1, 3072 + j * 128: 3072 + (j + 1) * 128],
                ctv[:, 4 * blk_ + j, e_:e_ + 1], env.consts["idf"][:],
                start=True, stop=True),
                waits=cw if j == 0 else [], inc="s_te" if j == 3 else None)
        crw = [t_ctr] + ([st["crow_user"]] if st["crow_user"] else [])
        t_crow = env.step("vector", lambda e2: e2.tensor_copy(
            out=crow[:], in_=ps[0:1, 3072:3584]), waits=crw, inc="s_ve")
        t_cmm = env.step("tensor", lambda e2: e2.matmul(
            psb(6), env.consts["ones_f"][:], crow[:],
            start=True, stop=True), waits=[t_crow], inc="s_te")
        st["crow_user"] = t_cmm
        ccw = [t_cmm] + ([st["cbc_user"][idx]] if st["cbc_user"][idx] else [])
        t_cbc = env.step("vector", lambda e2, idx=idx: e2.tensor_copy(
            out=cbc[idx][:], in_=psb(6)), waits=ccw, inc="s_ve")
        st["cbc_tok"][idx] = t_cbc
        return t_cbc

    def emit_gate(g, first_waits=()):
        blk, e = divmod(g, E)
        gpar = g % 2
        t_cbc = st["cbc_tok"][gpar] if cbc_src is not None else None
        t_hid = {}
        for p2 in range(NHC // 2):
            par = st["wpar"]
            st["wpar"] = (st["wpar"] + 1) % NWB
            w_w = [st["wg_used"][par]] if st["wg_used"][par] else []
            if g == 0 and p2 < 2:
                w_w += list(first_waits)
            t_lg = env.dma("sync", lambda e_, p2=p2, par=par, e=e: e_.dma_start(
                out=wg_buf[par][:], in_=wg_fn(e, p2)), waits=w_w, sem=f"d_wg{par}")
            t_lu = env.dma("scalar", lambda e_, p2=p2, par=par, e=e: e_.dma_start(
                out=wu_buf[par][:], in_=wu_fn(e, p2)), waits=w_w, sem=f"d_wu{par}")
            for i in range(2):
                hc = 2 * p2 + i
                bg = (hc % 2) * 2
                bu = bg + 1
                mmw = [t_lg, t_lu]
                for b in (bg, bu):
                    if st["gate_bank_user"][b]:
                        mmw.append(st["gate_bank_user"][b])
                t_g = t_u = None
                for c in range(NDC):
                    t_g = env.step("tensor", lambda e_, c=c, i=i, bg=bg, par=par, blk=blk:
                        e_.matmul(psb(bg), wg_buf[par][:, c * 256 + i * 128: c * 256 + (i + 1) * 128],
                                  hfv[:, c, blk * 512:(blk + 1) * 512],
                                  start=(c == 0), stop=(c == NDC - 1)),
                        waits=mmw if c == 0 else [], inc="s_te" if c == NDC - 1 else None)
                for c in range(NDC):
                    t_u = env.step("tensor", lambda e_, c=c, i=i, bu=bu, par=par, blk=blk:
                        e_.matmul(psb(bu), wu_buf[par][:, c * 256 + i * 128: c * 256 + (i + 1) * 128],
                                  hfv[:, c, blk * 512:(blk + 1) * 512],
                                  start=(c == 0), stop=(c == NDC - 1)),
                        waits=[], inc="s_te" if c == NDC - 1 else None)
                if i == 1:
                    st["wg_used"][par] = t_u
                spar = hc % 2
                sgw = [t_g] + ([st["sg_user"][spar]] if st["sg_user"][spar] else [])
                t_si = env.step("scalar", lambda e_, spar=spar, bg=bg: e_.activation(
                    out=sg[spar][:], in_=psb(bg), func=AF.Silu, scale=gate_scale),
                    waits=sgw, inc="s_se")
                st["gate_bank_user"][bg] = t_si
                hu = st["hid_user"][gpar]
                if cbc_src is None:
                    hw_ = [t_si, t_u] + ([hu] if hu else [])
                    t_h = env.step("vector", lambda e_, gpar=gpar, bu=bu, hc=hc, spar=spar:
                        e_.tensor_tensor(
                            out=hid[gpar][:, hc * 512:(hc + 1) * 512], in0=sg[spar][:],
                            in1=psb(bu), op=ALU.mult), waits=hw_, inc="s_ve")
                    st["gate_bank_user"][bu] = t_h
                    st["sg_user"][spar] = t_h
                else:
                    tw = [t_si, t_u] + ([st["tmp_user"][spar]] if st["tmp_user"][spar] else [])
                    t_t = env.step("vector", lambda e_, spar=spar, bu=bu: e_.tensor_tensor(
                        out=tmp16[spar][:], in0=sg[spar][:], in1=psb(bu),
                        op=ALU.mult), waits=tw, inc="s_ve")
                    st["gate_bank_user"][bu] = t_t
                    st["sg_user"][spar] = t_t
                    pw = [t_t, t_cbc] + ([hu] if hu else [])
                    t_h = env.step("gpsimd", lambda e_, spar=spar, hc=hc, gpar=gpar: e_.tensor_tensor(
                        out=hid[gpar][:, hc * 512:(hc + 1) * 512], in0=tmp16[spar][:],
                        in1=cbc[gpar][:],
                        op=ALU.mult), waits=pw, inc="s_po")
                    st["tmp_user"][spar] = t_h
                t_hid[hc] = t_h
        if cbc_src is not None:
            st["cbc_user"][gpar] = t_hid[NHC - 1]
        st["hid_toks"][gpar] = t_hid

    def emit_wdq(g, p, extra=()):
        pr2 = p % 2
        dw_w = [st["wdq_mm"][pr2]] if st["wdq_mm"][pr2] else []
        dw_w += list(extra)
        seg = 2 * NHC * 128
        st["wdq_tok"][p] = env.dma("gpsimd", lambda e_, g=g, p=p, pr2=pr2, seg=seg:
            e_.dma_start(out=wdq[pr2][:],
                         in_=wd_fn(g % E)[:, p * seg:(p + 1) * seg]),
            waits=dw_w, sem=f"d_wd{pr2}")

    def emit_down(g, t_cbc_next):
        blk, e = divmod(g, E)
        gpar = g % 2
        t_hid = st["hid_toks"][gpar]
        t_down = None
        for p in range(4):
            pr = p % 2
            b0 = 4 + 2 * pr
            t_down = None
            for hc in range(NHC):
                mmw = [st["wdq_tok"][p], t_hid[hc]]
                if hc == 0 and st["pair_user"][pr]:
                    mmw.append(st["pair_user"][pr])
                if hc == 0 and pr == 1 and t_cbc_next is not None:
                    mmw.append(t_cbc_next)
                for d in range(2):
                    doc = 2 * p + d
                    t_down = env.step("tensor", lambda e_, hc=hc, doc=doc, d=d, b0=b0, pr=pr:
                        e_.matmul(psb(b0 + d), wdqv[pr][:, d, hc, :],
                                  hid[gpar][:, hc * 512:(hc + 1) * 512],
                                  start=(hc == 0), stop=(hc == NHC - 1)),
                        waits=mmw if d == 0 else [],
                        inc="s_te" if (d == 1) else None)
            st["wdq_mm"][pr] = t_down
            if p < 2:
                emit_wdq(g, p + 2)
            for d in range(2):
                doc = 2 * p + d
                if e == 0:
                    st["acc_toks"][doc] = env.step("vector", lambda e_, doc=doc, d=d, b0=b0:
                        e_.tensor_copy(
                            out=acc[:, doc * 512:(doc + 1) * 512], in_=psb(b0 + d)),
                        waits=[t_down], inc="s_ve")
                else:
                    st["acc_toks"][doc] = env.step("vector", lambda e_, doc=doc, d=d, b0=b0:
                        e_.tensor_tensor(
                            out=acc[:, doc * 512:(doc + 1) * 512],
                            in0=acc[:, doc * 512:(doc + 1) * 512], in1=psb(b0 + d),
                            op=ALU.add),
                        waits=[t_down], inc="s_ve")
            st["pair_user"][pr] = st["acc_toks"][2 * p + 1]
        st["hid_user"][gpar] = t_down

    def emit_tail(blk):
        t_adds = []
        for rnd in range(2):
            xw = [st["x_wb"]] if st["x_wb"] else []
            xv = x_blk.rearrange("p (a d) -> p a d", a=2)
            t_xl = env.dma("gpsimd", lambda e_, blk=blk, rnd=rnd, xv=xv: e_.dma_start(
                out=xv[:], in_=x_in[blk * 512 + rnd * 256: blk * 512 + (rnd + 1) * 256, :]
                .rearrange("(a p) d -> p a d", p=128)), waits=xw, sem="d_fxl")
            t_tr = {}
            t_add = {}
            for dt in (2 * rnd, 2 * rnd + 1):
                base = 2048 + (dt % 2) * 1024
                for doc in range(8):
                    trw = []
                    if dt == 2 * rnd and doc == 0:
                        if rnd == 0:
                            trw = [st["acc_toks"][b] for b in range(8)] + \
                                  [x for x in st["pair_user"] if x]
                        else:
                            trw = t_adds[-2:]
                    t_tr[(dt, doc)] = env.step("tensor", lambda e_, dt=dt, doc=doc, base=base:
                        e_.transpose(
                            ps[:, base + doc * 128: base + (doc + 1) * 128],
                            acc[:, doc * 512 + dt * 128: doc * 512 + (dt + 1) * 128],
                            env.consts["idf"][:]),
                        waits=trw, inc="s_te" if doc == 7 else None)
            for li, dt in enumerate((2 * rnd, 2 * rnd + 1)):
                base = 2048 + (dt % 2) * 1024
                t_add[dt] = env.step("vector", lambda e_, li=li, base=base:
                    e_.scalar_tensor_tensor(
                        out=x_blk[:, li * 1024:(li + 1) * 1024], in0=ps[:, base: base + 1024],
                        scalar=res_scale, in1=x_blk[:, li * 1024:(li + 1) * 1024],
                        op0=ALU.mult, op1=ALU.add),
                    waits=[t_tr[(dt, 7)], t_xl], inc="s_ve")
                t_adds.append(t_add[dt])
            t_wb = env.dma("gpsimd", lambda e_, blk=blk, rnd=rnd, xv=xv: e_.dma_start(
                out=x_out[blk * 512 + rnd * 256: blk * 512 + (rnd + 1) * 256, :]
                .rearrange("(a p) d -> p a d", p=128),
                in_=xv[:]), waits=[t_add[2 * rnd], t_add[2 * rnd + 1]], sem="d_xw")
            st["x_wb"] = t_wb
        for pr in range(2):
            st["pair_user"][pr] = t_adds[-1]
        fin.append(st["x_wb"])

    for g in range(NG):
        if cbc_src is not None and g == 0:
            emit_cbc(0, extra_waits=list(waits))
        emit_gate(g, first_waits=waits)
        t_cbc_next = emit_cbc(g + 1) if (cbc_src is not None and g + 1 < NG) else None
        if g >= 1:
            emit_down(g - 1, t_cbc_next)
            if g % E == 0:
                emit_tail(g // E - 1)
        emit_wdq(g, 0, extra=list(waits) if g == 0 else ())
        emit_wdq(g, 1)
    emit_down(NG - 1, None)
    emit_tail(NB - 1)
    return fin


def emit_ffn_swiglu(env, h_fm, x_in, x_out, wgT, wuT, wdT, HID, gate_scale, res_scale,
                    waits=()):
    return emit_ffn_core(env, h_fm, x_in, x_out, 1,
                         lambda e, p2: wgT[p2 * 128:(p2 + 1) * 128, :],
                         lambda e, p2: wuT[p2 * 128:(p2 + 1) * 128, :],
                         lambda e: wdT, HID, gate_scale, res_scale, waits=waits)



def emit_moe_sublayer(env, x_in, x_out, c_wgu, c_wd0, routerT_hbm, nw_hbm, alpha_hbm,
                      normed_fm, combine_tm, waits=()):
    nc = env.nc
    with env.scope():
        nw_bc = env.sb("m_nwbc", [128, D], F32)
        alpha_col = env.sb("m_alpha", [128, 1], F32)
        router_sb = env.sb("m_router", [128, NDC * 8], BF16)
        nw_row = env.sb("m_nwrow", [1, D], F32)
        al_row = env.sb("m_alrow", [1, 2], F32)
        nrm = [env.sb("m_nrm", [128, D], F32) for _ in range(2)]
        eq1 = env.sb("m_eq1", [128, 8], F32)
        lm = env.sb("m_lm", [128, 8], F32)
        eq2 = env.sb("m_eq2", [128, 8], F32)
        rcol = env.sb("m_rcol", [128, 4], F32)
        ps_r = env.psum("m_psr", [128, 8], F32)
        ps_b = env.psum("m_psb", [128, 512], F32)

        rv = router_sb.rearrange("p (c e) -> p c e", c=NDC)
        t_rt = env.dma("sync", lambda e: e.dma_start(
            out=rv[:], in_=routerT_hbm.rearrange("(c p) e -> p c e", p=128)), waits=waits,
            sem="d_m0r")
        t_nw = env.dma("sync", lambda e: e.dma_start(out=nw_row[:], in_=nw_hbm[:]),
                       sem="d_m0n")
        t_al = env.dma("sync", lambda e: e.dma_start(out=al_row[0:1, 0:1], in_=alpha_hbm[:]),
                       sem="d_m0a")
        t_bc1e = None
        for half in range(2):
            bw = [t_nw] + list(env.consts["tok"]) if half == 0 else [t_bc1e]
            t_bc1 = env.step("tensor", lambda e, half=half: e.matmul(
                ps_b[:], env.consts["ones_f"][:], nw_row[0:1, half * 512:(half + 1) * 512],
                start=True, stop=True), waits=bw, inc="s_te")
            t_bc1e = env.step("vector", lambda e, half=half: e.tensor_copy(
                out=nw_bc[:, half * 512:(half + 1) * 512], in_=ps_b[:]),
                waits=[t_bc1], inc="s_ve")
        t_nwbc = t_bc1e
        t_alc = env.step("vector", lambda e: e.tensor_scalar(
            out=al_row[0:1, 1:2], in0=al_row[0:1, 0:1], scalar1=2.0, scalar2=-0.5,
            op0=ALU.min, op1=ALU.max), waits=[t_al], inc="s_ve")
        t_alh = env.step("vector", lambda e: e.tensor_scalar(
            out=al_row[0:1, 0:1], in0=al_row[0:1, 1:2], scalar1=0.5, scalar2=1.0,
            op0=ALU.mult, op1=ALU.mult), waits=[t_alc], inc="s_ve")
        t_amm = env.step("tensor", lambda e: e.matmul(
            ps_b[:, 0:1], env.consts["ones_f"][:], al_row[0:1, 0:1], start=True, stop=True),
            waits=[t_alh, t_nwbc], inc="s_te")
        t_acol = env.step("vector", lambda e: e.tensor_copy(
            out=alpha_col[:], in_=ps_b[:, 0:1]), waits=[t_amm], inc="s_ve")

        state = {"router_prev": t_acol, "wb": []}

        def apply_fn(tt, par, xt, lnb, c, toks):
            t_r = toks["t_r"]
            r = c[:, 6:7]; m = c[:, 2:3]
            rq = c[:, 7:8]; rr = c[:, 8:9]; srr = c[:, 9:10]; rrms = c[:, 10:11]
            s_n = c[:, 11:12]; s_h = c[:, 12:13]
            t1 = env.step("vector", lambda e, r=r, rq=rq: e.tensor_scalar(
                out=rq, in0=r, scalar1=r, scalar2=-1e-5, op0=ALU.mult, op1=ALU.mult),
                waits=[t_r], inc="s_ve")
            t2 = env.step("vector", lambda e, rq=rq, rr=rr: e.tensor_scalar(
                out=rr, in0=rq, scalar1=1.0 + 1e-6, scalar2=1.0, op0=ALU.add, op1=ALU.mult),
                waits=[t1], inc="s_ve")
            t3 = env.step("scalar", lambda e, rr=rr, srr=srr: e.activation(
                out=srr, in_=rr, func=AF.Sqrt), waits=[t2], inc="s_se")
            t4 = env.step("vector", lambda e, srr=srr, rrms=rrms: e.reciprocal(
                out=rrms, in_=srr), waits=[t3], inc="s_ve")
            t5 = env.step("vector", lambda e, r=r, rrms=rrms, s_n=s_n: e.tensor_scalar(
                out=s_n, in0=r, scalar1=rrms, scalar2=1.0, op0=ALU.mult, op1=ALU.mult),
                waits=[t4], inc="s_ve")
            t6 = env.step("vector", lambda e, r=r, s_h=s_h: e.tensor_scalar(
                out=s_h, in0=r, scalar1=0.5, scalar2=1.0, op0=ALU.mult, op1=ALU.mult),
                waits=[t_r], inc="s_ve")
            t_nr = env.step("vector", lambda e, par=par, m=m, s_n=s_n: e.tensor_scalar(
                out=nrm[par][:], in0=xt[:], scalar1=m, scalar2=s_n,
                op0=ALU.subtract, op1=ALU.mult), waits=[t5], inc="s_ve")
            t_nb = env.step("vector", lambda e, par=par: e.tensor_tensor(
                out=lnb[:], in0=nrm[par][:], in1=nw_bc[:], op=ALU.mult),
                waits=[t_nr, t_nwbc], inc="s_ve")
            t_t1 = env.step("vector", lambda e, par=par, m=m, s_h=s_h: e.tensor_scalar(
                out=nrm[par][:], in0=xt[:], scalar1=m, scalar2=s_h,
                op0=ALU.subtract, op1=ALU.mult), waits=[t6, t_nb], inc="s_ve")
            t_xu = env.step("vector", lambda e, par=par: e.tensor_tensor(
                out=xt[:], in0=xt[:], in1=nrm[par][:], op=ALU.add),
                waits=[t_t1], inc="s_ve")
            t_wb = env.dma("sync", lambda e, tt=tt: e.dma_start(
                out=x_out[tt * 128:(tt + 1) * 128, :], in_=xt[:]), waits=[t_xu])
            state["wb"].append(t_wb)
            return t_nb

        hv = normed_fm.rearrange("p (c t) -> p c t", c=NDC)

        def post_fn(tt, par, t_ev):
            prev = state["router_prev"]
            t_mm = None
            for cch in range(NDC):
                t_mm = env.step("tensor", lambda e, cch=cch, tt=tt: e.matmul(
                    ps_r[:], hv[:, cch, tt * 128:(tt + 1) * 128],
                    router_sb[:, cch * 8:(cch + 1) * 8],
                    start=(cch == 0), stop=(cch == NDC - 1)),
                    waits=[t_ev, t_rt, prev] if cch == 0 else [],
                    inc="s_te" if cch == NDC - 1 else None)
            rm1 = rcol[:, 0:1]; rm2 = rcol[:, 1:2]; w1 = rcol[:, 2:3]; w2 = rcol[:, 3:4]
            t_m1 = env.step("vector", lambda e: e.reduce_max(out=rm1, in_=ps_r[:], axis=AX),
                            waits=[t_mm], inc="s_ve")
            t_e1 = env.step("vector", lambda e: e.tensor_scalar(
                out=eq1[:], in0=ps_r[:], scalar1=rm1, scalar2=1.0,
                op0=ALU.is_equal, op1=ALU.mult), waits=[t_m1], inc="s_ve")
            t_lm = env.step("vector", lambda e: e.scalar_tensor_tensor(
                out=lm[:], in0=eq1[:], scalar=-1e30, in1=ps_r[:], op0=ALU.mult, op1=ALU.add),
                waits=[t_e1], inc="s_ve")
            t_m2 = env.step("vector", lambda e: e.reduce_max(out=rm2, in_=lm[:], axis=AX),
                            waits=[t_lm], inc="s_ve")
            t_d = env.step("vector", lambda e: e.tensor_tensor(
                out=w1, in0=rm1, in1=rm2, op=ALU.subtract), waits=[t_m2], inc="s_ve")
            t_w1 = env.step("scalar", lambda e: e.activation(
                out=w1, in_=w1, func=AF.Sigmoid), waits=[t_d], inc="s_se")
            t_w2 = env.step("vector", lambda e: e.tensor_scalar(
                out=w2, in0=w1, scalar1=-1.0, scalar2=1.0, op0=ALU.mult, op1=ALU.add),
                waits=[t_w1], inc="s_ve")
            t_e2 = env.step("vector", lambda e: e.tensor_scalar(
                out=eq2[:], in0=lm[:], scalar1=rm2, scalar2=1.0,
                op0=ALU.is_equal, op1=ALU.mult), waits=[t_m2], inc="s_ve")
            cslot = combine_tm[:, tt * 8:(tt + 1) * 8]
            t_c1 = env.step("vector", lambda e, cslot=cslot: e.tensor_scalar(
                out=cslot, in0=eq1[:], scalar1=w1, scalar2=1.0, op0=ALU.mult, op1=ALU.mult),
                waits=[t_w1], inc="s_ve")
            t_c2 = env.step("vector", lambda e, cslot=cslot: e.scalar_tensor_tensor(
                out=cslot, in0=eq2[:], scalar=w2, in1=cslot, op0=ALU.mult, op1=ALU.add),
                waits=[t_c1, t_e2, t_w2], inc="s_ve")
            t_c3 = env.step("vector", lambda e, cslot=cslot: e.tensor_scalar(
                out=cslot, in0=cslot, scalar1=alpha_col[:, 0:1], scalar2=1.0,
                op0=ALU.mult, op1=ALU.mult), waits=[t_c2, t_acol], inc="s_ve")
            state["router_prev"] = t_c3

        emit_lnT(env, x_in, normed_fm, waits=env.consts["tok"],
                 apply_fn=apply_fn, post_fn=post_fn)
    env.fence()
    with env.scope():
        fin = emit_ffn_core(
            env, normed_fm, x_out, x_out, 8,
            lambda e, p2: c_wgu[((e * 2 + 0) * 11 + p2) * 128: ((e * 2 + 0) * 11 + p2 + 1) * 128, :],
            lambda e, p2: c_wgu[((e * 2 + 1) * 11 + p2) * 128: ((e * 2 + 1) * 11 + p2 + 1) * 128, :],
            lambda e: c_wd0[e * 128:(e + 1) * 128, :],
            HID=2816, gate_scale=1.0, res_scale=1.0,
            waits=[], cbc_src=combine_tm)
    return fin



def emit_proj_residual(env, attn_all, w_sb, bo_bc, x_in, x_out, waits=()):
    acc = env.sb("pr_acc", [128, 4096], F32)
    x_blk = env.sb("pr_xblk", [128, 4096], F32)
    ps = env.psum("pr_ps", [128, 4096], F32)
    av = attn_all.rearrange("p (c t) -> p c t", c=NDC)
    wv = w_sb.rearrange("p (c j) -> p c j", c=NDC)
    fin = []
    prev_bank_user = [None] * 8
    prev_x_wb = None
    for blk in range(T // 512):
        t_mm = {}
        for doc in range(8):
            mmw = list(waits) if blk == 0 and doc == 0 else []
            if prev_bank_user[doc]:
                mmw.append(prev_bank_user[doc])
            t = None
            for c in range(NDC):
                t = env.step("tensor", lambda e, c=c, doc=doc, blk=blk: e.matmul(
                    ps[:, doc * 512:(doc + 1) * 512],
                    wv[:, c, doc * 128:(doc + 1) * 128],
                    av[:, c, blk * 512:(blk + 1) * 512],
                    start=(c == 0), stop=(c == NDC - 1)),
                    waits=mmw if c == 0 else [], inc="s_te" if c == NDC - 1 else None)
            t_mm[doc] = t
        t_ev = {}
        for doc in range(8):
            t_ev[doc] = env.step("vector", lambda e, doc=doc: e.tensor_copy(
                out=acc[:, doc * 512:(doc + 1) * 512], in_=ps[:, doc * 512:(doc + 1) * 512]),
                waits=[t_mm[doc]], inc="s_ve")
        xw = [prev_x_wb] if prev_x_wb else []
        xv = x_blk.rearrange("p (a d) -> p a d", a=4)
        t_xl = env.dma("sync", lambda e, blk=blk, xv=xv: e.dma_start(
            out=xv[:], in_=x_in[blk * 512:(blk + 1) * 512, :]
            .rearrange("(a p) d -> p a d", p=128)), waits=xw, sem="d_pxl")
        t_tr = {}
        for dt in range(4):
            for doc in range(8):
                trw = [t_ev[b] for b in range(8)] if (dt == 0 and doc == 0) else []
                t_tr[(dt, doc)] = env.step("tensor", lambda e, dt=dt, doc=doc: e.transpose(
                    ps[:, dt * 1024 + doc * 128: dt * 1024 + (doc + 1) * 128],
                    acc[:, doc * 512 + dt * 128: doc * 512 + (dt + 1) * 128],
                    env.consts["idf"][:]),
                    waits=trw, inc="s_te" if doc == 7 else None)
        t_add = {}
        for dt in range(4):
            t_add[dt] = env.step("vector", lambda e, dt=dt: e.scalar_tensor_tensor(
                out=x_blk[:, dt * 1024:(dt + 1) * 1024], in0=ps[:, dt * 1024:(dt + 1) * 1024],
                scalar=1.0, in1=x_blk[:, dt * 1024:(dt + 1) * 1024],
                op0=ALU.mult, op1=ALU.add),
                waits=[t_tr[(dt, 7)], t_xl], inc="s_ve")
            if bo_bc is not None:
                t_add[dt] = env.step("vector", lambda e, dt=dt: e.tensor_tensor(
                    out=x_blk[:, dt * 1024:(dt + 1) * 1024],
                    in0=x_blk[:, dt * 1024:(dt + 1) * 1024], in1=bo_bc[:],
                    op=ALU.add), waits=[t_add[dt]], inc="s_ve")
        for b in range(8):
            prev_bank_user[b] = t_add[3]
        t_wb = env.dma("sync", lambda e, blk=blk, xv=xv: e.dma_start(
            out=x_out[blk * 512:(blk + 1) * 512, :].rearrange("(a p) d -> p a d", p=128),
            in_=xv[:]), waits=[t_add[dt] for dt in range(4)])
        prev_x_wb = t_wb
        fin.append(t_wb)
    return fin


def emit_attn0(env, h_fm, x_in, x_out, c_wq, c_wk, c_wv, c_wo, bo_hbm, waits=()):
    nc = env.nc
    NH, DH = 16, 64
    PADT = 128
    attn_all = env.sb("a0_attn", [128, NDC * T], BF16)
    with env.scope():
        kpad = [env.sb("a0_kpad", [64, T + 2 * PADT], BF16) for _ in range(2)]
        vtm = [env.sb("a0_vtm", [128, (NTT + 2) * DH], BF16) for _ in range(2)]
        qh = [env.sb("a0_qh", [64, T], BF16) for _ in range(2)]
        wslc = [env.sb("a0_wslc", [128, 3 * NDC * DH], BF16) for _ in range(2)]
        vstg = env.sb("a0_vstg", [64, 512], BF16)
        mask = env.sb("a0_mask", [128, 384], F32)
        scf = [env.sb("a0_scf", [128, 384], F32) for _ in range(2)]
        p16 = [env.sb("a0_p16", [128, 384], BF16) for _ in range(2)]
        ptb = [env.sb("a0_ptb", [128, 384], BF16) for _ in range(2)]
        atmp = [env.sb("a0_atmp", [128, DH], BF16) for _ in range(2)]
        colb = [env.sb("a0_col", [128, 4], F32) for _ in range(2)]
        ps_p = env.psum("a0_psp", [64, 512], F32)
        ps_vt = env.psum("a0_psvt", [128, 2 * DH], BF16)
        ps_s = env.psum("a0_pss", [128, 2 * 512], F32)
        ps_pt = env.psum("a0_pspt", [128, 2 * 1024], BF16)
        ps_av = env.psum("a0_psav", [128, 2 * DH], F32)
        ps_o = env.psum("a0_pso", [64, 2 * 128], BF16)

        t_i0 = env.step("gpsimd", lambda e: e.iota(mask[:], [[1, 384]], channel_multiplier=-1,
                        allow_small_or_imprecise_dtypes=True), waits=waits, inc="s_po")
        t_i1 = env.step("vector", lambda e: e.tensor_scalar(
            out=scf[0][:], in0=mask[:], scalar1=96.0, scalar2=1.0,
            op0=ALU.is_ge, op1=ALU.mult), waits=[t_i0], inc="s_ve")
        t_i1b = env.step("vector", lambda e: e.tensor_scalar(
            out=scf[1][:], in0=mask[:], scalar1=159.0, scalar2=1.0,
            op0=ALU.is_le, op1=ALU.mult), waits=[t_i0], inc="s_ve")
        t_i2 = env.step("vector", lambda e: e.tensor_tensor(
            out=mask[:], in0=scf[0][:], in1=scf[1][:], op=ALU.mult),
            waits=[t_i1, t_i1b], inc="s_ve")
        t_msk = env.step("vector", lambda e: e.tensor_scalar(
            out=mask[:], in0=mask[:], scalar1=1.0, scalar2=1e30,
            op0=ALU.subtract, op1=ALU.mult), waits=[t_i2], inc="s_ve")

        fin_heads = []
        guards = {"kv": [None, None], "score_done": [None, None], "pss": [None, None],
                  "p16": [None, None], "ptb": [None, None], "psav": [None, None],
                  "atmp": [None, None], "scf": [t_msk, t_msk], "pso": [None, None],
                  "vstg": None, "psp": None, "psvt": [None, None],
                  "attn_done": [None, None]}
        hv = h_fm.rearrange("p (c t) -> p c t", c=NDC)
        hst = {}

        def start_head(h):
            par = h % 2
            ww = [guards["kv"][par]] if guards["kv"][par] else []
            if h < 2:
                ww += list(waits)
            t_lw = env.dma("sync", lambda e, h=h, par=par: e.dma_start(
                out=wslc[par][:, 0:512], in_=c_wq[h * 128:(h + 1) * 128, :]),
                waits=ww, sem=f"d_aq{par}")
            t_lw2 = env.dma("scalar", lambda e, h=h, par=par: e.dma_start(
                out=wslc[par][:, 512:1024], in_=c_wk[h * 128:(h + 1) * 128, :]),
                waits=ww, sem=f"d_ak{par}")
            t_lw3 = env.dma("gpsimd", lambda e, h=h, par=par: e.dma_start(
                out=wslc[par][:, 1024:1536], in_=c_wv[h * 128:(h + 1) * 128, :]),
                waits=ww, sem=f"d_av{par}")
            t_z1 = env.step("vector", lambda e, par=par: e.memset(
                kpad[par][:, 0:PADT], 0.0), waits=ww, inc="s_ve")
            t_z2 = env.step("vector", lambda e, par=par: e.memset(
                kpad[par][:, PADT + T:], 0.0), waits=[t_z1], inc="s_ve")
            t_z3 = env.step("vector", lambda e, par=par: e.memset(
                vtm[par][:, 0:DH], 0.0), waits=[t_z2], inc="s_ve")
            t_z4 = env.step("vector", lambda e, par=par: e.memset(
                vtm[par][:, (NTT + 1) * DH:], 0.0), waits=[t_z3], inc="s_ve")
            hst[h] = dict(lw=[t_lw, t_lw2, t_lw3], kv_last=t_z4)

        def emit_group(h, st, which):
            par = h % 2
            wv3 = wslc[par].rearrange("p (s c j) -> p s c j", s=3, c=NDC)
            mmw = list(hst[h]["lw"])
            if guards["psp"]:
                mmw.append(guards["psp"])
            t_pm = None
            for c in range(NDC):
                t_pm = env.step("tensor", lambda e, c=c, which=which, st=st, wv3=wv3:
                    e.matmul(ps_p[:],
                             wv3[:, which, c, :],
                             hv[:, c, st * 512:(st + 1) * 512],
                             start=(c == 0), stop=(c == NDC - 1)),
                    waits=mmw if c == 0 else [],
                    inc="s_te" if c == NDC - 1 else None)
            if which == 0:
                t_e = env.step("vector", lambda e, st=st, par=par: e.tensor_copy(
                    out=qh[par][:, st * 512:(st + 1) * 512], in_=ps_p[:]),
                    waits=[t_pm], inc="s_ve")
            elif which == 1:
                t_e = env.step("vector", lambda e, st=st, par=par: e.tensor_copy(
                    out=kpad[par][:, PADT + st * 512: PADT + (st + 1) * 512],
                    in_=ps_p[:]), waits=[t_pm], inc="s_ve")
            else:
                vw = [t_pm] + ([guards["vstg"]] if guards["vstg"] else [])
                t_vc = env.step("vector", lambda e: e.tensor_copy(
                    out=vstg[:], in_=ps_p[:]), waits=vw, inc="s_ve")
                for j in range(4):
                    tw = [t_vc] + ([guards["psvt"][0]] if guards["psvt"][0] else [])
                    t_vt = env.step("tensor", lambda e, j=j: e.transpose(
                        ps_vt[:, 0:DH],
                        vstg[:, j * 128:(j + 1) * 128], env.consts["idb"][0:64, 0:64]),
                        waits=tw, inc="s_te")
                    kt = 1 + st * 4 + j
                    t_ve = env.step("vector", lambda e, kt=kt, par=par: e.tensor_copy(
                        out=vtm[par][:, kt * DH:(kt + 1) * DH],
                        in_=ps_vt[:, 0:DH]),
                        waits=[t_vt], inc="s_ve")
                    guards["psvt"][0] = t_ve
                guards["vstg"] = t_ve
                t_e = t_ve
            guards["psp"] = t_e
            hst[h]["kv_last"] = t_e

        start_head(0)
        for st in range(8):
            for which in range(3):
                emit_group(0, st, which)

        for h in range(NH):
            par = h % 2
            if h + 1 < NH:
                start_head(h + 1)
                pend = [(st, w) for st in range(8) for w in range(3)]
            else:
                pend = []
            t_kv_last = hst[h]["kv_last"]
            t_attn_last = None
            for qt in range(NTT):
                qpar = qt % 2
                p_scf = p_p16 = p_ptb = p_pss = p_pspt = p_colb = qpar
                smw = [t_kv_last]
                if guards["pss"][p_pss]:
                    smw.append(guards["pss"][p_pss])
                t_sc = env.step("tensor", lambda e, qt=qt, par=par, qpar=p_pss: e.matmul(
                    ps_s[:, qpar * 512: qpar * 512 + 384],
                    qh[par][:, qt * 128:(qt + 1) * 128],
                    kpad[par][:, qt * 128: qt * 128 + 384],
                    start=True, stop=True), waits=smw, inc="s_te")
                if pend:
                    emit_group(h + 1, *pend.pop(0))
                mw = [t_sc, guards["scf"][p_scf]] if guards["scf"][p_scf] else [t_sc]
                t_ms = env.step("vector", lambda e, qs=p_scf, qp=p_pss: e.scalar_tensor_tensor(
                    out=scf[qs][:], in0=ps_s[:, qp * 512: qp * 512 + 384],
                    scalar=0.125, in1=mask[:], op0=ALU.mult, op1=ALU.add),
                    waits=mw, inc="s_ve")
                guards["pss"][p_pss] = t_ms
                c4 = colb[p_colb]
                t_rm = env.step("vector", lambda e, qs=p_scf, c4=c4: e.reduce_max(
                    out=c4[:, 0:1], in_=scf[qs][:], axis=AX, negate=True),
                    waits=[t_ms], inc="s_ve")
                pw = [t_rm] + ([guards["p16"][p_p16]] if guards["p16"][p_p16] else [])
                t_ex = env.step("scalar", lambda e, qs=p_scf, qx=p_p16, c4=c4: e.activation(
                    out=p16[qx][:], in_=scf[qs][:], func=AF.Exp, bias=c4[:, 0:1],
                    accum_out=c4[:, 2:3]), waits=pw, inc="s_se")
                guards["scf"][p_scf] = t_ex
                t_rc = env.step("vector", lambda e, c4=c4: e.reciprocal(
                    out=c4[:, 3:4], in_=c4[:, 2:3]), waits=[t_ex], inc="s_ve")
                t_ptr = None
                for j in range(3):
                    tw = [t_ex]
                    if j == 0 and guards["ptb"][p_ptb]:
                        tw.append(guards["ptb"][p_ptb])
                    t_ptr = env.step("tensor", lambda e, j=j, qx=p_p16, qz=p_pspt: e.transpose(
                        ps_pt[:, qz * 1024 + j * 128: qz * 1024 + (j + 1) * 128],
                        p16[qx][:, j * 128:(j + 1) * 128], env.consts["idb"][:]),
                        waits=tw, inc="s_te" if j == 2 else None)
                guards["p16"][p_p16] = t_ptr
                t_pte = env.step("vector", lambda e, qb=p_ptb, qz=p_pspt: e.tensor_copy(
                    out=ptb[qb][:], in_=ps_pt[:, qz * 1024: qz * 1024 + 384]),
                    waits=[t_ptr], inc="s_ve")
                t_av = None
                for j in range(3):
                    avw = [t_pte]
                    if j == 0 and guards["psav"][qpar]:
                        avw.append(guards["psav"][qpar])
                    t_av = env.step("tensor", lambda e, j=j, qt=qt, par=par, qb=p_ptb, qpar=qpar:
                        e.matmul(ps_av[:, qpar * DH:(qpar + 1) * DH],
                                 ptb[qb][:, j * 128:(j + 1) * 128],
                                 vtm[par][:, (qt + j) * DH:(qt + j + 1) * DH],
                                 start=(j == 0), stop=(j == 2)),
                        waits=avw, inc="s_te" if j == 2 else None)
                guards["ptb"][p_ptb] = t_av
                nw_ = [t_av, t_rc] + ([guards["atmp"][qpar]] if guards["atmp"][qpar] else [])
                t_no = env.step("vector", lambda e, qpar=qpar, c4=c4: e.tensor_scalar(
                    out=atmp[qpar][:], in0=ps_av[:, qpar * DH:(qpar + 1) * DH],
                    scalar1=c4[:, 3:4], scalar2=1.0, op0=ALU.mult, op1=ALU.mult),
                    waits=nw_, inc="s_ve")
                guards["psav"][qpar] = t_no
                tw = [t_no]
                if guards["pso"][qpar]:
                    tw.append(guards["pso"][qpar])
                t_otr = env.step("tensor", lambda e, qpar=qpar: e.transpose(
                    ps_o[:, qpar * 128:(qpar + 1) * 128], atmp[qpar][:],
                    env.consts["idb"][:]), waits=tw, inc="s_te")
                guards["atmp"][qpar] = t_otr
                cch = (h * DH) // 128
                off = (h * DH) % 128
                t_oev = env.step("vector", lambda e, qpar=qpar, qt=qt, cch=cch, off=off:
                    e.tensor_copy(
                        out=attn_all[off:off + DH, cch * T + qt * 128: cch * T + (qt + 1) * 128],
                        in_=ps_o[0:DH, qpar * 128:(qpar + 1) * 128]),
                    waits=[t_otr], inc="s_ve")
                guards["pso"][qpar] = t_oev
                t_attn_last = t_oev
            guards["kv"][par] = t_attn_last
            fin_heads.append(t_attn_last)
    env.fence()
    with env.scope():
        wo_sb = env.sb("a0_wo", [128, NDC * D], BF16)
        bo_row = env.sb("a0_borow", [1, D], F32)
        bo_bc = env.sb("a0_bobc", [128, D], F32)
        with env.scope():
            ps_b = env.psum("a0_psb", [128, 512], F32)
            wov = wo_sb.rearrange("p (c j) -> p c j", c=NDC)
            t_lwo = env.dma("sync", lambda e: e.dma_start(
                out=wov[:], in_=c_wo.rearrange("(c p) j -> p c j", p=128)), sem="d_wo")
            t_lbo = env.dma("sync", lambda e: e.dma_start(out=bo_row[:], in_=bo_hbm[:]),
                            sem="d_bo")
            t_bce = None
            for half in range(2):
                bw = [t_lbo] if half == 0 else [t_bce]
                t_bcm = env.step("tensor", lambda e, half=half: e.matmul(
                    ps_b[:], env.consts["ones_f"][:], bo_row[0:1, half * 512:(half + 1) * 512],
                    start=True, stop=True), waits=bw, inc="s_te")
                t_bce = env.step("vector", lambda e, half=half: e.tensor_copy(
                    out=bo_bc[:, half * 512:(half + 1) * 512], in_=ps_b[:]),
                    waits=[t_bcm], inc="s_ve")
        env.fence()
        fin = emit_proj_residual(env, attn_all, wo_sb, bo_bc, x_in, x_out,
                                 waits=[t_lwo, t_bce])
    return fin


def emit_attn1(env, h_fm, x_in, x_out, c_wqP, c_wkP, c_wvP, c_wproj, waits=()):
    nc = env.nc
    NH, DH = 16, 64
    attn_all = env.sb("a1_attn", [128, NDC * T], BF16)
    with env.scope():
        kh = [env.sb("a1_kh", [64, T], BF16) for _ in range(2)]
        vtm = [env.sb("a1_vtm", [128, NTT * 65], BF16) for _ in range(2)]
        qh = [env.sb("a1_qh", [64, T], BF16) for _ in range(2)]
        wslc = [env.sb("a1_wslc", [128, 3 * NDC * DH], BF16) for _ in range(2)]
        vstg = env.sb("a1_vstg", [64, 512], BF16)
        sq = env.sb("a1_sq", [64, T], BF16)
        ptile = [env.sb("a1_pt", [128, 1024], BF16) for _ in range(2)]
        cmask = env.sb("a1_cmask", [128, 4 * 512], BF16)
        mskf = env.sb("a1_mskf", [128, 512], F32)
        nmax = env.sb("a1_nmax", [1, 24], F32)
        rrow = env.sb("a1_rrow", [1, 512], F32)
        rb_sb = env.sb("a1_rb", [64, 512], F32)
        cneg = env.sb("a1_cneg", [128, 1], F32)
        ps_p = env.psum("a1_psp", [64, 512], F32)
        ps_vt = env.psum("a1_psvt", [128, 2 * DH], BF16)
        ps_s = env.psum("a1_pss", [128, 4 * 512], F32)
        ps_av = env.psum("a1_psav", [65, 512], F32)
        ps_rb = env.psum("a1_psrb", [128, 512], F32)
        ps_n = ps_rb

        t_i0 = env.step("gpsimd", lambda e: e.iota(mskf[:], [[1, 512]], channel_multiplier=-1,
                        allow_small_or_imprecise_dtypes=True), waits=waits, inc="s_po")
        t_cm = None
        for r in range(4):
            t_c1 = env.step("vector", lambda e, r=r: e.tensor_scalar(
                out=cmask[:, r * 512:(r + 1) * 512], in0=mskf[:], scalar1=float(128 * r),
                scalar2=1.0, op0=ALU.is_ge, op1=ALU.mult), waits=[t_i0], inc="s_ve")
            t_cm = t_c1
        t_vo = env.step("vector", lambda e: e.memset(
            vtm[0].rearrange("p (k c) -> p k c", c=65)[:, :, 64:65], 1.0), inc="s_ve")
        t_vo2 = env.step("vector", lambda e: e.memset(
            vtm[1].rearrange("p (k c) -> p k c", c=65)[:, :, 64:65], 1.0), inc="s_ve")

        hv = h_fm.rearrange("p (c t) -> p c t", c=NDC)
        fin_heads = []
        guards = {"kv": [t_vo, t_vo2], "psp": None, "psvt": [None, None], "vstg": None,
                  "sq": None, "psn": None, "ptile": [None, None], "psav": None,
                  "pss": [None, None], "rrow": None, "rb": None, "cneg": None,
                  "nmax": None}
        hst = {}

        def start_head(h):
            par = h % 2
            ww = [guards["kv"][par]]
            if h < 2:
                ww += list(waits)
            t_lw = env.dma("sync", lambda e, h=h, par=par: e.dma_start(
                out=wslc[par][:, 0:512], in_=c_wqP[h * 128:(h + 1) * 128, :]),
                waits=ww, sem=f"d_aq{par}")
            t_lw2 = env.dma("scalar", lambda e, h=h, par=par: e.dma_start(
                out=wslc[par][:, 512:1024], in_=c_wkP[h * 128:(h + 1) * 128, :]),
                waits=ww, sem=f"d_ak{par}")
            t_lw3 = env.dma("gpsimd", lambda e, h=h, par=par: e.dma_start(
                out=wslc[par][:, 1024:1536], in_=c_wvP[h * 128:(h + 1) * 128, :]),
                waits=ww, sem=f"d_av{par}")
            hst[h] = dict(lw=[t_lw, t_lw2, t_lw3], kv_last=None)

        def emit_group(h, st, which):
            par = h % 2
            wv3 = wslc[par].rearrange("p (s c j) -> p s c j", s=3, c=NDC)
            mmw = list(hst[h]["lw"])
            if guards["psp"]:
                mmw.append(guards["psp"])
            t_pm = None
            for c in range(NDC):
                t_pm = env.step("tensor", lambda e, c=c, which=which, st=st, wv3=wv3:
                    e.matmul(ps_p[:], wv3[:, which, c, :],
                             hv[:, c, st * 512:(st + 1) * 512],
                             start=(c == 0), stop=(c == NDC - 1)),
                    waits=mmw if c == 0 else [],
                    inc="s_te" if c == NDC - 1 else None)
            if which == 0:
                t_e = env.step("vector", lambda e, st=st, par=par: e.tensor_copy(
                    out=qh[par][:, st * 512:(st + 1) * 512], in_=ps_p[:]),
                    waits=[t_pm], inc="s_ve")
            elif which == 1:
                t_e = env.step("vector", lambda e, st=st, par=par: e.tensor_copy(
                    out=kh[par][:, st * 512:(st + 1) * 512], in_=ps_p[:]),
                    waits=[t_pm], inc="s_ve")
            else:
                vw = [t_pm] + ([guards["vstg"]] if guards["vstg"] else [])
                t_vc = env.step("vector", lambda e: e.tensor_copy(
                    out=vstg[:], in_=ps_p[:]), waits=vw, inc="s_ve")
                for j in range(4):
                    tw = [t_vc] + ([guards["psvt"][0]] if guards["psvt"][0] else [])
                    t_vt = env.step("tensor", lambda e, j=j: e.transpose(
                        ps_vt[:, 0:DH], vstg[:, j * 128:(j + 1) * 128],
                        env.consts["idb"][0:64, 0:64]), waits=tw, inc="s_te")
                    kt = st * 4 + j
                    t_ve = env.step("vector", lambda e, kt=kt, par=par: e.tensor_copy(
                        out=vtm[par][:, kt * 65: kt * 65 + DH],
                        in_=ps_vt[:, 0:DH]),
                        waits=[t_vt], inc="s_ve")
                    guards["psvt"][0] = t_ve
                guards["vstg"] = t_ve
                t_e = t_ve
            guards["psp"] = t_e
            hst[h]["kv_last"] = t_e

        start_head(0)
        for st in range(8):
            for which in range(3):
                emit_group(0, st, which)

        for h in range(NH):
            par = h % 2
            t_kv_last = hst[h]["kv_last"]
            t_nm_last = None
            for which in range(2):
                src_b = qh[par] if which == 0 else kh[par]
                sw = [t_kv_last] + ([guards["sq"]] if guards["sq"] else [])
                t_sqr = env.step("scalar", lambda e, src_b=src_b: e.activation(
                    out=sq[:], in_=src_b[:], func=AF.Square), waits=sw, inc="s_se")
                for j in range(8):
                    nw_ = [t_sqr]
                    if guards["psn"]:
                        nw_.append(guards["psn"])
                    if j == 0 and which == 0 and guards["rb"]:
                        nw_.append(guards["rb"])
                    t_nmm = env.step("tensor", lambda e, j=j: e.matmul(
                        ps_n[0:1, :], env.consts["ocb"][0:64, 0:1], sq[:, j * 512:(j + 1) * 512],
                        start=True, stop=True), waits=nw_, inc="s_te")
                    nmw = [t_nmm] + ([guards["nmax"]] if guards["nmax"] and j == 0 and which == 0 else [])
                    t_nr = env.step("vector", lambda e, j=j, which=which: e.reduce_max(
                        out=nmax[0:1, which * 8 + j: which * 8 + j + 1], in_=ps_n[0:1, :],
                        axis=AX), waits=nmw, inc="s_ve")
                    guards["psn"] = t_nr
                    t_nm_last = t_nr
                guards["sq"] = t_nm_last
            t_mq = env.step("vector", lambda e: e.reduce_max(
                out=nmax[0:1, 16:17], in_=nmax[0:1, 0:8], axis=AX), waits=[t_nm_last], inc="s_ve")
            t_mk = env.step("vector", lambda e: e.reduce_max(
                out=nmax[0:1, 17:18], in_=nmax[0:1, 8:16], axis=AX), waits=[t_mq], inc="s_ve")
            t_c2 = env.step("vector", lambda e: e.tensor_tensor(
                out=nmax[0:1, 18:19], in0=nmax[0:1, 16:17], in1=nmax[0:1, 17:18],
                op=ALU.mult), waits=[t_mk], inc="s_ve")
            t_cs = env.step("scalar", lambda e: e.activation(
                out=nmax[0:1, 19:20], in_=nmax[0:1, 18:19], func=AF.Sqrt),
                waits=[t_c2], inc="s_se")
            t_cn = env.step("vector", lambda e: e.tensor_scalar(
                out=nmax[0:1, 20:21], in0=nmax[0:1, 19:20], scalar1=-0.125, scalar2=1.0,
                op0=ALU.mult, op1=ALU.mult), waits=[t_cs], inc="s_ve")
            cbw = [t_cn] + ([guards["cneg"]] if guards["cneg"] else [])
            t_cb = env.step("tensor", lambda e: e.matmul(
                ps_rb[:, 0:1], env.consts["ones_f"][:], nmax[0:1, 20:21],
                start=True, stop=True), waits=cbw, inc="s_te")
            guards["nmax"] = t_cb
            t_cv = env.step("vector", lambda e: e.tensor_copy(
                out=cneg[:], in_=ps_rb[:, 0:1]), waits=[t_cb], inc="s_ve")
            if h + 1 < NH:
                start_head(h + 1)
                pend = [(st, w) for st in range(8) for w in range(3)]
            else:
                pend = []
            gctr = 0
            sctok = [None, None]

            def emit_S(qb, mpair, t_kv_last=t_kv_last, par=par):
                sp2 = mpair % 2
                smw = [t_kv_last]
                if guards["pss"][sp2]:
                    smw.append(guards["pss"][sp2])
                t_sm = None
                for sub in range(2):
                    kc = 2 * mpair + sub
                    t_sm = env.step("tensor", lambda e, kc=kc, qb=qb, par=par, sp2=sp2, sub=sub:
                        e.matmul(ps_s[:, sp2 * 1024 + sub * 512: sp2 * 1024 + (sub + 1) * 512],
                                 kh[par][:, kc * 128:(kc + 1) * 128],
                                 qh[par][:, qb * 512:(qb + 1) * 512],
                                 start=True, stop=True),
                        waits=smw if sub == 0 else [], inc="s_te" if sub == 1 else None)
                sctok[sp2] = t_sm

            t_attn_last = None
            emit_S(0, 0)
            for qb in range(8):
                nkc = 4 * (qb + 1)
                t_av = None
                for mpair in range(nkc // 2):
                    sp2 = mpair % 2
                    if mpair + 1 < nkc // 2:
                        emit_S(qb, mpair + 1)
                    elif qb + 1 < 8:
                        emit_S(qb + 1, 0)
                    if pend and gctr % 3 == 0:
                        emit_group(h + 1, *pend.pop(0))
                    gctr += 1
                    t_sm = sctok[sp2]
                    pw = [t_sm, t_cv] + ([guards["ptile"][sp2]] if guards["ptile"][sp2] else [])
                    t_ex = env.step("scalar", lambda e, sp2=sp2: e.activation(
                        out=ptile[sp2][:], in_=ps_s[:, sp2 * 1024:(sp2 + 1) * 1024],
                        func=AF.Exp, scale=0.125, bias=cneg[:, 0:1]),
                        waits=pw, inc="s_se")
                    guards["pss"][sp2] = t_ex
                    for sub in range(2):
                        kc = 2 * mpair + sub
                        r = kc - 4 * qb
                        t_in = t_ex
                        if r >= 0:
                            t_in = env.step("vector", lambda e, sp2=sp2, sub=sub, r=r: e.tensor_tensor(
                                out=ptile[sp2][:, sub * 512:(sub + 1) * 512],
                                in0=ptile[sp2][:, sub * 512:(sub + 1) * 512],
                                in1=cmask[:, r * 512:(r + 1) * 512], op=ALU.mult),
                                waits=[t_ex, t_cm], inc="s_ve")
                        avw = [t_in]
                        if kc == 0 and guards["psav"]:
                            avw.append(guards["psav"])
                        t_av = env.step("tensor", lambda e, kc=kc, par=par, sp2=sp2, sub=sub, nkc=nkc:
                            e.matmul(ps_av[:], vtm[par][:, kc * 65:(kc + 1) * 65],
                                     ptile[sp2][:, sub * 512:(sub + 1) * 512],
                                     start=(kc == 0), stop=(kc == nkc - 1)),
                            waits=avw, inc="s_te" if sub == 1 else None)
                    guards["ptile"][sp2] = t_av
                dw = [t_av] + ([guards["rrow"]] if guards["rrow"] else [])
                t_l1 = env.step("scalar", lambda e: e.activation(
                    out=rrow[:], in_=ps_av[64:65, :], func=AF.Ln), waits=dw, inc="s_se")
                t_l2 = env.step("scalar", lambda e: e.activation(
                    out=rrow[:], in_=rrow[:], func=AF.Exp, scale=-1.0),
                    waits=[t_l1], inc="s_se")
                rbw = [t_l2, t_cv] + ([guards["rb"]] if guards["rb"] else [])
                t_rb = env.step("tensor", lambda e: e.matmul(
                    ps_rb[:], env.consts["ones_f"][:], rrow[:],
                    start=True, stop=True), waits=rbw, inc="s_te")
                t_rv = env.step("vector", lambda e: e.tensor_copy(
                    out=rb_sb[:], in_=ps_rb[0:64, :]), waits=[t_rb], inc="s_ve")
                guards["cneg"] = t_rv
                guards["rrow"] = t_rv
                cch = (h * DH) // 128
                off = (h * DH) % 128
                t_oev = env.step("vector", lambda e, qb=qb, cch=cch, off=off: e.tensor_tensor(
                    out=attn_all[off:off + DH, cch * T + qb * 512: cch * T + (qb + 1) * 512],
                    in0=ps_av[0:DH, :], in1=rb_sb[:], op=ALU.mult),
                    waits=[t_rv], inc="s_ve")
                guards["psav"] = t_oev
                guards["rb"] = t_oev
                t_attn_last = t_oev
            guards["kv"][par] = t_attn_last
            fin_heads.append(t_attn_last)
    env.fence()
    with env.scope():
        wp_sb = env.sb("a1_wp", [128, NDC * D], BF16)
        wpv = wp_sb.rearrange("p (c j) -> p c j", c=NDC)
        t_lwp = env.dma("sync", lambda e: e.dma_start(
            out=wpv[:], in_=c_wproj.rearrange("(c p) j -> p c j", p=128)), sem="d_wo")
        fin = emit_proj_residual(env, attn_all, wp_sb, None, x_in, x_out, waits=[t_lwp])
    return fin



def _pre_lhsT(wT, np2):
    a = wT.reshape(8, 128, np2, 2, 128).transpose(2, 1, 0, 3, 4)
    return np.ascontiguousarray(a).reshape(np2 * 128, 2048)


def _pre_down(dw_e, nhc):
    a = dw_e.reshape(4, 2, 128, nhc, 128).transpose(4, 0, 1, 3, 2)
    return np.ascontiguousarray(a).reshape(128, 4 * 2 * nhc * 128)


def _pre_head(wT):
    a = wT.reshape(8, 128, 16, 64).transpose(2, 1, 0, 3)
    return np.ascontiguousarray(a).reshape(2048, 512)


def _prep(inputs):
    g = {}
    f32 = np.float32
    gu_t = np.asarray(inputs["L0_ffn_gate_up_weight"], f32).transpose(0, 2, 1)
    g["wguP"] = np.ascontiguousarray(np.concatenate(
        [_pre_lhsT(np.ascontiguousarray(gu_t[e, :, half * 2816:(half + 1) * 2816]), 11)
         for e in range(8) for half in range(2)], axis=0))
    dw = np.asarray(inputs["L0_ffn_down_weight"], f32)
    g["wdP"] = np.ascontiguousarray(np.concatenate(
        [_pre_down(dw[e], 22) for e in range(8)], axis=0))
    g["routerT"] = np.ascontiguousarray(np.asarray(inputs["L0_ffn_router_weight"], f32).T)
    g["nw0"] = np.asarray(inputs["L0_ffn_norm_weight"], f32).reshape(1, D)
    g["alpha"] = np.asarray(inputs["L0_ffn_residual_alpha"], f32).reshape(1, 1)
    g["wqP"] = _pre_head(np.ascontiguousarray(np.asarray(inputs["L0_attn_Wq"], f32).T))
    g["wkP"] = _pre_head(np.ascontiguousarray(np.asarray(inputs["L0_attn_Wk"], f32).T))
    g["wvP"] = _pre_head(np.ascontiguousarray(np.asarray(inputs["L0_attn_Wv"], f32).T))
    g["wo0T"] = np.ascontiguousarray(np.asarray(inputs["L0_attn_Wo"], f32).T)
    g["bo0"] = np.asarray(inputs["L0_attn_bo"], f32).reshape(1, D)
    wqkvT = np.ascontiguousarray(np.asarray(inputs["L1_attn_w_qkv"], f32).T)
    g["q1P"] = _pre_head(np.ascontiguousarray(wqkvT[:, 0:D]))
    g["k1P"] = _pre_head(np.ascontiguousarray(wqkvT[:, D:2 * D]))
    g["v1P"] = _pre_head(np.ascontiguousarray(wqkvT[:, 2 * D:3 * D]))
    g["wprojT"] = np.ascontiguousarray(np.asarray(inputs["L1_attn_w_proj"], f32).T)
    g["wg1P"] = _pre_lhsT(np.ascontiguousarray(np.asarray(inputs["L1_ffn_w_gate"], np.int32).T), 12)
    g["wu1P"] = _pre_lhsT(np.ascontiguousarray(np.asarray(inputs["L1_ffn_w_up"], np.int32).T), 12)
    g["wd1P"] = _pre_down(np.asarray(inputs["L1_ffn_w_down"], np.int32), 24)
    xs = [np.ascontiguousarray(np.asarray(inputs["L0_attn_x"][i], f32)) for i in range(8)]
    return g, xs


def build_full(dbg=False):
    nc = bass.Bass("TRN2", target_bir_lowering=False, debug=False)
    env = Env(nc)
    f32in = {}
    for name, shape in [
        ("x0", [T, D]), ("wguP", [8 * 2 * 11 * 128, 2048]), ("wdP", [8 * 128, 22 * 1024]),
        ("routerT", [D, 8]), ("nw0", [1, D]), ("alpha", [1, 1]),
        ("wqP", [2048, 512]), ("wkP", [2048, 512]), ("wvP", [2048, 512]),
        ("wo0T", [D, D]), ("bo0", [1, D]),
        ("q1P", [2048, 512]), ("k1P", [2048, 512]), ("v1P", [2048, 512]),
        ("wprojT", [D, D]),
    ]:
        f32in[name] = nc.dram_tensor(name, shape, F32, kind="ExternalInput").ap()
    i32in = {}
    for name, shape in [("wg1P", [12 * 128, 2048]), ("wu1P", [12 * 128, 2048]),
                        ("wd1P", [128, 24 * 1024])]:
        i32in[name] = nc.dram_tensor(name, shape, I32, kind="ExternalInput").ap()
    y = nc.dram_tensor("y", [T, D], F32, kind="ExternalOutput").ap()
    xs = nc.dram_tensor("xs", [T, D], F32).ap()
    dbgo = [nc.dram_tensor(f"dbg{i}", [T, D], F32, kind="ExternalOutput").ap()
            for i in range(5)] if dbg else None
    def dump(i):
        if dbg:
            env.dma("gpsimd", lambda e, i=i: e.dma_start(out=dbgo[i][:], in_=xs[:]),
                    waits=env.join_all(), sem="d_dbg")
    c = {}
    for name in ("routerT", "wguP", "wdP", "wqP", "wkP", "wvP", "wo0T", "q1P", "k1P",
                 "v1P", "wprojT"):
        c[name] = nc.dram_tensor("c_" + name, list(f32in[name].shape), BF16).ap()
    for name in ("wg1P", "wu1P", "wd1P"):
        c[name] = nc.dram_tensor("c_" + name, list(i32in[name].shape), BF16).ap()
    with env.es:
        env.consts = emit_consts(env)
        tconv = emit_conv(env, [(f32in[n], c[n]) for n in
                                ("routerT", "wguP", "wdP", "wqP", "wkP", "wvP", "wo0T",
                                 "q1P", "k1P", "v1P", "wprojT")]
                          + [(i32in[n], c[n]) for n in ("wg1P", "wu1P", "wd1P")])
        with env.scope():
            normed_fm = env.sb("normed_fm", [128, NDC * T], BF16)
            combine_tm = env.sb("combine_tm", [128, NTT * 8], F32)
            emit_moe_sublayer(env, f32in["x0"], xs, c["wguP"], c["wdP"],
                              c["routerT"], f32in["nw0"], f32in["alpha"],
                              normed_fm, combine_tm, waits=[tconv[0], tconv[-1]])
        dump(0)
        env.fence()
        with env.scope():
            h_fm = env.sb("h_fm_a0", [128, NDC * T], BF16)
            with env.scope():
                emit_lnT(env, xs, h_fm, waits=env.consts["tok"])
            env.fence()
            emit_attn0(env, h_fm, xs, xs, c["wqP"], c["wkP"], c["wvP"], c["wo0T"],
                       f32in["bo0"], waits=[])
        dump(1)
        env.fence()
        with env.scope():
            normed_fm = env.sb("normed_fm2", [128, NDC * T], BF16)
            combine_tm = env.sb("combine_tm2", [128, NTT * 8], F32)
            emit_moe_sublayer(env, xs, xs, c["wguP"], c["wdP"],
                              c["routerT"], f32in["nw0"], f32in["alpha"],
                              normed_fm, combine_tm, waits=[])
        dump(2)
        env.fence()
        with env.scope():
            h_fm = env.sb("h_fm_f1", [128, NDC * T], BF16)
            with env.scope():
                emit_lnT(env, xs, h_fm, waits=[])
            env.fence()
            emit_ffn_swiglu(env, h_fm, xs, xs, c["wg1P"], c["wu1P"], c["wd1P"],
                            HID=3072, gate_scale=1.0 / 127.0,
                            res_scale=0.5 / (127.0 * 127.0), waits=[])
        dump(3)
        env.fence()
        with env.scope():
            h_fm = env.sb("h_fm_a1", [128, NDC * T], BF16)
            with env.scope():
                emit_lnT(env, xs, h_fm, waits=[])
            env.fence()
            emit_attn1(env, h_fm, xs, xs, c["q1P"], c["k1P"], c["v1P"], c["wprojT"],
                       waits=[])
        dump(4)
        env.fence()
        with env.scope():
            h_fm = env.sb("h_fm_f2", [128, NDC * T], BF16)
            with env.scope():
                emit_lnT(env, xs, h_fm, waits=[])
            env.fence()
            fin = emit_ffn_swiglu(env, h_fm, xs, y, c["wg1P"], c["wu1P"], c["wd1P"],
                                  HID=3072, gate_scale=1.0 / 127.0,
                                  res_scale=0.5 / (127.0 * 127.0), waits=[])
        env.wait_only("gpsimd", fin + env.join_all())
        env.emit()
    names = list(f32in) + list(i32in)
    return nc, names


_FULL_CACHE = []


def kernel_fused(**inputs):
    g, xs_np = _prep(inputs)
    if not _FULL_CACHE:
        _FULL_CACHE.append(build_full())
    nc, innames = _FULL_CACHE[0]
    in_maps = []
    for i in range(8):
        m = {k: g[k] for k in innames if k != "x0"}
        m["x0"] = xs_np[i]
        in_maps.append(m)
    res = run_bass_kernel_spmd(nc, in_maps, core_ids=list(range(8)))
    return np.stack([np.asarray(r["y"]) for r in res.results], axis=0).astype(np.float32)


_PROG_CACHE = {}


def _get_prog(name):
    if name not in _PROG_CACHE:
        _PROG_CACHE[name] = {
            "moe": build_moe_only, "attn0": build_attn0_only,
            "ffn1": build_ffn1_only, "attn1": build_attn1_only,
        }[name]()
    return _PROG_CACHE[name]


def _launch(name, g, xs):
    nc, innames = _get_prog(name)
    in_maps = []
    for i in range(8):
        m = {k: g[k] for k in innames if k != "x0"}
        m["x0"] = xs[i]
        in_maps.append(m)
    res = run_bass_kernel_spmd(nc, in_maps, core_ids=list(range(8)))
    return [np.asarray(r["y"]) for r in res.results]


def kernel(**inputs):
    return kernel_fused(**inputs)


def build_moe_only():
    nc = bass.Bass("TRN2", target_bir_lowering=False, debug=False)
    env = Env(nc)
    x0 = nc.dram_tensor("x0", [T, D], F32, kind="ExternalInput").ap()
    y = nc.dram_tensor("y", [T, D], F32, kind="ExternalOutput").ap()
    wgu = nc.dram_tensor("wgu0T", [8 * D, 5632], F32, kind="ExternalInput").ap()
    wd0 = nc.dram_tensor("wd0T", [8 * 2816, D], F32, kind="ExternalInput").ap()
    router = nc.dram_tensor("routerT", [D, 8], F32, kind="ExternalInput").ap()
    nw = nc.dram_tensor("nw0", [1, D], F32, kind="ExternalInput").ap()
    alpha = nc.dram_tensor("alpha", [1, 1], F32, kind="ExternalInput").ap()
    c_wgu = nc.dram_tensor("c_wgu", [8 * D, 5632], BF16).ap()
    c_wd0 = nc.dram_tensor("c_wd0", [8 * 2816, D], BF16).ap()
    with env.es:
        env.consts = emit_consts(env)
        tconv = emit_conv(env, [(wgu, c_wgu), (wd0, c_wd0)])
        normed_fm = env.sb("normed_fm", [128, NDC * T], BF16)
        combine_tm = env.sb("combine_tm", [128, NTT * 8], F32)
        fin = emit_moe_sublayer(env, x0, y, c_wgu, c_wd0, router, nw, alpha,
                                normed_fm, combine_tm, waits=[tconv[-1]])
        env.wait_only("gpsimd", fin)
        env.emit()
    return nc, ["x0", "wgu0T", "wd0T", "routerT", "nw0", "alpha"]


def build_attn0_only():
    nc = bass.Bass("TRN2", target_bir_lowering=False, debug=False)
    env = Env(nc)
    x0 = nc.dram_tensor("x0", [T, D], F32, kind="ExternalInput").ap()
    y = nc.dram_tensor("y", [T, D], F32, kind="ExternalOutput").ap()
    wq = nc.dram_tensor("wqP", [2048, 512], F32, kind="ExternalInput").ap()
    wk = nc.dram_tensor("wkP", [2048, 512], F32, kind="ExternalInput").ap()
    wv = nc.dram_tensor("wvP", [2048, 512], F32, kind="ExternalInput").ap()
    wo = nc.dram_tensor("wo0T", [D, D], F32, kind="ExternalInput").ap()
    bo = nc.dram_tensor("bo0", [1, D], F32, kind="ExternalInput").ap()
    cs = {n: nc.dram_tensor("c_" + n, list(ap.shape), BF16).ap()
          for n, ap in (("wqP", wq), ("wkP", wk), ("wvP", wv), ("wo0T", wo))}
    with env.es:
        env.consts = emit_consts(env)
        tconv = emit_conv(env, [(wq, cs["wqP"]), (wk, cs["wkP"]), (wv, cs["wvP"]),
                                (wo, cs["wo0T"])])
        h_fm = env.sb("h_fm", [128, NDC * T], BF16)
        with env.scope():
            tln = emit_lnT(env, x0, h_fm, waits=env.consts["tok"])
        env.fence()
        with env.scope():
            fin = emit_attn0(env, h_fm, x0, y, cs["wqP"], cs["wkP"], cs["wvP"],
                             cs["wo0T"], bo, waits=[tconv[-1]])
        env.wait_only("gpsimd", fin)
        env.emit()
    return nc, ["x0", "wqP", "wkP", "wvP", "wo0T", "bo0"]


def build_attn1_only():
    nc = bass.Bass("TRN2", target_bir_lowering=False, debug=False)
    env = Env(nc)
    x0 = nc.dram_tensor("x0", [T, D], F32, kind="ExternalInput").ap()
    y = nc.dram_tensor("y", [T, D], F32, kind="ExternalOutput").ap()
    qP = nc.dram_tensor("q1P", [2048, 512], F32, kind="ExternalInput").ap()
    kP = nc.dram_tensor("k1P", [2048, 512], F32, kind="ExternalInput").ap()
    vP = nc.dram_tensor("v1P", [2048, 512], F32, kind="ExternalInput").ap()
    wproj = nc.dram_tensor("wprojT", [D, D], F32, kind="ExternalInput").ap()
    cs = {n: nc.dram_tensor("c_" + n, list(ap.shape), BF16).ap()
          for n, ap in (("q1P", qP), ("k1P", kP), ("v1P", vP), ("wprojT", wproj))}
    with env.es:
        env.consts = emit_consts(env)
        tconv = emit_conv(env, [(qP, cs["q1P"]), (kP, cs["k1P"]), (vP, cs["v1P"]),
                                (wproj, cs["wprojT"])])
        h_fm = env.sb("h_fm", [128, NDC * T], BF16)
        with env.scope():
            tln = emit_lnT(env, x0, h_fm, waits=env.consts["tok"])
        env.fence()
        with env.scope():
            fin = emit_attn1(env, h_fm, x0, y, cs["q1P"], cs["k1P"], cs["v1P"],
                             cs["wprojT"], waits=[tconv[-1]])
        env.wait_only("gpsimd", fin)
        env.emit()
    return nc, ["x0", "q1P", "k1P", "v1P", "wprojT"]


def build_ffn1_only():
    nc = bass.Bass("TRN2", target_bir_lowering=False, debug=False)
    env = Env(nc)
    x0 = nc.dram_tensor("x0", [T, D], F32, kind="ExternalInput").ap()
    y = nc.dram_tensor("y", [T, D], F32, kind="ExternalOutput").ap()
    wg1 = nc.dram_tensor("wg1P", [12 * 128, 2048], I32, kind="ExternalInput").ap()
    wu1 = nc.dram_tensor("wu1P", [12 * 128, 2048], I32, kind="ExternalInput").ap()
    wd1 = nc.dram_tensor("wd1P", [128, 24576], I32, kind="ExternalInput").ap()
    cg = nc.dram_tensor("c_wg1P", [12 * 128, 2048], BF16).ap()
    cu = nc.dram_tensor("c_wu1P", [12 * 128, 2048], BF16).ap()
    cd = nc.dram_tensor("c_wd1P", [128, 24576], BF16).ap()
    with env.es:
        env.consts = emit_consts(env)
        tconv = emit_conv(env, [(wg1, cg), (wu1, cu), (wd1, cd)])
        h_fm = env.sb("h_fm", [128, NDC * T], BF16)
        with env.scope():
            tln = emit_lnT(env, x0, h_fm, waits=env.consts["tok"])
        env.fence()
        with env.scope():
            tf = emit_ffn_swiglu(env, h_fm, x0, y, cg, cu, cd, HID=3072,
                                 gate_scale=1.0 / 127.0, res_scale=0.5 / (127.0 * 127.0),
                                 waits=[tconv[-1]])
        env.wait_only("gpsimd", tf)
        env.emit()
    return nc, ["x0", "wg1P", "wu1P", "wd1P"]




# version 15
# speedup vs baseline: 1.1557x; 1.1557x over previous
import contextlib
import numpy as np
import ml_dtypes
import concourse.bass as bass
import concourse.mybir as mybir
from concourse.bass_utils import run_bass_kernel_spmd

F32 = mybir.dt.float32
BF16 = mybir.dt.bfloat16
I32 = mybir.dt.int32
AF = mybir.ActivationFunctionType
ALU = mybir.AluOpType
AX = mybir.AxisListType.X
BF = ml_dtypes.bfloat16

T = 4096
D = 1024
NTT = T // 128
NDC = D // 128


ENGINES = ("sync", "tensor", "vector", "scalar", "gpsimd")


class Env:

    def __init__(self, nc):
        self.nc = nc
        self.es = contextlib.ExitStack()
        self.scopes = [self.es]
        self.steps = []
        self.sems = {}
        self.cnt = {}
        self.uid = 0
        self.final_tokens = []

    @contextlib.contextmanager
    def scope(self):
        s = contextlib.ExitStack()
        self.scopes.append(s)
        try:
            yield
        finally:
            self.scopes.pop()
            s.close()

    def fence(self):
        self.steps.append(("__fence__", None, tuple(self.join_all()), None))

    def name(self, base):
        self.uid += 1
        return f"{base}_{self.uid}"

    def sem(self, name):
        if name not in self.sems:
            self.sems[name] = self.es.enter_context(self.nc.semaphore(name))
            self.cnt[name] = 0
        return name

    def sb(self, base, shape, dt):
        return self.scopes[-1].enter_context(self.nc.sbuf_tensor(self.name(base), shape, dt))

    def psum(self, base, shape, dt=F32):
        return self.scopes[-1].enter_context(self.nc.psum_tensor(self.name(base), shape, dt))

    def step(self, eng, fn, waits=(), inc=None):
        tok = None
        if inc is not None:
            if isinstance(inc, str):
                inc = (inc, 1)
            s, n = inc
            self.sem(s)
            self.cnt[s] += n
            tok = (s, self.cnt[s])
        ws = tuple((s, v) for (s, v) in waits if v > 0)
        for s, _v in ws:
            self.sem(s)
        self.steps.append((eng, fn, ws, inc))
        return tok

    def dma(self, queue, fn, waits=(), sem=None):
        if sem is None:
            sem = {"sync": "d_sy", "scalar": "d_sc", "gpsimd": "d_gp"}[queue]
        return self.step(queue, fn, waits=waits, inc=(sem, 16))

    def emit(self):
        block = self.es.enter_context(self.nc.Block())
        env = self

        def mk2(eng_name):
            def body(eng):
                waited = {}
                pending = {}
                for (e, fn, waits, inc) in env.steps:
                    if e == "__fence__":
                        for (s, v) in waits:
                            if pending.get(s, 0) < v:
                                pending[s] = v
                        continue
                    if e != eng_name:
                        continue
                    if pending:
                        for s, v in pending.items():
                            if waited.get(s, 0) < v:
                                eng.wait_ge(env.sems[s], v)
                                waited[s] = v
                        pending = {}
                    for (s, v) in waits:
                        if waited.get(s, 0) < v:
                            eng.wait_ge(env.sems[s], v)
                            waited[s] = v
                    if fn is None:
                        continue
                    ins = fn(eng)
                    if inc is not None:
                        s, n = inc
                        ins.then_inc(env.sems[s], n)
            return body

        block.sync(mk2("sync"))
        block.tensor(mk2("tensor"))
        block.vector(mk2("vector"))
        block.scalar(mk2("scalar"))
        block.gpsimd(mk2("gpsimd"))

    def join_all(self):
        return [(s, v) for s, v in self.cnt.items()]

    def wait_only(self, eng, waits):
        self.steps.append((eng, None, tuple((s, v) for (s, v) in waits if v > 0), None))
        for s, _v in waits:
            self.sem(s)



def emit_consts(env):
    nc = env.nc
    idf = env.sb("idf", [128, 128], F32)
    idb = env.sb("idb", [128, 128], BF16)
    ones_f = env.sb("ones_f", [1, 128], F32)
    ones_b = env.sb("ones_b", [1, 128], BF16)
    t_io = env.step("gpsimd", lambda e: e.iota(idf[:], [[1, 128]], channel_multiplier=-1,
                                              allow_small_or_imprecise_dtypes=True),
                    inc="s_po")
    t_id = env.step("vector", lambda e: e.tensor_scalar(out=idf[:], in0=idf[:],
                    scalar1=0.0, scalar2=1.0, op0=ALU.is_equal, op1=ALU.mult),
                    waits=[t_io], inc="s_ve")
    t_idb = env.step("vector", lambda e: e.tensor_copy(out=idb[:], in_=idf[:]),
                     waits=[t_id], inc="s_ve")
    t_o1 = env.step("vector", lambda e: e.memset(ones_f[:], 1.0), inc="s_ve")
    t_o2 = env.step("vector", lambda e: e.memset(ones_b[:], 1.0), inc="s_ve")
    ocb = env.sb("ocb", [128, 1], BF16)
    t_o3 = env.step("vector", lambda e: e.memset(ocb[:], 1.0), inc="s_ve")
    return dict(idf=idf, idb=idb, ones_f=ones_f, ones_b=ones_b, ocb=ocb,
                tok=[t_id, t_idb, t_o1, t_o2, t_o3])



def emit_conv(env, pairs):
    toks = []
    for src, dst in pairs:
        R, C = src.shape
        rows_per = max(1, (4 << 20) // (C * 4))
        r = 0
        while r < R:
            r1 = min(R, r + rows_per)
            toks.append(env.dma("gpsimd",
                lambda e, src=src, dst=dst, r=r, r1=r1:
                    e.dma_start(out=dst[r:r1, :], in_=src[r:r1, :])))
            r = r1
    flush = env.sb("convflush", [1, 64], BF16)
    src0, dst0 = pairs[0]
    nfl = min(64, dst0.shape[1])
    toks.append(env.dma("gpsimd", lambda e, nfl=nfl: e.dma_start(
        out=flush[0:1, 0:nfl], in_=dst0[0:1, 0:nfl])))
    return toks



def emit_lnT(env, x_in, h_fm, waits=(), apply_fn=None, post_fn=None):
    nc = env.nc
    xt = [env.sb("ln_xt", [128, D], F32) for _ in range(2)]
    dump = env.sb("ln_dump", [128, D], F32)
    col = [env.sb("ln_col", [128, 16], F32) for _ in range(2)]
    lnb = [env.sb("ln_lnb", [128, D], BF16) for _ in range(2)]
    pst = env.psum("ln_pst", [128, 2 * D], BF16)

    done = []
    prev_evac = [None, None]
    prev_col = [None, None]
    prev_sq = [None, None]
    for tt in range(NTT):
        par = tt % 2
        w = list(waits) if tt < 2 else []
        if prev_evac[par] is not None:
            w.append(prev_evac[par])
        t_x = env.dma("sync", lambda e, tt=tt, par=par: e.dma_start(
            out=xt[par][:], in_=x_in[tt * 128:(tt + 1) * 128, :]), waits=w,
            sem=f"d_ln{par}")
        cw = [t_x] + ([prev_col[par]] if prev_col[par] else [])
        c = col[par]
        sx2 = c[:, 0:1]; sx = c[:, 1:2]; m = c[:, 2:3]; m2 = c[:, 3:4]
        v = c[:, 4:5]; sdev = c[:, 5:6]; r = c[:, 6:7]
        sqw = cw + ([prev_sq[par]] if prev_sq[par] else [])
        t_sq = env.step("scalar", lambda e, par=par, sx2=sx2: e.activation(
            out=dump[:], in_=xt[par][:], func=AF.Square, accum_out=sx2), waits=sqw, inc="s_se")
        prev_sq[par] = t_sq
        t_sm = env.step("vector", lambda e, par=par, sx=sx: e.reduce_sum(
            out=sx, in_=xt[par][:], axis=AX), waits=cw, inc="s_ve")
        t_m = env.step("vector", lambda e, sx=sx, m=m: e.tensor_scalar(
            out=m, in0=sx, scalar1=1.0 / D, scalar2=1.0, op0=ALU.mult, op1=ALU.mult),
            waits=[t_sm], inc="s_ve")
        t_m2 = env.step("vector", lambda e, m=m, m2=m2: e.tensor_scalar(
            out=m2, in0=m, scalar1=m, scalar2=1e-5, op0=ALU.mult, op1=ALU.subtract),
            waits=[t_m], inc="s_ve")
        t_v = env.step("vector", lambda e, sx2=sx2, m2=m2, v=v: e.scalar_tensor_tensor(
            out=v, in0=sx2, scalar=1.0 / D, in1=m2, op0=ALU.mult, op1=ALU.subtract),
            waits=[t_sq, t_m2], inc="s_ve")
        t_sd = env.step("scalar", lambda e, v=v, sdev=sdev: e.activation(
            out=sdev, in_=v, func=AF.Sqrt), waits=[t_v], inc="s_se")
        t_r = env.step("vector", lambda e, sdev=sdev, r=r: e.reciprocal(
            out=r, in_=sdev), waits=[t_sd], inc="s_ve")
        if apply_fn is None:
            t_ln = env.step("vector", lambda e, par=par, m=m, r=r: e.tensor_scalar(
                out=lnb[par][:], in0=xt[par][:], scalar1=m, scalar2=r,
                op0=ALU.subtract, op1=ALU.mult), waits=[t_r], inc="s_ve")
        else:
            t_ln = apply_fn(tt, par, xt[par], lnb[par], c, dict(t_r=t_r, t_x=t_x))
        prev_col[par] = t_ln
        trw = [t_ln] + ([prev_evac[par]] if prev_evac[par] else [])
        t_tr = None
        for cch in range(NDC):
            t_tr = env.step("tensor", lambda e, par=par, cch=cch: e.transpose(
                pst[:, par * D + cch * 128: par * D + (cch + 1) * 128],
                lnb[par][:, cch * 128:(cch + 1) * 128], env.consts["idb"][:]),
                waits=trw if cch == 0 else [], inc="s_te" if cch == NDC - 1 else None)
        hv = h_fm.rearrange("p (c t) -> p c t", c=NDC)
        pv = pst.rearrange("p (b c f) -> p b c f", b=2, c=NDC)
        t_ev = env.step("vector", lambda e, par=par, tt=tt, hv=hv, pv=pv: e.tensor_copy(
            out=hv[:, :, tt * 128:(tt + 1) * 128], in_=pv[:, par]),
            waits=[t_tr], inc="s_ve")
        prev_evac[par] = t_ev
        if post_fn is not None:
            post_fn(tt, par, t_ev)
        done.append(t_ev)
    return done[-2:]



def emit_ffn_core(env, h_fm, x_in, x_out, E, wg_fn, wu_fn, wd_fn, HID, gate_scale,
                  res_scale, waits=(), cbc_src=None):
    nc = env.nc
    NHC = HID // 128
    NB = T // 512
    NG = NB * E
    NWB = 4
    assert NHC % 2 == 0
    wg_buf = [env.sb("f_wg", [128, NDC * 256], BF16) for _ in range(NWB)]
    wu_buf = [env.sb("f_wu", [128, NDC * 256], BF16) for _ in range(NWB)]
    wdq = [env.sb("f_wdq", [128, 2 * NHC * 128], BF16) for _ in range(2)]
    hid = [env.sb("f_hid", [128, NHC * 512], BF16) for _ in range(2)]
    sg = [env.sb("f_sg", [128, 512], F32) for _ in range(2)]
    tmp16 = [env.sb("f_tmp16", [128, 512], BF16) for _ in range(2)] if cbc_src is not None else None
    cbc = [env.sb("f_cbc", [128, 512], BF16) for _ in range(2)] if cbc_src is not None else None
    crow = env.sb("f_crow", [1, 512], F32) if cbc_src is not None else None
    acc = env.sb("f_acc", [128, 4096], F32)
    x_blk = env.sb("f_xblk", [128, 2048], F32)
    ps = env.psum("f_ps", [128, 4096], F32)

    def psb(i):
        return ps[:, i * 512:(i + 1) * 512]

    hfv = h_fm.rearrange("p (c t) -> p c t", c=NDC)
    wdqv = [w.rearrange("p (d hc j) -> p d hc j", d=2, hc=NHC) for w in wdq]

    fin = []
    st = {
        "hid_user": [None, None],
        "gate_bank_user": [None] * 4,
        "pair_user": [None, None],
        "x_wb": None,
        "sg_user": [None, None],
        "tmp_user": [None, None],
        "cbc_user": [None, None],
        "cbc_tok": [None, None],
        "crow_user": None,
        "wg_used": [None] * NWB,
        "wdq_mm": [None, None],
        "wdq_tok": [None] * 4,
        "hid_toks": [None, None],
        "acc_toks": {},
        "wpar": 0,
    }

    def emit_cbc(g, extra_waits=()):
        blk_, e_ = divmod(g, E)
        idx = g % 2
        ctv = cbc_src.rearrange("p (tt ex) -> p tt ex", ex=8)
        cw = [x for x in (st["pair_user"][1],) if x] + list(extra_waits)
        t_ctr = None
        for j in range(4):
            t_ctr = env.step("tensor", lambda e2, j=j, e_=e_, blk_=blk_: e2.matmul(
                ps[0:1, 3072 + j * 128: 3072 + (j + 1) * 128],
                ctv[:, 4 * blk_ + j, e_:e_ + 1], env.consts["idf"][:],
                start=True, stop=True),
                waits=cw if j == 0 else [], inc="s_te" if j == 3 else None)
        crw = [t_ctr] + ([st["crow_user"]] if st["crow_user"] else [])
        t_crow = env.step("vector", lambda e2: e2.tensor_copy(
            out=crow[:], in_=ps[0:1, 3072:3584]), waits=crw, inc="s_ve")
        t_cmm = env.step("tensor", lambda e2: e2.matmul(
            psb(6), env.consts["ones_f"][:], crow[:],
            start=True, stop=True), waits=[t_crow], inc="s_te")
        st["crow_user"] = t_cmm
        ccw = [t_cmm] + ([st["cbc_user"][idx]] if st["cbc_user"][idx] else [])
        t_cbc = env.step("vector", lambda e2, idx=idx: e2.tensor_copy(
            out=cbc[idx][:], in_=psb(6)), waits=ccw, inc="s_ve")
        st["cbc_tok"][idx] = t_cbc
        return t_cbc

    def emit_gate(g, first_waits=()):
        blk, e = divmod(g, E)
        gpar = g % 2
        t_cbc = st["cbc_tok"][gpar] if cbc_src is not None else None
        t_hid = {}
        for p2 in range(NHC // 2):
            par = st["wpar"]
            st["wpar"] = (st["wpar"] + 1) % NWB
            w_w = [st["wg_used"][par]] if st["wg_used"][par] else []
            if g == 0 and p2 < 2:
                w_w += list(first_waits)
            t_lg = env.dma("sync", lambda e_, p2=p2, par=par, e=e: e_.dma_start(
                out=wg_buf[par][:], in_=wg_fn(e, p2)), waits=w_w, sem=f"d_wg{par}")
            t_lu = env.dma("scalar", lambda e_, p2=p2, par=par, e=e: e_.dma_start(
                out=wu_buf[par][:], in_=wu_fn(e, p2)), waits=w_w, sem=f"d_wu{par}")
            for i in range(2):
                hc = 2 * p2 + i
                bg = (hc % 2) * 2
                bu = bg + 1
                mmw = [t_lg, t_lu]
                for b in (bg, bu):
                    if st["gate_bank_user"][b]:
                        mmw.append(st["gate_bank_user"][b])
                t_g = t_u = None
                for c in range(NDC):
                    t_g = env.step("tensor", lambda e_, c=c, i=i, bg=bg, par=par, blk=blk:
                        e_.matmul(psb(bg), wg_buf[par][:, c * 256 + i * 128: c * 256 + (i + 1) * 128],
                                  hfv[:, c, blk * 512:(blk + 1) * 512],
                                  start=(c == 0), stop=(c == NDC - 1)),
                        waits=mmw if c == 0 else [], inc="s_te" if c == NDC - 1 else None)
                for c in range(NDC):
                    t_u = env.step("tensor", lambda e_, c=c, i=i, bu=bu, par=par, blk=blk:
                        e_.matmul(psb(bu), wu_buf[par][:, c * 256 + i * 128: c * 256 + (i + 1) * 128],
                                  hfv[:, c, blk * 512:(blk + 1) * 512],
                                  start=(c == 0), stop=(c == NDC - 1)),
                        waits=[], inc="s_te" if c == NDC - 1 else None)
                if i == 1:
                    st["wg_used"][par] = t_u
                spar = hc % 2
                sgw = [t_g] + ([st["sg_user"][spar]] if st["sg_user"][spar] else [])
                t_si = env.step("scalar", lambda e_, spar=spar, bg=bg: e_.activation(
                    out=sg[spar][:], in_=psb(bg), func=AF.Silu, scale=gate_scale),
                    waits=sgw, inc="s_se")
                st["gate_bank_user"][bg] = t_si
                hu = st["hid_user"][gpar]
                if cbc_src is None:
                    hw_ = [t_si, t_u] + ([hu] if hu else [])
                    t_h = env.step("vector", lambda e_, gpar=gpar, bu=bu, hc=hc, spar=spar:
                        e_.tensor_tensor(
                            out=hid[gpar][:, hc * 512:(hc + 1) * 512], in0=sg[spar][:],
                            in1=psb(bu), op=ALU.mult), waits=hw_, inc="s_ve")
                    st["gate_bank_user"][bu] = t_h
                    st["sg_user"][spar] = t_h
                else:
                    tw = [t_si, t_u] + ([st["tmp_user"][spar]] if st["tmp_user"][spar] else [])
                    t_t = env.step("vector", lambda e_, spar=spar, bu=bu: e_.tensor_tensor(
                        out=tmp16[spar][:], in0=sg[spar][:], in1=psb(bu),
                        op=ALU.mult), waits=tw, inc="s_ve")
                    st["gate_bank_user"][bu] = t_t
                    st["sg_user"][spar] = t_t
                    pw = [t_t, t_cbc] + ([hu] if hu else [])
                    t_h = env.step("gpsimd", lambda e_, spar=spar, hc=hc, gpar=gpar: e_.tensor_tensor(
                        out=hid[gpar][:, hc * 512:(hc + 1) * 512], in0=tmp16[spar][:],
                        in1=cbc[gpar][:],
                        op=ALU.mult), waits=pw, inc="s_po")
                    st["tmp_user"][spar] = t_h
                t_hid[hc] = t_h
        if cbc_src is not None:
            st["cbc_user"][gpar] = t_hid[NHC - 1]
        st["hid_toks"][gpar] = t_hid

    def emit_wdq(g, p, extra=()):
        pr2 = p % 2
        dw_w = [st["wdq_mm"][pr2]] if st["wdq_mm"][pr2] else []
        dw_w += list(extra)
        seg = 2 * NHC * 128
        st["wdq_tok"][p] = env.dma("gpsimd", lambda e_, g=g, p=p, pr2=pr2, seg=seg:
            e_.dma_start(out=wdq[pr2][:],
                         in_=wd_fn(g % E)[:, p * seg:(p + 1) * seg]),
            waits=dw_w, sem=f"d_wd{pr2}")

    def emit_down(g, t_cbc_next):
        blk, e = divmod(g, E)
        gpar = g % 2
        t_hid = st["hid_toks"][gpar]
        t_down = None
        for p in range(4):
            pr = p % 2
            b0 = 4 + 2 * pr
            t_down = None
            for hc in range(NHC):
                mmw = [st["wdq_tok"][p], t_hid[hc]]
                if hc == 0 and st["pair_user"][pr]:
                    mmw.append(st["pair_user"][pr])
                if hc == 0 and pr == 1 and t_cbc_next is not None:
                    mmw.append(t_cbc_next)
                for d in range(2):
                    doc = 2 * p + d
                    t_down = env.step("tensor", lambda e_, hc=hc, doc=doc, d=d, b0=b0, pr=pr:
                        e_.matmul(psb(b0 + d), wdqv[pr][:, d, hc, :],
                                  hid[gpar][:, hc * 512:(hc + 1) * 512],
                                  start=(hc == 0), stop=(hc == NHC - 1)),
                        waits=mmw if d == 0 else [],
                        inc="s_te" if (d == 1) else None)
            st["wdq_mm"][pr] = t_down
            if p < 2:
                emit_wdq(g, p + 2)
            for d in range(2):
                doc = 2 * p + d
                if e == 0:
                    st["acc_toks"][doc] = env.step("vector", lambda e_, doc=doc, d=d, b0=b0:
                        e_.tensor_copy(
                            out=acc[:, doc * 512:(doc + 1) * 512], in_=psb(b0 + d)),
                        waits=[t_down], inc="s_ve")
                else:
                    st["acc_toks"][doc] = env.step("vector", lambda e_, doc=doc, d=d, b0=b0:
                        e_.tensor_tensor(
                            out=acc[:, doc * 512:(doc + 1) * 512],
                            in0=acc[:, doc * 512:(doc + 1) * 512], in1=psb(b0 + d),
                            op=ALU.add),
                        waits=[t_down], inc="s_ve")
            st["pair_user"][pr] = st["acc_toks"][2 * p + 1]
        st["hid_user"][gpar] = t_down

    def emit_tail(blk):
        t_adds = []
        for rnd in range(2):
            xw = [st["x_wb"]] if st["x_wb"] else []
            xv = x_blk.rearrange("p (a d) -> p a d", a=2)
            t_xl = env.dma("gpsimd", lambda e_, blk=blk, rnd=rnd, xv=xv: e_.dma_start(
                out=xv[:], in_=x_in[blk * 512 + rnd * 256: blk * 512 + (rnd + 1) * 256, :]
                .rearrange("(a p) d -> p a d", p=128)), waits=xw, sem="d_fxl")
            t_tr = {}
            t_add = {}
            for dt in (2 * rnd, 2 * rnd + 1):
                base = 2048 + (dt % 2) * 1024
                for doc in range(8):
                    trw = []
                    if dt == 2 * rnd and doc == 0:
                        if rnd == 0:
                            trw = [st["acc_toks"][b] for b in range(8)] + \
                                  [x for x in st["pair_user"] if x]
                        else:
                            trw = t_adds[-2:]
                    t_tr[(dt, doc)] = env.step("tensor", lambda e_, dt=dt, doc=doc, base=base:
                        e_.transpose(
                            ps[:, base + doc * 128: base + (doc + 1) * 128],
                            acc[:, doc * 512 + dt * 128: doc * 512 + (dt + 1) * 128],
                            env.consts["idf"][:]),
                        waits=trw, inc="s_te" if doc == 7 else None)
            for li, dt in enumerate((2 * rnd, 2 * rnd + 1)):
                base = 2048 + (dt % 2) * 1024
                t_add[dt] = env.step("vector", lambda e_, li=li, base=base:
                    e_.scalar_tensor_tensor(
                        out=x_blk[:, li * 1024:(li + 1) * 1024], in0=ps[:, base: base + 1024],
                        scalar=res_scale, in1=x_blk[:, li * 1024:(li + 1) * 1024],
                        op0=ALU.mult, op1=ALU.add),
                    waits=[t_tr[(dt, 7)], t_xl], inc="s_ve")
                t_adds.append(t_add[dt])
            t_wb = env.dma("gpsimd", lambda e_, blk=blk, rnd=rnd, xv=xv: e_.dma_start(
                out=x_out[blk * 512 + rnd * 256: blk * 512 + (rnd + 1) * 256, :]
                .rearrange("(a p) d -> p a d", p=128),
                in_=xv[:]), waits=[t_add[2 * rnd], t_add[2 * rnd + 1]], sem="d_xw")
            st["x_wb"] = t_wb
        for pr in range(2):
            st["pair_user"][pr] = t_adds[-1]
        fin.append(st["x_wb"])

    for g in range(NG):
        if cbc_src is not None and g == 0:
            emit_cbc(0, extra_waits=list(waits))
        emit_gate(g, first_waits=waits)
        t_cbc_next = emit_cbc(g + 1) if (cbc_src is not None and g + 1 < NG) else None
        if g >= 1:
            emit_down(g - 1, t_cbc_next)
            if g % E == 0:
                emit_tail(g // E - 1)
        emit_wdq(g, 0, extra=list(waits) if g == 0 else ())
        emit_wdq(g, 1)
    emit_down(NG - 1, None)
    emit_tail(NB - 1)
    return fin


def emit_ffn_swiglu(env, h_fm, x_in, x_out, wgT, wuT, wdT, HID, gate_scale, res_scale,
                    waits=()):
    return emit_ffn_core(env, h_fm, x_in, x_out, 1,
                         lambda e, p2: wgT[p2 * 128:(p2 + 1) * 128, :],
                         lambda e, p2: wuT[p2 * 128:(p2 + 1) * 128, :],
                         lambda e: wdT, HID, gate_scale, res_scale, waits=waits)



def emit_moe_sublayer(env, x_in, x_out, c_wgu, c_wd0, routerT_hbm, nw_hbm, alpha_hbm,
                      normed_fm, combine_tm, waits=()):
    nc = env.nc
    with env.scope():
        nw_bc = env.sb("m_nwbc", [128, D], F32)
        alpha_col = env.sb("m_alpha", [128, 1], F32)
        router_sb = env.sb("m_router", [128, NDC * 8], BF16)
        nw_row = env.sb("m_nwrow", [1, D], F32)
        al_row = env.sb("m_alrow", [1, 2], F32)
        nrm = [env.sb("m_nrm", [128, D], F32) for _ in range(2)]
        eq1 = env.sb("m_eq1", [128, 8], F32)
        lm = env.sb("m_lm", [128, 8], F32)
        eq2 = env.sb("m_eq2", [128, 8], F32)
        rcol = env.sb("m_rcol", [128, 4], F32)
        ps_r = env.psum("m_psr", [128, 8], F32)
        ps_b = env.psum("m_psb", [128, 512], F32)

        rv = router_sb.rearrange("p (c e) -> p c e", c=NDC)
        t_rt = env.dma("sync", lambda e: e.dma_start(
            out=rv[:], in_=routerT_hbm.rearrange("(c p) e -> p c e", p=128)), waits=waits,
            sem="d_m0r")
        t_nw = env.dma("sync", lambda e: e.dma_start(out=nw_row[:], in_=nw_hbm[:]),
                       sem="d_m0n")
        t_al = env.dma("sync", lambda e: e.dma_start(out=al_row[0:1, 0:1], in_=alpha_hbm[:]),
                       sem="d_m0a")
        t_bc1e = None
        for half in range(2):
            bw = [t_nw] + list(env.consts["tok"]) if half == 0 else [t_bc1e]
            t_bc1 = env.step("tensor", lambda e, half=half: e.matmul(
                ps_b[:], env.consts["ones_f"][:], nw_row[0:1, half * 512:(half + 1) * 512],
                start=True, stop=True), waits=bw, inc="s_te")
            t_bc1e = env.step("vector", lambda e, half=half: e.tensor_copy(
                out=nw_bc[:, half * 512:(half + 1) * 512], in_=ps_b[:]),
                waits=[t_bc1], inc="s_ve")
        t_nwbc = t_bc1e
        t_alc = env.step("vector", lambda e: e.tensor_scalar(
            out=al_row[0:1, 1:2], in0=al_row[0:1, 0:1], scalar1=2.0, scalar2=-0.5,
            op0=ALU.min, op1=ALU.max), waits=[t_al], inc="s_ve")
        t_alh = env.step("vector", lambda e: e.tensor_scalar(
            out=al_row[0:1, 0:1], in0=al_row[0:1, 1:2], scalar1=0.5, scalar2=1.0,
            op0=ALU.mult, op1=ALU.mult), waits=[t_alc], inc="s_ve")
        t_amm = env.step("tensor", lambda e: e.matmul(
            ps_b[:, 0:1], env.consts["ones_f"][:], al_row[0:1, 0:1], start=True, stop=True),
            waits=[t_alh, t_nwbc], inc="s_te")
        t_acol = env.step("vector", lambda e: e.tensor_copy(
            out=alpha_col[:], in_=ps_b[:, 0:1]), waits=[t_amm], inc="s_ve")

        state = {"router_prev": t_acol, "wb": []}

        def apply_fn(tt, par, xt, lnb, c, toks):
            t_r = toks["t_r"]
            r = c[:, 6:7]; m = c[:, 2:3]
            rq = c[:, 7:8]; rr = c[:, 8:9]; srr = c[:, 9:10]; rrms = c[:, 10:11]
            s_n = c[:, 11:12]; s_h = c[:, 12:13]
            t1 = env.step("vector", lambda e, r=r, rq=rq: e.tensor_scalar(
                out=rq, in0=r, scalar1=r, scalar2=-1e-5, op0=ALU.mult, op1=ALU.mult),
                waits=[t_r], inc="s_ve")
            t2 = env.step("vector", lambda e, rq=rq, rr=rr: e.tensor_scalar(
                out=rr, in0=rq, scalar1=1.0 + 1e-6, scalar2=1.0, op0=ALU.add, op1=ALU.mult),
                waits=[t1], inc="s_ve")
            t3 = env.step("scalar", lambda e, rr=rr, srr=srr: e.activation(
                out=srr, in_=rr, func=AF.Sqrt), waits=[t2], inc="s_se")
            t4 = env.step("vector", lambda e, srr=srr, rrms=rrms: e.reciprocal(
                out=rrms, in_=srr), waits=[t3], inc="s_ve")
            t5 = env.step("vector", lambda e, r=r, rrms=rrms, s_n=s_n: e.tensor_scalar(
                out=s_n, in0=r, scalar1=rrms, scalar2=1.0, op0=ALU.mult, op1=ALU.mult),
                waits=[t4], inc="s_ve")
            t6 = env.step("vector", lambda e, r=r, s_h=s_h: e.tensor_scalar(
                out=s_h, in0=r, scalar1=0.5, scalar2=1.0, op0=ALU.mult, op1=ALU.mult),
                waits=[t_r], inc="s_ve")
            t_nr = env.step("vector", lambda e, par=par, m=m, s_n=s_n: e.tensor_scalar(
                out=nrm[par][:], in0=xt[:], scalar1=m, scalar2=s_n,
                op0=ALU.subtract, op1=ALU.mult), waits=[t5], inc="s_ve")
            t_nb = env.step("vector", lambda e, par=par: e.tensor_tensor(
                out=lnb[:], in0=nrm[par][:], in1=nw_bc[:], op=ALU.mult),
                waits=[t_nr, t_nwbc], inc="s_ve")
            t_t1 = env.step("vector", lambda e, par=par, m=m, s_h=s_h: e.tensor_scalar(
                out=nrm[par][:], in0=xt[:], scalar1=m, scalar2=s_h,
                op0=ALU.subtract, op1=ALU.mult), waits=[t6, t_nb], inc="s_ve")
            t_xu = env.step("vector", lambda e, par=par: e.tensor_tensor(
                out=xt[:], in0=xt[:], in1=nrm[par][:], op=ALU.add),
                waits=[t_t1], inc="s_ve")
            t_wb = env.dma("sync", lambda e, tt=tt: e.dma_start(
                out=x_out[tt * 128:(tt + 1) * 128, :], in_=xt[:]), waits=[t_xu])
            state["wb"].append(t_wb)
            return t_nb

        hv = normed_fm.rearrange("p (c t) -> p c t", c=NDC)

        def post_fn(tt, par, t_ev):
            prev = state["router_prev"]
            t_mm = None
            for cch in range(NDC):
                t_mm = env.step("tensor", lambda e, cch=cch, tt=tt: e.matmul(
                    ps_r[:], hv[:, cch, tt * 128:(tt + 1) * 128],
                    router_sb[:, cch * 8:(cch + 1) * 8],
                    start=(cch == 0), stop=(cch == NDC - 1)),
                    waits=[t_ev, t_rt, prev] if cch == 0 else [],
                    inc="s_te" if cch == NDC - 1 else None)
            rm1 = rcol[:, 0:1]; rm2 = rcol[:, 1:2]; w1 = rcol[:, 2:3]; w2 = rcol[:, 3:4]
            t_m1 = env.step("vector", lambda e: e.reduce_max(out=rm1, in_=ps_r[:], axis=AX),
                            waits=[t_mm], inc="s_ve")
            t_e1 = env.step("vector", lambda e: e.tensor_scalar(
                out=eq1[:], in0=ps_r[:], scalar1=rm1, scalar2=1.0,
                op0=ALU.is_equal, op1=ALU.mult), waits=[t_m1], inc="s_ve")
            t_lm = env.step("vector", lambda e: e.scalar_tensor_tensor(
                out=lm[:], in0=eq1[:], scalar=-1e30, in1=ps_r[:], op0=ALU.mult, op1=ALU.add),
                waits=[t_e1], inc="s_ve")
            t_m2 = env.step("vector", lambda e: e.reduce_max(out=rm2, in_=lm[:], axis=AX),
                            waits=[t_lm], inc="s_ve")
            t_d = env.step("vector", lambda e: e.tensor_tensor(
                out=w1, in0=rm1, in1=rm2, op=ALU.subtract), waits=[t_m2], inc="s_ve")
            t_w1 = env.step("scalar", lambda e: e.activation(
                out=w1, in_=w1, func=AF.Sigmoid), waits=[t_d], inc="s_se")
            t_w2 = env.step("vector", lambda e: e.tensor_scalar(
                out=w2, in0=w1, scalar1=-1.0, scalar2=1.0, op0=ALU.mult, op1=ALU.add),
                waits=[t_w1], inc="s_ve")
            t_e2 = env.step("vector", lambda e: e.tensor_scalar(
                out=eq2[:], in0=lm[:], scalar1=rm2, scalar2=1.0,
                op0=ALU.is_equal, op1=ALU.mult), waits=[t_m2], inc="s_ve")
            cslot = combine_tm[:, tt * 8:(tt + 1) * 8]
            t_c1 = env.step("vector", lambda e, cslot=cslot: e.tensor_scalar(
                out=cslot, in0=eq1[:], scalar1=w1, scalar2=1.0, op0=ALU.mult, op1=ALU.mult),
                waits=[t_w1], inc="s_ve")
            t_c2 = env.step("vector", lambda e, cslot=cslot: e.scalar_tensor_tensor(
                out=cslot, in0=eq2[:], scalar=w2, in1=cslot, op0=ALU.mult, op1=ALU.add),
                waits=[t_c1, t_e2, t_w2], inc="s_ve")
            t_c3 = env.step("vector", lambda e, cslot=cslot: e.tensor_scalar(
                out=cslot, in0=cslot, scalar1=alpha_col[:, 0:1], scalar2=1.0,
                op0=ALU.mult, op1=ALU.mult), waits=[t_c2, t_acol], inc="s_ve")
            state["router_prev"] = t_c3

        emit_lnT(env, x_in, normed_fm, waits=env.consts["tok"],
                 apply_fn=apply_fn, post_fn=post_fn)
    env.fence()
    with env.scope():
        fin = emit_ffn_core(
            env, normed_fm, x_out, x_out, 8,
            lambda e, p2: c_wgu[((e * 2 + 0) * 11 + p2) * 128: ((e * 2 + 0) * 11 + p2 + 1) * 128, :],
            lambda e, p2: c_wgu[((e * 2 + 1) * 11 + p2) * 128: ((e * 2 + 1) * 11 + p2 + 1) * 128, :],
            lambda e: c_wd0[e * 128:(e + 1) * 128, :],
            HID=2816, gate_scale=1.0, res_scale=1.0,
            waits=[], cbc_src=combine_tm)
    return fin



def emit_proj_residual(env, attn_all, w_sb, bo_bc, x_in, x_out, waits=()):
    acc = env.sb("pr_acc", [128, 4096], F32)
    x_blk = env.sb("pr_xblk", [128, 4096], F32)
    ps = env.psum("pr_ps", [128, 4096], F32)
    av = attn_all.rearrange("p (c t) -> p c t", c=NDC)
    wv = w_sb.rearrange("p (c j) -> p c j", c=NDC)
    fin = []
    prev_bank_user = [None] * 8
    prev_x_wb = None
    for blk in range(T // 512):
        t_mm = {}
        for doc in range(8):
            mmw = list(waits) if blk == 0 and doc == 0 else []
            if prev_bank_user[doc]:
                mmw.append(prev_bank_user[doc])
            t = None
            for c in range(NDC):
                t = env.step("tensor", lambda e, c=c, doc=doc, blk=blk: e.matmul(
                    ps[:, doc * 512:(doc + 1) * 512],
                    wv[:, c, doc * 128:(doc + 1) * 128],
                    av[:, c, blk * 512:(blk + 1) * 512],
                    start=(c == 0), stop=(c == NDC - 1)),
                    waits=mmw if c == 0 else [], inc="s_te" if c == NDC - 1 else None)
            t_mm[doc] = t
        t_ev = {}
        for doc in range(8):
            t_ev[doc] = env.step("vector", lambda e, doc=doc: e.tensor_copy(
                out=acc[:, doc * 512:(doc + 1) * 512], in_=ps[:, doc * 512:(doc + 1) * 512]),
                waits=[t_mm[doc]], inc="s_ve")
        xw = [prev_x_wb] if prev_x_wb else []
        xv = x_blk.rearrange("p (a d) -> p a d", a=4)
        t_xl = env.dma("sync", lambda e, blk=blk, xv=xv: e.dma_start(
            out=xv[:], in_=x_in[blk * 512:(blk + 1) * 512, :]
            .rearrange("(a p) d -> p a d", p=128)), waits=xw, sem="d_pxl")
        t_tr = {}
        for dt in range(4):
            for doc in range(8):
                trw = [t_ev[b] for b in range(8)] if (dt == 0 and doc == 0) else []
                t_tr[(dt, doc)] = env.step("tensor", lambda e, dt=dt, doc=doc: e.transpose(
                    ps[:, dt * 1024 + doc * 128: dt * 1024 + (doc + 1) * 128],
                    acc[:, doc * 512 + dt * 128: doc * 512 + (dt + 1) * 128],
                    env.consts["idf"][:]),
                    waits=trw, inc="s_te" if doc == 7 else None)
        t_add = {}
        for dt in range(4):
            t_add[dt] = env.step("vector", lambda e, dt=dt: e.scalar_tensor_tensor(
                out=x_blk[:, dt * 1024:(dt + 1) * 1024], in0=ps[:, dt * 1024:(dt + 1) * 1024],
                scalar=1.0, in1=x_blk[:, dt * 1024:(dt + 1) * 1024],
                op0=ALU.mult, op1=ALU.add),
                waits=[t_tr[(dt, 7)], t_xl], inc="s_ve")
            if bo_bc is not None:
                t_add[dt] = env.step("vector", lambda e, dt=dt: e.tensor_tensor(
                    out=x_blk[:, dt * 1024:(dt + 1) * 1024],
                    in0=x_blk[:, dt * 1024:(dt + 1) * 1024], in1=bo_bc[:],
                    op=ALU.add), waits=[t_add[dt]], inc="s_ve")
        for b in range(8):
            prev_bank_user[b] = t_add[3]
        t_wb = env.dma("sync", lambda e, blk=blk, xv=xv: e.dma_start(
            out=x_out[blk * 512:(blk + 1) * 512, :].rearrange("(a p) d -> p a d", p=128),
            in_=xv[:]), waits=[t_add[dt] for dt in range(4)])
        prev_x_wb = t_wb
        fin.append(t_wb)
    return fin


def emit_attn0(env, h_fm, x_in, x_out, c_wq, c_wk, c_wv, c_wo, bo_hbm, waits=()):
    nc = env.nc
    NH, DH = 16, 64
    PADT = 128
    attn_all = env.sb("a0_attn", [128, NDC * T], BF16)
    with env.scope():
        kpad = [env.sb("a0_kpad", [64, T + 2 * PADT], BF16) for _ in range(2)]
        vtm = [env.sb("a0_vtm", [128, (NTT + 2) * DH], BF16) for _ in range(2)]
        qh = [env.sb("a0_qh", [64, T], BF16) for _ in range(2)]
        wslc = [env.sb("a0_wslc", [128, 3 * NDC * DH], BF16) for _ in range(2)]
        vstg = env.sb("a0_vstg", [64, 512], BF16)
        mask = env.sb("a0_mask", [128, 384], F32)
        scf = [env.sb("a0_scf", [128, 384], F32) for _ in range(3)]
        p16 = [env.sb("a0_p16", [128, 384], BF16) for _ in range(3)]
        ptb = [env.sb("a0_ptb", [128, 384], BF16) for _ in range(2)]
        atmp = [env.sb("a0_atmp", [128, DH], BF16) for _ in range(2)]
        colb = [env.sb("a0_col", [128, 4], F32) for _ in range(3)]
        ps_p = env.psum("a0_psp", [64, 512], F32)
        ps_vt = env.psum("a0_psvt", [128, 2 * DH], BF16)
        ps_s = env.psum("a0_pss", [128, 3 * 512], F32)
        ps_pt = env.psum("a0_pspt", [128, 2 * 512], BF16)
        ps_av = env.psum("a0_psav", [128, 2 * DH], F32)
        ps_o = env.psum("a0_pso", [64, 2 * 128], BF16)

        t_i0 = env.step("gpsimd", lambda e: e.iota(mask[:], [[1, 384]], channel_multiplier=-1,
                        allow_small_or_imprecise_dtypes=True), waits=waits, inc="s_po")
        t_i1 = env.step("vector", lambda e: e.tensor_scalar(
            out=scf[0][:], in0=mask[:], scalar1=96.0, scalar2=1.0,
            op0=ALU.is_ge, op1=ALU.mult), waits=[t_i0], inc="s_ve")
        t_i1b = env.step("vector", lambda e: e.tensor_scalar(
            out=scf[1][:], in0=mask[:], scalar1=159.0, scalar2=1.0,
            op0=ALU.is_le, op1=ALU.mult), waits=[t_i0], inc="s_ve")
        t_i2 = env.step("vector", lambda e: e.tensor_tensor(
            out=mask[:], in0=scf[0][:], in1=scf[1][:], op=ALU.mult),
            waits=[t_i1, t_i1b], inc="s_ve")
        t_msk = env.step("vector", lambda e: e.tensor_scalar(
            out=mask[:], in0=mask[:], scalar1=1.0, scalar2=1e30,
            op0=ALU.subtract, op1=ALU.mult), waits=[t_i2], inc="s_ve")

        fin_heads = []
        guards = {"kv": [None, None], "score_done": [None, None], "pss": [None] * 3,
                  "p16": [None] * 3, "ptb": [None, None], "psav": [None, None],
                  "atmp": [None, None], "scf": [t_msk] * 3, "pso": [None, None],
                  "vstg": None, "psp": None, "psvt": [None, None],
                  "attn_done": [None, None]}
        hv = h_fm.rearrange("p (c t) -> p c t", c=NDC)
        hst = {}

        def start_head(h):
            par = h % 2
            ww = [guards["kv"][par]] if guards["kv"][par] else []
            if h < 2:
                ww += list(waits)
            t_lw = env.dma("sync", lambda e, h=h, par=par: e.dma_start(
                out=wslc[par][:, 0:512], in_=c_wq[h * 128:(h + 1) * 128, :]),
                waits=ww, sem=f"d_aq{par}")
            t_lw2 = env.dma("scalar", lambda e, h=h, par=par: e.dma_start(
                out=wslc[par][:, 512:1024], in_=c_wk[h * 128:(h + 1) * 128, :]),
                waits=ww, sem=f"d_ak{par}")
            t_lw3 = env.dma("gpsimd", lambda e, h=h, par=par: e.dma_start(
                out=wslc[par][:, 1024:1536], in_=c_wv[h * 128:(h + 1) * 128, :]),
                waits=ww, sem=f"d_av{par}")
            t_z1 = env.step("vector", lambda e, par=par: e.memset(
                kpad[par][:, 0:PADT], 0.0), waits=ww, inc="s_ve")
            t_z2 = env.step("vector", lambda e, par=par: e.memset(
                kpad[par][:, PADT + T:], 0.0), waits=[t_z1], inc="s_ve")
            t_z3 = env.step("vector", lambda e, par=par: e.memset(
                vtm[par][:, 0:DH], 0.0), waits=[t_z2], inc="s_ve")
            t_z4 = env.step("vector", lambda e, par=par: e.memset(
                vtm[par][:, (NTT + 1) * DH:], 0.0), waits=[t_z3], inc="s_ve")
            hst[h] = dict(lw=[t_lw, t_lw2, t_lw3], kv_last=t_z4)

        def emit_group(h, st, which):
            par = h % 2
            wv3 = wslc[par].rearrange("p (s c j) -> p s c j", s=3, c=NDC)
            mmw = list(hst[h]["lw"])
            if guards["psp"]:
                mmw.append(guards["psp"])
            t_pm = None
            for c in range(NDC):
                t_pm = env.step("tensor", lambda e, c=c, which=which, st=st, wv3=wv3:
                    e.matmul(ps_p[:],
                             wv3[:, which, c, :],
                             hv[:, c, st * 512:(st + 1) * 512],
                             start=(c == 0), stop=(c == NDC - 1)),
                    waits=mmw if c == 0 else [],
                    inc="s_te" if c == NDC - 1 else None)
            if which == 0:
                t_e = env.step("vector", lambda e, st=st, par=par: e.tensor_copy(
                    out=qh[par][:, st * 512:(st + 1) * 512], in_=ps_p[:]),
                    waits=[t_pm], inc="s_ve")
            elif which == 1:
                t_e = env.step("vector", lambda e, st=st, par=par: e.tensor_copy(
                    out=kpad[par][:, PADT + st * 512: PADT + (st + 1) * 512],
                    in_=ps_p[:]), waits=[t_pm], inc="s_ve")
            else:
                vw = [t_pm] + ([guards["vstg"]] if guards["vstg"] else [])
                t_vc = env.step("vector", lambda e: e.tensor_copy(
                    out=vstg[:], in_=ps_p[:]), waits=vw, inc="s_ve")
                for j in range(4):
                    tw = [t_vc] + ([guards["psvt"][0]] if guards["psvt"][0] else [])
                    t_vt = env.step("tensor", lambda e, j=j: e.transpose(
                        ps_vt[:, 0:DH],
                        vstg[:, j * 128:(j + 1) * 128], env.consts["idb"][0:64, 0:64]),
                        waits=tw, inc="s_te")
                    kt = 1 + st * 4 + j
                    t_ve = env.step("vector", lambda e, kt=kt, par=par: e.tensor_copy(
                        out=vtm[par][:, kt * DH:(kt + 1) * DH],
                        in_=ps_vt[:, 0:DH]),
                        waits=[t_vt], inc="s_ve")
                    guards["psvt"][0] = t_ve
                guards["vstg"] = t_ve
                t_e = t_ve
            guards["psp"] = t_e
            hst[h]["kv_last"] = t_e

        start_head(0)
        for st in range(8):
            for which in range(3):
                emit_group(0, st, which)

        for h in range(NH):
            par = h % 2
            if h + 1 < NH:
                start_head(h + 1)
                pend = [(st, w) for st in range(8) for w in range(3)]
            else:
                pend = []
            t_kv_last = hst[h]["kv_last"]
            t_attn_last = None
            for qt in range(NTT):
                qpar = qt % 2
                q3 = qt % 3
                p_scf = p_p16 = p_pss = p_colb = q3
                p_ptb = p_pspt = qpar
                smw = [t_kv_last]
                if guards["pss"][p_pss]:
                    smw.append(guards["pss"][p_pss])
                t_sc = env.step("tensor", lambda e, qt=qt, par=par, qpar=p_pss: e.matmul(
                    ps_s[:, qpar * 512: qpar * 512 + 384],
                    qh[par][:, qt * 128:(qt + 1) * 128],
                    kpad[par][:, qt * 128: qt * 128 + 384],
                    start=True, stop=True), waits=smw, inc="s_te")
                if pend:
                    emit_group(h + 1, *pend.pop(0))
                mw = [t_sc, guards["scf"][p_scf]] if guards["scf"][p_scf] else [t_sc]
                t_ms = env.step("vector", lambda e, qs=p_scf, qp=p_pss: e.scalar_tensor_tensor(
                    out=scf[qs][:], in0=ps_s[:, qp * 512: qp * 512 + 384],
                    scalar=0.125, in1=mask[:], op0=ALU.mult, op1=ALU.add),
                    waits=mw, inc="s_ve")
                guards["pss"][p_pss] = t_ms
                c4 = colb[p_colb]
                t_rm = env.step("vector", lambda e, qs=p_scf, c4=c4: e.reduce_max(
                    out=c4[:, 0:1], in_=scf[qs][:], axis=AX, negate=True),
                    waits=[t_ms], inc="s_ve")
                pw = [t_rm] + ([guards["p16"][p_p16]] if guards["p16"][p_p16] else [])
                t_ex = env.step("scalar", lambda e, qs=p_scf, qx=p_p16, c4=c4: e.activation(
                    out=p16[qx][:], in_=scf[qs][:], func=AF.Exp, bias=c4[:, 0:1],
                    accum_out=c4[:, 2:3]), waits=pw, inc="s_se")
                guards["scf"][p_scf] = t_ex
                t_rc = env.step("vector", lambda e, c4=c4: e.reciprocal(
                    out=c4[:, 3:4], in_=c4[:, 2:3]), waits=[t_ex], inc="s_ve")
                t_ptr = None
                for j in range(3):
                    tw = [t_ex]
                    if j == 0 and guards["ptb"][p_ptb]:
                        tw.append(guards["ptb"][p_ptb])
                    t_ptr = env.step("tensor", lambda e, j=j, qx=p_p16, qz=p_pspt: e.transpose(
                        ps_pt[:, qz * 512 + j * 128: qz * 512 + (j + 1) * 128],
                        p16[qx][:, j * 128:(j + 1) * 128], env.consts["idb"][:]),
                        waits=tw, inc="s_te" if j == 2 else None)
                guards["p16"][p_p16] = t_ptr
                t_pte = env.step("vector", lambda e, qb=p_ptb, qz=p_pspt: e.tensor_copy(
                    out=ptb[qb][:], in_=ps_pt[:, qz * 512: qz * 512 + 384]),
                    waits=[t_ptr], inc="s_ve")
                t_av = None
                for j in range(3):
                    avw = [t_pte]
                    if j == 0 and guards["psav"][qpar]:
                        avw.append(guards["psav"][qpar])
                    t_av = env.step("tensor", lambda e, j=j, qt=qt, par=par, qb=p_ptb, qpar=qpar:
                        e.matmul(ps_av[:, qpar * DH:(qpar + 1) * DH],
                                 ptb[qb][:, j * 128:(j + 1) * 128],
                                 vtm[par][:, (qt + j) * DH:(qt + j + 1) * DH],
                                 start=(j == 0), stop=(j == 2)),
                        waits=avw, inc="s_te" if j == 2 else None)
                guards["ptb"][p_ptb] = t_av
                nw_ = [t_av, t_rc] + ([guards["atmp"][qpar]] if guards["atmp"][qpar] else [])
                t_no = env.step("vector", lambda e, qpar=qpar, c4=c4: e.tensor_scalar(
                    out=atmp[qpar][:], in0=ps_av[:, qpar * DH:(qpar + 1) * DH],
                    scalar1=c4[:, 3:4], scalar2=1.0, op0=ALU.mult, op1=ALU.mult),
                    waits=nw_, inc="s_ve")
                guards["psav"][qpar] = t_no
                tw = [t_no]
                if guards["pso"][qpar]:
                    tw.append(guards["pso"][qpar])
                t_otr = env.step("tensor", lambda e, qpar=qpar: e.transpose(
                    ps_o[:, qpar * 128:(qpar + 1) * 128], atmp[qpar][:],
                    env.consts["idb"][:]), waits=tw, inc="s_te")
                guards["atmp"][qpar] = t_otr
                cch = (h * DH) // 128
                off = (h * DH) % 128
                t_oev = env.step("vector", lambda e, qpar=qpar, qt=qt, cch=cch, off=off:
                    e.tensor_copy(
                        out=attn_all[off:off + DH, cch * T + qt * 128: cch * T + (qt + 1) * 128],
                        in_=ps_o[0:DH, qpar * 128:(qpar + 1) * 128]),
                    waits=[t_otr], inc="s_ve")
                guards["pso"][qpar] = t_oev
                t_attn_last = t_oev
            guards["kv"][par] = t_attn_last
            fin_heads.append(t_attn_last)
    env.fence()
    with env.scope():
        wo_sb = env.sb("a0_wo", [128, NDC * D], BF16)
        bo_row = env.sb("a0_borow", [1, D], F32)
        bo_bc = env.sb("a0_bobc", [128, D], F32)
        with env.scope():
            ps_b = env.psum("a0_psb", [128, 512], F32)
            wov = wo_sb.rearrange("p (c j) -> p c j", c=NDC)
            t_lwo = env.dma("sync", lambda e: e.dma_start(
                out=wov[:], in_=c_wo.rearrange("(c p) j -> p c j", p=128)), sem="d_wo")
            t_lbo = env.dma("sync", lambda e: e.dma_start(out=bo_row[:], in_=bo_hbm[:]),
                            sem="d_bo")
            t_bce = None
            for half in range(2):
                bw = [t_lbo] if half == 0 else [t_bce]
                t_bcm = env.step("tensor", lambda e, half=half: e.matmul(
                    ps_b[:], env.consts["ones_f"][:], bo_row[0:1, half * 512:(half + 1) * 512],
                    start=True, stop=True), waits=bw, inc="s_te")
                t_bce = env.step("vector", lambda e, half=half: e.tensor_copy(
                    out=bo_bc[:, half * 512:(half + 1) * 512], in_=ps_b[:]),
                    waits=[t_bcm], inc="s_ve")
        env.fence()
        fin = emit_proj_residual(env, attn_all, wo_sb, bo_bc, x_in, x_out,
                                 waits=[t_lwo, t_bce])
    return fin


def emit_attn1(env, h_fm, x_in, x_out, c_wqP, c_wkP, c_wvP, c_wproj, waits=()):
    nc = env.nc
    NH, DH = 16, 64
    attn_all = env.sb("a1_attn", [128, NDC * T], BF16)
    with env.scope():
        kh = [env.sb("a1_kh", [64, T], BF16) for _ in range(2)]
        vtm = [env.sb("a1_vtm", [128, NTT * 65], BF16) for _ in range(2)]
        qh = [env.sb("a1_qh", [64, T], BF16) for _ in range(2)]
        wslc = [env.sb("a1_wslc", [128, 3 * NDC * DH], BF16) for _ in range(2)]
        vstg = env.sb("a1_vstg", [64, 512], BF16)
        sq = env.sb("a1_sq", [64, T], BF16)
        ptile = [env.sb("a1_pt", [128, 1024], BF16) for _ in range(2)]
        cmask = env.sb("a1_cmask", [128, 4 * 512], BF16)
        mskf = env.sb("a1_mskf", [128, 512], F32)
        nmax = env.sb("a1_nmax", [1, 24], F32)
        rrow = env.sb("a1_rrow", [1, 512], F32)
        rb_sb = env.sb("a1_rb", [64, 512], F32)
        cneg = env.sb("a1_cneg", [128, 1], F32)
        ps_p = env.psum("a1_psp", [64, 512], F32)
        ps_vt = env.psum("a1_psvt", [128, 2 * DH], BF16)
        ps_s = env.psum("a1_pss", [128, 4 * 512], F32)
        ps_av = env.psum("a1_psav", [65, 512], F32)
        ps_rb = env.psum("a1_psrb", [128, 512], F32)
        ps_n = ps_rb

        t_i0 = env.step("gpsimd", lambda e: e.iota(mskf[:], [[1, 512]], channel_multiplier=-1,
                        allow_small_or_imprecise_dtypes=True), waits=waits, inc="s_po")
        t_cm = None
        for r in range(4):
            t_c1 = env.step("vector", lambda e, r=r: e.tensor_scalar(
                out=cmask[:, r * 512:(r + 1) * 512], in0=mskf[:], scalar1=float(128 * r),
                scalar2=1.0, op0=ALU.is_ge, op1=ALU.mult), waits=[t_i0], inc="s_ve")
            t_cm = t_c1
        t_vo = env.step("vector", lambda e: e.memset(
            vtm[0].rearrange("p (k c) -> p k c", c=65)[:, :, 64:65], 1.0), inc="s_ve")
        t_vo2 = env.step("vector", lambda e: e.memset(
            vtm[1].rearrange("p (k c) -> p k c", c=65)[:, :, 64:65], 1.0), inc="s_ve")

        hv = h_fm.rearrange("p (c t) -> p c t", c=NDC)
        fin_heads = []
        guards = {"kv": [t_vo, t_vo2], "psp": None, "psvt": [None, None], "vstg": None,
                  "sq": None, "psn": None, "ptile": [None, None], "psav": None,
                  "pss": [None, None], "rrow": None, "rb": None, "cneg": None,
                  "nmax": None}
        hst = {}

        def start_head(h):
            par = h % 2
            ww = [guards["kv"][par]]
            if h < 2:
                ww += list(waits)
            t_lw = env.dma("sync", lambda e, h=h, par=par: e.dma_start(
                out=wslc[par][:, 0:512], in_=c_wqP[h * 128:(h + 1) * 128, :]),
                waits=ww, sem=f"d_aq{par}")
            t_lw2 = env.dma("scalar", lambda e, h=h, par=par: e.dma_start(
                out=wslc[par][:, 512:1024], in_=c_wkP[h * 128:(h + 1) * 128, :]),
                waits=ww, sem=f"d_ak{par}")
            t_lw3 = env.dma("gpsimd", lambda e, h=h, par=par: e.dma_start(
                out=wslc[par][:, 1024:1536], in_=c_wvP[h * 128:(h + 1) * 128, :]),
                waits=ww, sem=f"d_av{par}")
            hst[h] = dict(lw=[t_lw, t_lw2, t_lw3], kv_last=None)

        def emit_group(h, st, which):
            par = h % 2
            wv3 = wslc[par].rearrange("p (s c j) -> p s c j", s=3, c=NDC)
            mmw = list(hst[h]["lw"])
            if guards["psp"]:
                mmw.append(guards["psp"])
            t_pm = None
            for c in range(NDC):
                t_pm = env.step("tensor", lambda e, c=c, which=which, st=st, wv3=wv3:
                    e.matmul(ps_p[:], wv3[:, which, c, :],
                             hv[:, c, st * 512:(st + 1) * 512],
                             start=(c == 0), stop=(c == NDC - 1)),
                    waits=mmw if c == 0 else [],
                    inc="s_te" if c == NDC - 1 else None)
            if which == 0:
                t_e = env.step("vector", lambda e, st=st, par=par: e.tensor_copy(
                    out=qh[par][:, st * 512:(st + 1) * 512], in_=ps_p[:]),
                    waits=[t_pm], inc="s_ve")
            elif which == 1:
                t_e = env.step("vector", lambda e, st=st, par=par: e.tensor_copy(
                    out=kh[par][:, st * 512:(st + 1) * 512], in_=ps_p[:]),
                    waits=[t_pm], inc="s_ve")
            else:
                vw = [t_pm] + ([guards["vstg"]] if guards["vstg"] else [])
                t_vc = env.step("vector", lambda e: e.tensor_copy(
                    out=vstg[:], in_=ps_p[:]), waits=vw, inc="s_ve")
                for j in range(4):
                    tw = [t_vc] + ([guards["psvt"][0]] if guards["psvt"][0] else [])
                    t_vt = env.step("tensor", lambda e, j=j: e.transpose(
                        ps_vt[:, 0:DH], vstg[:, j * 128:(j + 1) * 128],
                        env.consts["idb"][0:64, 0:64]), waits=tw, inc="s_te")
                    kt = st * 4 + j
                    t_ve = env.step("vector", lambda e, kt=kt, par=par: e.tensor_copy(
                        out=vtm[par][:, kt * 65: kt * 65 + DH],
                        in_=ps_vt[:, 0:DH]),
                        waits=[t_vt], inc="s_ve")
                    guards["psvt"][0] = t_ve
                guards["vstg"] = t_ve
                t_e = t_ve
            guards["psp"] = t_e
            hst[h]["kv_last"] = t_e

        start_head(0)
        for st in range(8):
            for which in range(3):
                emit_group(0, st, which)

        for h in range(NH):
            par = h % 2
            t_kv_last = hst[h]["kv_last"]
            t_nm_last = None
            for which in range(2):
                src_b = qh[par] if which == 0 else kh[par]
                sw = [t_kv_last] + ([guards["sq"]] if guards["sq"] else [])
                t_sqr = env.step("scalar", lambda e, src_b=src_b: e.activation(
                    out=sq[:], in_=src_b[:], func=AF.Square), waits=sw, inc="s_se")
                for j in range(8):
                    nw_ = [t_sqr]
                    if guards["psn"]:
                        nw_.append(guards["psn"])
                    if j == 0 and which == 0 and guards["rb"]:
                        nw_.append(guards["rb"])
                    t_nmm = env.step("tensor", lambda e, j=j: e.matmul(
                        ps_n[0:1, :], env.consts["ocb"][0:64, 0:1], sq[:, j * 512:(j + 1) * 512],
                        start=True, stop=True), waits=nw_, inc="s_te")
                    nmw = [t_nmm] + ([guards["nmax"]] if guards["nmax"] and j == 0 and which == 0 else [])
                    t_nr = env.step("vector", lambda e, j=j, which=which: e.reduce_max(
                        out=nmax[0:1, which * 8 + j: which * 8 + j + 1], in_=ps_n[0:1, :],
                        axis=AX), waits=nmw, inc="s_ve")
                    guards["psn"] = t_nr
                    t_nm_last = t_nr
                guards["sq"] = t_nm_last
            t_mq = env.step("vector", lambda e: e.reduce_max(
                out=nmax[0:1, 16:17], in_=nmax[0:1, 0:8], axis=AX), waits=[t_nm_last], inc="s_ve")
            t_mk = env.step("vector", lambda e: e.reduce_max(
                out=nmax[0:1, 17:18], in_=nmax[0:1, 8:16], axis=AX), waits=[t_mq], inc="s_ve")
            t_c2 = env.step("vector", lambda e: e.tensor_tensor(
                out=nmax[0:1, 18:19], in0=nmax[0:1, 16:17], in1=nmax[0:1, 17:18],
                op=ALU.mult), waits=[t_mk], inc="s_ve")
            t_cs = env.step("scalar", lambda e: e.activation(
                out=nmax[0:1, 19:20], in_=nmax[0:1, 18:19], func=AF.Sqrt),
                waits=[t_c2], inc="s_se")
            t_cn = env.step("vector", lambda e: e.tensor_scalar(
                out=nmax[0:1, 20:21], in0=nmax[0:1, 19:20], scalar1=-0.125, scalar2=1.0,
                op0=ALU.mult, op1=ALU.mult), waits=[t_cs], inc="s_ve")
            cbw = [t_cn] + ([guards["cneg"]] if guards["cneg"] else [])
            t_cb = env.step("tensor", lambda e: e.matmul(
                ps_rb[:, 0:1], env.consts["ones_f"][:], nmax[0:1, 20:21],
                start=True, stop=True), waits=cbw, inc="s_te")
            guards["nmax"] = t_cb
            t_cv = env.step("vector", lambda e: e.tensor_copy(
                out=cneg[:], in_=ps_rb[:, 0:1]), waits=[t_cb], inc="s_ve")
            if h + 1 < NH:
                start_head(h + 1)
                pend = [(st, w) for st in range(8) for w in range(3)]
            else:
                pend = []
            gctr = 0
            sctok = [None, None]

            def emit_S(qb, mpair, t_kv_last=t_kv_last, par=par):
                sp2 = mpair % 2
                smw = [t_kv_last]
                if guards["pss"][sp2]:
                    smw.append(guards["pss"][sp2])
                t_sm = None
                for sub in range(2):
                    kc = 2 * mpair + sub
                    t_sm = env.step("tensor", lambda e, kc=kc, qb=qb, par=par, sp2=sp2, sub=sub:
                        e.matmul(ps_s[:, sp2 * 1024 + sub * 512: sp2 * 1024 + (sub + 1) * 512],
                                 kh[par][:, kc * 128:(kc + 1) * 128],
                                 qh[par][:, qb * 512:(qb + 1) * 512],
                                 start=True, stop=True),
                        waits=smw if sub == 0 else [], inc="s_te" if sub == 1 else None)
                sctok[sp2] = t_sm

            t_attn_last = None
            emit_S(0, 0)
            for qb in range(8):
                nkc = 4 * (qb + 1)
                t_av = None
                for mpair in range(nkc // 2):
                    sp2 = mpair % 2
                    if mpair + 1 < nkc // 2:
                        emit_S(qb, mpair + 1)
                    elif qb + 1 < 8:
                        emit_S(qb + 1, 0)
                    if pend and gctr % 3 == 0:
                        emit_group(h + 1, *pend.pop(0))
                    gctr += 1
                    t_sm = sctok[sp2]
                    pw = [t_sm, t_cv] + ([guards["ptile"][sp2]] if guards["ptile"][sp2] else [])
                    t_ex = env.step("scalar", lambda e, sp2=sp2: e.activation(
                        out=ptile[sp2][:], in_=ps_s[:, sp2 * 1024:(sp2 + 1) * 1024],
                        func=AF.Exp, scale=0.125, bias=cneg[:, 0:1]),
                        waits=pw, inc="s_se")
                    guards["pss"][sp2] = t_ex
                    for sub in range(2):
                        kc = 2 * mpair + sub
                        r = kc - 4 * qb
                        t_in = t_ex
                        if r >= 0:
                            t_in = env.step("vector", lambda e, sp2=sp2, sub=sub, r=r: e.tensor_tensor(
                                out=ptile[sp2][:, sub * 512:(sub + 1) * 512],
                                in0=ptile[sp2][:, sub * 512:(sub + 1) * 512],
                                in1=cmask[:, r * 512:(r + 1) * 512], op=ALU.mult),
                                waits=[t_ex, t_cm], inc="s_ve")
                        avw = [t_in]
                        if kc == 0 and guards["psav"]:
                            avw.append(guards["psav"])
                        t_av = env.step("tensor", lambda e, kc=kc, par=par, sp2=sp2, sub=sub, nkc=nkc:
                            e.matmul(ps_av[:], vtm[par][:, kc * 65:(kc + 1) * 65],
                                     ptile[sp2][:, sub * 512:(sub + 1) * 512],
                                     start=(kc == 0), stop=(kc == nkc - 1)),
                            waits=avw, inc="s_te" if sub == 1 else None)
                    guards["ptile"][sp2] = t_av
                dw = [t_av] + ([guards["rrow"]] if guards["rrow"] else [])
                t_l1 = env.step("scalar", lambda e: e.activation(
                    out=rrow[:], in_=ps_av[64:65, :], func=AF.Ln), waits=dw, inc="s_se")
                t_l2 = env.step("scalar", lambda e: e.activation(
                    out=rrow[:], in_=rrow[:], func=AF.Exp, scale=-1.0),
                    waits=[t_l1], inc="s_se")
                rbw = [t_l2, t_cv] + ([guards["rb"]] if guards["rb"] else [])
                t_rb = env.step("tensor", lambda e: e.matmul(
                    ps_rb[:], env.consts["ones_f"][:], rrow[:],
                    start=True, stop=True), waits=rbw, inc="s_te")
                t_rv = env.step("vector", lambda e: e.tensor_copy(
                    out=rb_sb[:], in_=ps_rb[0:64, :]), waits=[t_rb], inc="s_ve")
                guards["cneg"] = t_rv
                guards["rrow"] = t_rv
                cch = (h * DH) // 128
                off = (h * DH) % 128
                t_oev = env.step("vector", lambda e, qb=qb, cch=cch, off=off: e.tensor_tensor(
                    out=attn_all[off:off + DH, cch * T + qb * 512: cch * T + (qb + 1) * 512],
                    in0=ps_av[0:DH, :], in1=rb_sb[:], op=ALU.mult),
                    waits=[t_rv], inc="s_ve")
                guards["psav"] = t_oev
                guards["rb"] = t_oev
                t_attn_last = t_oev
            guards["kv"][par] = t_attn_last
            fin_heads.append(t_attn_last)
    env.fence()
    with env.scope():
        wp_sb = env.sb("a1_wp", [128, NDC * D], BF16)
        wpv = wp_sb.rearrange("p (c j) -> p c j", c=NDC)
        t_lwp = env.dma("sync", lambda e: e.dma_start(
            out=wpv[:], in_=c_wproj.rearrange("(c p) j -> p c j", p=128)), sem="d_wo")
        fin = emit_proj_residual(env, attn_all, wp_sb, None, x_in, x_out, waits=[t_lwp])
    return fin



def _pre_lhsT(wT, np2):
    a = wT.reshape(8, 128, np2, 2, 128).transpose(2, 1, 0, 3, 4)
    return np.ascontiguousarray(a).reshape(np2 * 128, 2048)


def _pre_down(dw_e, nhc):
    a = dw_e.reshape(4, 2, 128, nhc, 128).transpose(4, 0, 1, 3, 2)
    return np.ascontiguousarray(a).reshape(128, 4 * 2 * nhc * 128)


def _pre_head(wT):
    a = wT.reshape(8, 128, 16, 64).transpose(2, 1, 0, 3)
    return np.ascontiguousarray(a).reshape(2048, 512)


def _prep(inputs):
    g = {}
    f32 = np.float32
    gu_t = np.asarray(inputs["L0_ffn_gate_up_weight"], f32).transpose(0, 2, 1)
    g["wguP"] = np.ascontiguousarray(np.concatenate(
        [_pre_lhsT(np.ascontiguousarray(gu_t[e, :, half * 2816:(half + 1) * 2816]), 11)
         for e in range(8) for half in range(2)], axis=0))
    dw = np.asarray(inputs["L0_ffn_down_weight"], f32)
    g["wdP"] = np.ascontiguousarray(np.concatenate(
        [_pre_down(dw[e], 22) for e in range(8)], axis=0))
    g["routerT"] = np.ascontiguousarray(np.asarray(inputs["L0_ffn_router_weight"], f32).T)
    g["nw0"] = np.asarray(inputs["L0_ffn_norm_weight"], f32).reshape(1, D)
    g["alpha"] = np.asarray(inputs["L0_ffn_residual_alpha"], f32).reshape(1, 1)
    g["wqP"] = _pre_head(np.ascontiguousarray(np.asarray(inputs["L0_attn_Wq"], f32).T))
    g["wkP"] = _pre_head(np.ascontiguousarray(np.asarray(inputs["L0_attn_Wk"], f32).T))
    g["wvP"] = _pre_head(np.ascontiguousarray(np.asarray(inputs["L0_attn_Wv"], f32).T))
    g["wo0T"] = np.ascontiguousarray(np.asarray(inputs["L0_attn_Wo"], f32).T)
    g["bo0"] = np.asarray(inputs["L0_attn_bo"], f32).reshape(1, D)
    wqkvT = np.ascontiguousarray(np.asarray(inputs["L1_attn_w_qkv"], f32).T)
    g["q1P"] = _pre_head(np.ascontiguousarray(wqkvT[:, 0:D]))
    g["k1P"] = _pre_head(np.ascontiguousarray(wqkvT[:, D:2 * D]))
    g["v1P"] = _pre_head(np.ascontiguousarray(wqkvT[:, 2 * D:3 * D]))
    g["wprojT"] = np.ascontiguousarray(np.asarray(inputs["L1_attn_w_proj"], f32).T)
    g["wg1P"] = _pre_lhsT(np.ascontiguousarray(np.asarray(inputs["L1_ffn_w_gate"], np.int32).T), 12)
    g["wu1P"] = _pre_lhsT(np.ascontiguousarray(np.asarray(inputs["L1_ffn_w_up"], np.int32).T), 12)
    g["wd1P"] = _pre_down(np.asarray(inputs["L1_ffn_w_down"], np.int32), 24)
    xs = [np.ascontiguousarray(np.asarray(inputs["L0_attn_x"][i], f32)) for i in range(8)]
    return g, xs


def build_full(dbg=False):
    nc = bass.Bass("TRN2", target_bir_lowering=False, debug=False)
    env = Env(nc)
    f32in = {}
    for name, shape in [
        ("x0", [T, D]), ("wguP", [8 * 2 * 11 * 128, 2048]), ("wdP", [8 * 128, 22 * 1024]),
        ("routerT", [D, 8]), ("nw0", [1, D]), ("alpha", [1, 1]),
        ("wqP", [2048, 512]), ("wkP", [2048, 512]), ("wvP", [2048, 512]),
        ("wo0T", [D, D]), ("bo0", [1, D]),
        ("q1P", [2048, 512]), ("k1P", [2048, 512]), ("v1P", [2048, 512]),
        ("wprojT", [D, D]),
    ]:
        f32in[name] = nc.dram_tensor(name, shape, F32, kind="ExternalInput").ap()
    i32in = {}
    for name, shape in [("wg1P", [12 * 128, 2048]), ("wu1P", [12 * 128, 2048]),
                        ("wd1P", [128, 24 * 1024])]:
        i32in[name] = nc.dram_tensor(name, shape, I32, kind="ExternalInput").ap()
    y = nc.dram_tensor("y", [T, D], F32, kind="ExternalOutput").ap()
    xs = nc.dram_tensor("xs", [T, D], F32).ap()
    dbgo = [nc.dram_tensor(f"dbg{i}", [T, D], F32, kind="ExternalOutput").ap()
            for i in range(5)] if dbg else None
    def dump(i):
        if dbg:
            env.dma("gpsimd", lambda e, i=i: e.dma_start(out=dbgo[i][:], in_=xs[:]),
                    waits=env.join_all(), sem="d_dbg")
    c = {}
    for name in ("routerT", "wguP", "wdP", "wqP", "wkP", "wvP", "wo0T", "q1P", "k1P",
                 "v1P", "wprojT"):
        c[name] = nc.dram_tensor("c_" + name, list(f32in[name].shape), BF16).ap()
    for name in ("wg1P", "wu1P", "wd1P"):
        c[name] = nc.dram_tensor("c_" + name, list(i32in[name].shape), BF16).ap()
    with env.es:
        env.consts = emit_consts(env)
        tconv = emit_conv(env, [(f32in[n], c[n]) for n in
                                ("routerT", "wguP", "wdP", "wqP", "wkP", "wvP", "wo0T",
                                 "q1P", "k1P", "v1P", "wprojT")]
                          + [(i32in[n], c[n]) for n in ("wg1P", "wu1P", "wd1P")])
        with env.scope():
            normed_fm = env.sb("normed_fm", [128, NDC * T], BF16)
            combine_tm = env.sb("combine_tm", [128, NTT * 8], F32)
            emit_moe_sublayer(env, f32in["x0"], xs, c["wguP"], c["wdP"],
                              c["routerT"], f32in["nw0"], f32in["alpha"],
                              normed_fm, combine_tm, waits=[tconv[0], tconv[-1]])
        dump(0)
        env.fence()
        with env.scope():
            h_fm = env.sb("h_fm_a0", [128, NDC * T], BF16)
            with env.scope():
                emit_lnT(env, xs, h_fm, waits=env.consts["tok"])
            env.fence()
            emit_attn0(env, h_fm, xs, xs, c["wqP"], c["wkP"], c["wvP"], c["wo0T"],
                       f32in["bo0"], waits=[])
        dump(1)
        env.fence()
        with env.scope():
            normed_fm = env.sb("normed_fm2", [128, NDC * T], BF16)
            combine_tm = env.sb("combine_tm2", [128, NTT * 8], F32)
            emit_moe_sublayer(env, xs, xs, c["wguP"], c["wdP"],
                              c["routerT"], f32in["nw0"], f32in["alpha"],
                              normed_fm, combine_tm, waits=[])
        dump(2)
        env.fence()
        with env.scope():
            h_fm = env.sb("h_fm_f1", [128, NDC * T], BF16)
            with env.scope():
                emit_lnT(env, xs, h_fm, waits=[])
            env.fence()
            emit_ffn_swiglu(env, h_fm, xs, xs, c["wg1P"], c["wu1P"], c["wd1P"],
                            HID=3072, gate_scale=1.0 / 127.0,
                            res_scale=0.5 / (127.0 * 127.0), waits=[])
        dump(3)
        env.fence()
        with env.scope():
            h_fm = env.sb("h_fm_a1", [128, NDC * T], BF16)
            with env.scope():
                emit_lnT(env, xs, h_fm, waits=[])
            env.fence()
            emit_attn1(env, h_fm, xs, xs, c["q1P"], c["k1P"], c["v1P"], c["wprojT"],
                       waits=[])
        dump(4)
        env.fence()
        with env.scope():
            h_fm = env.sb("h_fm_f2", [128, NDC * T], BF16)
            with env.scope():
                emit_lnT(env, xs, h_fm, waits=[])
            env.fence()
            fin = emit_ffn_swiglu(env, h_fm, xs, y, c["wg1P"], c["wu1P"], c["wd1P"],
                                  HID=3072, gate_scale=1.0 / 127.0,
                                  res_scale=0.5 / (127.0 * 127.0), waits=[])
        env.wait_only("gpsimd", fin + env.join_all())
        env.emit()
    names = list(f32in) + list(i32in)
    return nc, names


_FULL_CACHE = []


def kernel_fused(**inputs):
    g, xs_np = _prep(inputs)
    if not _FULL_CACHE:
        _FULL_CACHE.append(build_full())
    nc, innames = _FULL_CACHE[0]
    in_maps = []
    for i in range(8):
        m = {k: g[k] for k in innames if k != "x0"}
        m["x0"] = xs_np[i]
        in_maps.append(m)
    res = run_bass_kernel_spmd(nc, in_maps, core_ids=list(range(8)))
    return np.stack([np.asarray(r["y"]) for r in res.results], axis=0).astype(np.float32)


_PROG_CACHE = {}


def _get_prog(name):
    if name not in _PROG_CACHE:
        _PROG_CACHE[name] = {
            "moe": build_moe_only, "attn0": build_attn0_only,
            "ffn1": build_ffn1_only, "attn1": build_attn1_only,
        }[name]()
    return _PROG_CACHE[name]


def _launch(name, g, xs):
    nc, innames = _get_prog(name)
    in_maps = []
    for i in range(8):
        m = {k: g[k] for k in innames if k != "x0"}
        m["x0"] = xs[i]
        in_maps.append(m)
    res = run_bass_kernel_spmd(nc, in_maps, core_ids=list(range(8)))
    return [np.asarray(r["y"]) for r in res.results]


def kernel(**inputs):
    return kernel_fused(**inputs)


def build_moe_only():
    nc = bass.Bass("TRN2", target_bir_lowering=False, debug=False)
    env = Env(nc)
    x0 = nc.dram_tensor("x0", [T, D], F32, kind="ExternalInput").ap()
    y = nc.dram_tensor("y", [T, D], F32, kind="ExternalOutput").ap()
    wgu = nc.dram_tensor("wgu0T", [8 * D, 5632], F32, kind="ExternalInput").ap()
    wd0 = nc.dram_tensor("wd0T", [8 * 2816, D], F32, kind="ExternalInput").ap()
    router = nc.dram_tensor("routerT", [D, 8], F32, kind="ExternalInput").ap()
    nw = nc.dram_tensor("nw0", [1, D], F32, kind="ExternalInput").ap()
    alpha = nc.dram_tensor("alpha", [1, 1], F32, kind="ExternalInput").ap()
    c_wgu = nc.dram_tensor("c_wgu", [8 * D, 5632], BF16).ap()
    c_wd0 = nc.dram_tensor("c_wd0", [8 * 2816, D], BF16).ap()
    with env.es:
        env.consts = emit_consts(env)
        tconv = emit_conv(env, [(wgu, c_wgu), (wd0, c_wd0)])
        normed_fm = env.sb("normed_fm", [128, NDC * T], BF16)
        combine_tm = env.sb("combine_tm", [128, NTT * 8], F32)
        fin = emit_moe_sublayer(env, x0, y, c_wgu, c_wd0, router, nw, alpha,
                                normed_fm, combine_tm, waits=[tconv[-1]])
        env.wait_only("gpsimd", fin)
        env.emit()
    return nc, ["x0", "wgu0T", "wd0T", "routerT", "nw0", "alpha"]


def build_attn0_only():
    nc = bass.Bass("TRN2", target_bir_lowering=False, debug=False)
    env = Env(nc)
    x0 = nc.dram_tensor("x0", [T, D], F32, kind="ExternalInput").ap()
    y = nc.dram_tensor("y", [T, D], F32, kind="ExternalOutput").ap()
    wq = nc.dram_tensor("wqP", [2048, 512], F32, kind="ExternalInput").ap()
    wk = nc.dram_tensor("wkP", [2048, 512], F32, kind="ExternalInput").ap()
    wv = nc.dram_tensor("wvP", [2048, 512], F32, kind="ExternalInput").ap()
    wo = nc.dram_tensor("wo0T", [D, D], F32, kind="ExternalInput").ap()
    bo = nc.dram_tensor("bo0", [1, D], F32, kind="ExternalInput").ap()
    cs = {n: nc.dram_tensor("c_" + n, list(ap.shape), BF16).ap()
          for n, ap in (("wqP", wq), ("wkP", wk), ("wvP", wv), ("wo0T", wo))}
    with env.es:
        env.consts = emit_consts(env)
        tconv = emit_conv(env, [(wq, cs["wqP"]), (wk, cs["wkP"]), (wv, cs["wvP"]),
                                (wo, cs["wo0T"])])
        h_fm = env.sb("h_fm", [128, NDC * T], BF16)
        with env.scope():
            tln = emit_lnT(env, x0, h_fm, waits=env.consts["tok"])
        env.fence()
        with env.scope():
            fin = emit_attn0(env, h_fm, x0, y, cs["wqP"], cs["wkP"], cs["wvP"],
                             cs["wo0T"], bo, waits=[tconv[-1]])
        env.wait_only("gpsimd", fin)
        env.emit()
    return nc, ["x0", "wqP", "wkP", "wvP", "wo0T", "bo0"]


def build_attn1_only():
    nc = bass.Bass("TRN2", target_bir_lowering=False, debug=False)
    env = Env(nc)
    x0 = nc.dram_tensor("x0", [T, D], F32, kind="ExternalInput").ap()
    y = nc.dram_tensor("y", [T, D], F32, kind="ExternalOutput").ap()
    qP = nc.dram_tensor("q1P", [2048, 512], F32, kind="ExternalInput").ap()
    kP = nc.dram_tensor("k1P", [2048, 512], F32, kind="ExternalInput").ap()
    vP = nc.dram_tensor("v1P", [2048, 512], F32, kind="ExternalInput").ap()
    wproj = nc.dram_tensor("wprojT", [D, D], F32, kind="ExternalInput").ap()
    cs = {n: nc.dram_tensor("c_" + n, list(ap.shape), BF16).ap()
          for n, ap in (("q1P", qP), ("k1P", kP), ("v1P", vP), ("wprojT", wproj))}
    with env.es:
        env.consts = emit_consts(env)
        tconv = emit_conv(env, [(qP, cs["q1P"]), (kP, cs["k1P"]), (vP, cs["v1P"]),
                                (wproj, cs["wprojT"])])
        h_fm = env.sb("h_fm", [128, NDC * T], BF16)
        with env.scope():
            tln = emit_lnT(env, x0, h_fm, waits=env.consts["tok"])
        env.fence()
        with env.scope():
            fin = emit_attn1(env, h_fm, x0, y, cs["q1P"], cs["k1P"], cs["v1P"],
                             cs["wprojT"], waits=[tconv[-1]])
        env.wait_only("gpsimd", fin)
        env.emit()
    return nc, ["x0", "q1P", "k1P", "v1P", "wprojT"]


def build_ffn1_only():
    nc = bass.Bass("TRN2", target_bir_lowering=False, debug=False)
    env = Env(nc)
    x0 = nc.dram_tensor("x0", [T, D], F32, kind="ExternalInput").ap()
    y = nc.dram_tensor("y", [T, D], F32, kind="ExternalOutput").ap()
    wg1 = nc.dram_tensor("wg1P", [12 * 128, 2048], I32, kind="ExternalInput").ap()
    wu1 = nc.dram_tensor("wu1P", [12 * 128, 2048], I32, kind="ExternalInput").ap()
    wd1 = nc.dram_tensor("wd1P", [128, 24576], I32, kind="ExternalInput").ap()
    cg = nc.dram_tensor("c_wg1P", [12 * 128, 2048], BF16).ap()
    cu = nc.dram_tensor("c_wu1P", [12 * 128, 2048], BF16).ap()
    cd = nc.dram_tensor("c_wd1P", [128, 24576], BF16).ap()
    with env.es:
        env.consts = emit_consts(env)
        tconv = emit_conv(env, [(wg1, cg), (wu1, cu), (wd1, cd)])
        h_fm = env.sb("h_fm", [128, NDC * T], BF16)
        with env.scope():
            tln = emit_lnT(env, x0, h_fm, waits=env.consts["tok"])
        env.fence()
        with env.scope():
            tf = emit_ffn_swiglu(env, h_fm, x0, y, cg, cu, cd, HID=3072,
                                 gate_scale=1.0 / 127.0, res_scale=0.5 / (127.0 * 127.0),
                                 waits=[tconv[-1]])
        env.wait_only("gpsimd", tf)
        env.emit()
    return nc, ["x0", "wg1P", "wu1P", "wd1P"]




# version 16 (final)
# speedup vs baseline: 1.1659x; 1.0089x over previous
import contextlib
import numpy as np
import ml_dtypes
import concourse.bass as bass
import concourse.mybir as mybir
from concourse.bass_utils import run_bass_kernel_spmd

F32 = mybir.dt.float32
BF16 = mybir.dt.bfloat16
I32 = mybir.dt.int32
AF = mybir.ActivationFunctionType
ALU = mybir.AluOpType
AX = mybir.AxisListType.X
BF = ml_dtypes.bfloat16

T = 4096
D = 1024
NTT = T // 128
NDC = D // 128


ENGINES = ("sync", "tensor", "vector", "scalar", "gpsimd")


class Env:

    def __init__(self, nc):
        self.nc = nc
        self.es = contextlib.ExitStack()
        self.scopes = [self.es]
        self.steps = []
        self.sems = {}
        self.cnt = {}
        self.uid = 0
        self.final_tokens = []

    @contextlib.contextmanager
    def scope(self):
        s = contextlib.ExitStack()
        self.scopes.append(s)
        try:
            yield
        finally:
            self.scopes.pop()
            s.close()

    def fence(self):
        self.steps.append(("__fence__", None, tuple(self.join_all()), None))

    def name(self, base):
        self.uid += 1
        return f"{base}_{self.uid}"

    def sem(self, name):
        if name not in self.sems:
            self.sems[name] = self.es.enter_context(self.nc.semaphore(name))
            self.cnt[name] = 0
        return name

    def sb(self, base, shape, dt):
        return self.scopes[-1].enter_context(self.nc.sbuf_tensor(self.name(base), shape, dt))

    def psum(self, base, shape, dt=F32):
        return self.scopes[-1].enter_context(self.nc.psum_tensor(self.name(base), shape, dt))

    def step(self, eng, fn, waits=(), inc=None):
        tok = None
        if inc is not None:
            if isinstance(inc, str):
                inc = (inc, 1)
            s, n = inc
            self.sem(s)
            self.cnt[s] += n
            tok = (s, self.cnt[s])
        ws = tuple((s, v) for (s, v) in waits if v > 0)
        for s, _v in ws:
            self.sem(s)
        self.steps.append((eng, fn, ws, inc))
        return tok

    def dma(self, queue, fn, waits=(), sem=None):
        if sem is None:
            sem = {"sync": "d_sy", "scalar": "d_sc", "gpsimd": "d_gp"}[queue]
        return self.step(queue, fn, waits=waits, inc=(sem, 16))

    def emit(self):
        block = self.es.enter_context(self.nc.Block())
        env = self

        def mk2(eng_name):
            def body(eng):
                waited = {}
                pending = {}
                for (e, fn, waits, inc) in env.steps:
                    if e == "__fence__":
                        for (s, v) in waits:
                            if pending.get(s, 0) < v:
                                pending[s] = v
                        continue
                    if e != eng_name:
                        continue
                    if pending:
                        for s, v in pending.items():
                            if waited.get(s, 0) < v:
                                eng.wait_ge(env.sems[s], v)
                                waited[s] = v
                        pending = {}
                    for (s, v) in waits:
                        if waited.get(s, 0) < v:
                            eng.wait_ge(env.sems[s], v)
                            waited[s] = v
                    if fn is None:
                        continue
                    ins = fn(eng)
                    if inc is not None:
                        s, n = inc
                        ins.then_inc(env.sems[s], n)
            return body

        block.sync(mk2("sync"))
        block.tensor(mk2("tensor"))
        block.vector(mk2("vector"))
        block.scalar(mk2("scalar"))
        block.gpsimd(mk2("gpsimd"))

    def join_all(self):
        return [(s, v) for s, v in self.cnt.items()]

    def wait_only(self, eng, waits):
        self.steps.append((eng, None, tuple((s, v) for (s, v) in waits if v > 0), None))
        for s, _v in waits:
            self.sem(s)



def emit_consts(env):
    nc = env.nc
    idf = env.sb("idf", [128, 128], F32)
    idb = env.sb("idb", [128, 128], BF16)
    ones_f = env.sb("ones_f", [1, 128], F32)
    ones_b = env.sb("ones_b", [1, 128], BF16)
    t_io = env.step("gpsimd", lambda e: e.iota(idf[:], [[1, 128]], channel_multiplier=-1,
                                              allow_small_or_imprecise_dtypes=True),
                    inc="s_po")
    t_id = env.step("vector", lambda e: e.tensor_scalar(out=idf[:], in0=idf[:],
                    scalar1=0.0, scalar2=1.0, op0=ALU.is_equal, op1=ALU.mult),
                    waits=[t_io], inc="s_ve")
    t_idb = env.step("vector", lambda e: e.tensor_copy(out=idb[:], in_=idf[:]),
                     waits=[t_id], inc="s_ve")
    t_o1 = env.step("vector", lambda e: e.memset(ones_f[:], 1.0), inc="s_ve")
    t_o2 = env.step("vector", lambda e: e.memset(ones_b[:], 1.0), inc="s_ve")
    ocb = env.sb("ocb", [128, 1], BF16)
    t_o3 = env.step("vector", lambda e: e.memset(ocb[:], 1.0), inc="s_ve")
    return dict(idf=idf, idb=idb, ones_f=ones_f, ones_b=ones_b, ocb=ocb,
                tok=[t_id, t_idb, t_o1, t_o2, t_o3])



def emit_conv(env, pairs):
    toks = []
    for src, dst in pairs:
        R, C = src.shape
        rows_per = max(1, (4 << 20) // (C * 4))
        r = 0
        while r < R:
            r1 = min(R, r + rows_per)
            toks.append(env.dma("gpsimd",
                lambda e, src=src, dst=dst, r=r, r1=r1:
                    e.dma_start(out=dst[r:r1, :], in_=src[r:r1, :])))
            r = r1
    flush = env.sb("convflush", [1, 64], BF16)
    src0, dst0 = pairs[0]
    nfl = min(64, dst0.shape[1])
    toks.append(env.dma("gpsimd", lambda e, nfl=nfl: e.dma_start(
        out=flush[0:1, 0:nfl], in_=dst0[0:1, 0:nfl])))
    return toks



def emit_lnT(env, x_in, h_fm, waits=(), apply_fn=None, post_fn=None):
    nc = env.nc
    xt = [env.sb("ln_xt", [128, D], F32) for _ in range(2)]
    dump = env.sb("ln_dump", [128, D], F32)
    col = [env.sb("ln_col", [128, 16], F32) for _ in range(2)]
    lnb = [env.sb("ln_lnb", [128, D], BF16) for _ in range(2)]
    pst = env.psum("ln_pst", [128, 2 * D], BF16)

    done = []
    prev_evac = [None, None]
    prev_col = [None, None]
    prev_sq = [None, None]
    for tt in range(NTT):
        par = tt % 2
        w = list(waits) if tt < 2 else []
        if prev_evac[par] is not None:
            w.append(prev_evac[par])
        t_x = env.dma("sync", lambda e, tt=tt, par=par: e.dma_start(
            out=xt[par][:], in_=x_in[tt * 128:(tt + 1) * 128, :]), waits=w,
            sem=f"d_ln{par}")
        cw = [t_x] + ([prev_col[par]] if prev_col[par] else [])
        c = col[par]
        sx2 = c[:, 0:1]; sx = c[:, 1:2]; m = c[:, 2:3]; m2 = c[:, 3:4]
        v = c[:, 4:5]; sdev = c[:, 5:6]; r = c[:, 6:7]
        sqw = cw + ([prev_sq[par]] if prev_sq[par] else [])
        t_sq = env.step("scalar", lambda e, par=par, sx2=sx2: e.activation(
            out=dump[:], in_=xt[par][:], func=AF.Square, accum_out=sx2), waits=sqw, inc="s_se")
        prev_sq[par] = t_sq
        t_sm = env.step("vector", lambda e, par=par, sx=sx: e.reduce_sum(
            out=sx, in_=xt[par][:], axis=AX), waits=cw, inc="s_ve")
        t_m = env.step("vector", lambda e, sx=sx, m=m: e.tensor_scalar(
            out=m, in0=sx, scalar1=1.0 / D, scalar2=1.0, op0=ALU.mult, op1=ALU.mult),
            waits=[t_sm], inc="s_ve")
        t_m2 = env.step("vector", lambda e, m=m, m2=m2: e.tensor_scalar(
            out=m2, in0=m, scalar1=m, scalar2=1e-5, op0=ALU.mult, op1=ALU.subtract),
            waits=[t_m], inc="s_ve")
        t_v = env.step("vector", lambda e, sx2=sx2, m2=m2, v=v: e.scalar_tensor_tensor(
            out=v, in0=sx2, scalar=1.0 / D, in1=m2, op0=ALU.mult, op1=ALU.subtract),
            waits=[t_sq, t_m2], inc="s_ve")
        t_sd = env.step("scalar", lambda e, v=v, sdev=sdev: e.activation(
            out=sdev, in_=v, func=AF.Sqrt), waits=[t_v], inc="s_se")
        t_r = env.step("vector", lambda e, sdev=sdev, r=r: e.reciprocal(
            out=r, in_=sdev), waits=[t_sd], inc="s_ve")
        if apply_fn is None:
            t_ln = env.step("vector", lambda e, par=par, m=m, r=r: e.tensor_scalar(
                out=lnb[par][:], in0=xt[par][:], scalar1=m, scalar2=r,
                op0=ALU.subtract, op1=ALU.mult), waits=[t_r], inc="s_ve")
        else:
            t_ln = apply_fn(tt, par, xt[par], lnb[par], c, dict(t_r=t_r, t_x=t_x))
        prev_col[par] = t_ln
        trw = [t_ln] + ([prev_evac[par]] if prev_evac[par] else [])
        t_tr = None
        for cch in range(NDC):
            t_tr = env.step("tensor", lambda e, par=par, cch=cch: e.transpose(
                pst[:, par * D + cch * 128: par * D + (cch + 1) * 128],
                lnb[par][:, cch * 128:(cch + 1) * 128], env.consts["idb"][:]),
                waits=trw if cch == 0 else [], inc="s_te" if cch == NDC - 1 else None)
        hv = h_fm.rearrange("p (c t) -> p c t", c=NDC)
        pv = pst.rearrange("p (b c f) -> p b c f", b=2, c=NDC)
        t_ev = env.step("vector", lambda e, par=par, tt=tt, hv=hv, pv=pv: e.tensor_copy(
            out=hv[:, :, tt * 128:(tt + 1) * 128], in_=pv[:, par]),
            waits=[t_tr], inc="s_ve")
        prev_evac[par] = t_ev
        if post_fn is not None:
            post_fn(tt, par, t_ev)
        done.append(t_ev)
    return done[-2:]



def emit_ffn_core(env, h_fm, x_in, x_out, E, wg_fn, wu_fn, wd_fn, HID, gate_scale,
                  res_scale, waits=(), cbc_src=None):
    nc = env.nc
    NHC = HID // 128
    NB = T // 512
    NG = NB * E
    NWB = 4
    assert NHC % 2 == 0
    wg_buf = [env.sb("f_wg", [128, NDC * 256], BF16) for _ in range(NWB)]
    wu_buf = [env.sb("f_wu", [128, NDC * 256], BF16) for _ in range(NWB)]
    wdq = [env.sb("f_wdq", [128, 2 * NHC * 128], BF16) for _ in range(2)]
    hid = [env.sb("f_hid", [128, NHC * 512], BF16) for _ in range(2)]
    sg = [env.sb("f_sg", [128, 512], F32) for _ in range(2)]
    tmp16 = [env.sb("f_tmp16", [128, 512], BF16) for _ in range(2)] if cbc_src is not None else None
    cbc = [env.sb("f_cbc", [128, 512], BF16) for _ in range(2)] if cbc_src is not None else None
    crow = env.sb("f_crow", [1, 512], F32) if cbc_src is not None else None
    acc = env.sb("f_acc", [128, 4096], F32)
    x_blk = env.sb("f_xblk", [128, 2048], F32)
    ps = env.psum("f_ps", [128, 4096], F32)

    def psb(i):
        return ps[:, i * 512:(i + 1) * 512]

    hfv = h_fm.rearrange("p (c t) -> p c t", c=NDC)
    wdqv = [w.rearrange("p (d hc j) -> p d hc j", d=2, hc=NHC) for w in wdq]

    fin = []
    st = {
        "hid_user": [None, None],
        "gate_bank_user": [None] * 4,
        "pair_user": [None, None],
        "x_wb": None,
        "sg_user": [None, None],
        "tmp_user": [None, None],
        "cbc_user": [None, None],
        "cbc_tok": [None, None],
        "crow_user": None,
        "wg_used": [None] * NWB,
        "wdq_mm": [None, None],
        "wdq_tok": [None] * 4,
        "hid_toks": [None, None],
        "acc_toks": {},
        "wpar": 0,
    }

    def emit_cbc(g, extra_waits=()):
        blk_, e_ = divmod(g, E)
        idx = g % 2
        ctv = cbc_src.rearrange("p (tt ex) -> p tt ex", ex=8)
        cw = [x for x in (st["pair_user"][1],) if x] + list(extra_waits)
        t_ctr = None
        for j in range(4):
            t_ctr = env.step("tensor", lambda e2, j=j, e_=e_, blk_=blk_: e2.matmul(
                ps[0:1, 3072 + j * 128: 3072 + (j + 1) * 128],
                ctv[:, 4 * blk_ + j, e_:e_ + 1], env.consts["idf"][:],
                start=True, stop=True),
                waits=cw if j == 0 else [], inc="s_te" if j == 3 else None)
        crw = [t_ctr] + ([st["crow_user"]] if st["crow_user"] else [])
        t_crow = env.step("vector", lambda e2: e2.tensor_copy(
            out=crow[:], in_=ps[0:1, 3072:3584]), waits=crw, inc="s_ve")
        t_cmm = env.step("tensor", lambda e2: e2.matmul(
            psb(6), env.consts["ones_f"][:], crow[:],
            start=True, stop=True), waits=[t_crow], inc="s_te")
        st["crow_user"] = t_cmm
        ccw = [t_cmm] + ([st["cbc_user"][idx]] if st["cbc_user"][idx] else [])
        t_cbc = env.step("vector", lambda e2, idx=idx: e2.tensor_copy(
            out=cbc[idx][:], in_=psb(6)), waits=ccw, inc="s_ve")
        st["cbc_tok"][idx] = t_cbc
        return t_cbc

    def emit_gate(g, first_waits=()):
        blk, e = divmod(g, E)
        gpar = g % 2
        t_cbc = st["cbc_tok"][gpar] if cbc_src is not None else None
        t_hid = {}
        for p2 in range(NHC // 2):
            par = st["wpar"]
            st["wpar"] = (st["wpar"] + 1) % NWB
            w_w = [st["wg_used"][par]] if st["wg_used"][par] else []
            if g == 0 and p2 < 2:
                w_w += list(first_waits)
            t_lg = env.dma("sync", lambda e_, p2=p2, par=par, e=e: e_.dma_start(
                out=wg_buf[par][:], in_=wg_fn(e, p2)), waits=w_w, sem=f"d_wg{par}")
            t_lu = env.dma("scalar", lambda e_, p2=p2, par=par, e=e: e_.dma_start(
                out=wu_buf[par][:], in_=wu_fn(e, p2)), waits=w_w, sem=f"d_wu{par}")
            for i in range(2):
                hc = 2 * p2 + i
                bg = (hc % 2) * 2
                bu = bg + 1
                mmw = [t_lg, t_lu]
                for b in (bg, bu):
                    if st["gate_bank_user"][b]:
                        mmw.append(st["gate_bank_user"][b])
                t_g = t_u = None
                for c in range(NDC):
                    t_g = env.step("tensor", lambda e_, c=c, i=i, bg=bg, par=par, blk=blk:
                        e_.matmul(psb(bg), wg_buf[par][:, c * 256 + i * 128: c * 256 + (i + 1) * 128],
                                  hfv[:, c, blk * 512:(blk + 1) * 512],
                                  start=(c == 0), stop=(c == NDC - 1)),
                        waits=mmw if c == 0 else [], inc="s_te" if c == NDC - 1 else None)
                for c in range(NDC):
                    t_u = env.step("tensor", lambda e_, c=c, i=i, bu=bu, par=par, blk=blk:
                        e_.matmul(psb(bu), wu_buf[par][:, c * 256 + i * 128: c * 256 + (i + 1) * 128],
                                  hfv[:, c, blk * 512:(blk + 1) * 512],
                                  start=(c == 0), stop=(c == NDC - 1)),
                        waits=[], inc="s_te" if c == NDC - 1 else None)
                if i == 1:
                    st["wg_used"][par] = t_u
                spar = hc % 2
                sgw = [t_g] + ([st["sg_user"][spar]] if st["sg_user"][spar] else [])
                t_si = env.step("scalar", lambda e_, spar=spar, bg=bg: e_.activation(
                    out=sg[spar][:], in_=psb(bg), func=AF.Silu, scale=gate_scale),
                    waits=sgw, inc="s_se")
                st["gate_bank_user"][bg] = t_si
                hu = st["hid_user"][gpar]
                if cbc_src is None:
                    hw_ = [t_si, t_u] + ([hu] if hu else [])
                    t_h = env.step("vector", lambda e_, gpar=gpar, bu=bu, hc=hc, spar=spar:
                        e_.tensor_tensor(
                            out=hid[gpar][:, hc * 512:(hc + 1) * 512], in0=sg[spar][:],
                            in1=psb(bu), op=ALU.mult), waits=hw_, inc="s_ve")
                    st["gate_bank_user"][bu] = t_h
                    st["sg_user"][spar] = t_h
                else:
                    tw = [t_si, t_u] + ([st["tmp_user"][spar]] if st["tmp_user"][spar] else [])
                    t_t = env.step("vector", lambda e_, spar=spar, bu=bu: e_.tensor_tensor(
                        out=tmp16[spar][:], in0=sg[spar][:], in1=psb(bu),
                        op=ALU.mult), waits=tw, inc="s_ve")
                    st["gate_bank_user"][bu] = t_t
                    st["sg_user"][spar] = t_t
                    pw = [t_t, t_cbc] + ([hu] if hu else [])
                    t_h = env.step("gpsimd", lambda e_, spar=spar, hc=hc, gpar=gpar: e_.tensor_tensor(
                        out=hid[gpar][:, hc * 512:(hc + 1) * 512], in0=tmp16[spar][:],
                        in1=cbc[gpar][:],
                        op=ALU.mult), waits=pw, inc="s_po")
                    st["tmp_user"][spar] = t_h
                t_hid[hc] = t_h
        if cbc_src is not None:
            st["cbc_user"][gpar] = t_hid[NHC - 1]
        st["hid_toks"][gpar] = t_hid

    def emit_wdq(g, p, extra=()):
        pr2 = p % 2
        dw_w = [st["wdq_mm"][pr2]] if st["wdq_mm"][pr2] else []
        dw_w += list(extra)
        seg = 2 * NHC * 128
        st["wdq_tok"][p] = env.dma("gpsimd", lambda e_, g=g, p=p, pr2=pr2, seg=seg:
            e_.dma_start(out=wdq[pr2][:],
                         in_=wd_fn(g % E)[:, p * seg:(p + 1) * seg]),
            waits=dw_w, sem=f"d_wd{pr2}")

    def emit_down(g, t_cbc_next):
        blk, e = divmod(g, E)
        gpar = g % 2
        t_hid = st["hid_toks"][gpar]
        t_down = None
        for p in range(4):
            pr = p % 2
            b0 = 4 + 2 * pr
            t_down = None
            for hc in range(NHC):
                mmw = [st["wdq_tok"][p], t_hid[hc]]
                if hc == 0 and st["pair_user"][pr]:
                    mmw.append(st["pair_user"][pr])
                if hc == 0 and pr == 1 and t_cbc_next is not None:
                    mmw.append(t_cbc_next)
                for d in range(2):
                    doc = 2 * p + d
                    t_down = env.step("tensor", lambda e_, hc=hc, doc=doc, d=d, b0=b0, pr=pr:
                        e_.matmul(psb(b0 + d), wdqv[pr][:, d, hc, :],
                                  hid[gpar][:, hc * 512:(hc + 1) * 512],
                                  start=(hc == 0), stop=(hc == NHC - 1)),
                        waits=mmw if d == 0 else [],
                        inc="s_te" if (d == 1) else None)
            st["wdq_mm"][pr] = t_down
            if p < 2:
                emit_wdq(g, p + 2)
            for d in range(2):
                doc = 2 * p + d
                if e == 0:
                    st["acc_toks"][doc] = env.step("vector", lambda e_, doc=doc, d=d, b0=b0:
                        e_.tensor_copy(
                            out=acc[:, doc * 512:(doc + 1) * 512], in_=psb(b0 + d)),
                        waits=[t_down], inc="s_ve")
                else:
                    st["acc_toks"][doc] = env.step("vector", lambda e_, doc=doc, d=d, b0=b0:
                        e_.tensor_tensor(
                            out=acc[:, doc * 512:(doc + 1) * 512],
                            in0=acc[:, doc * 512:(doc + 1) * 512], in1=psb(b0 + d),
                            op=ALU.add),
                        waits=[t_down], inc="s_ve")
            st["pair_user"][pr] = st["acc_toks"][2 * p + 1]
        st["hid_user"][gpar] = t_down

    def emit_tail(blk):
        t_adds = []
        for rnd in range(2):
            xw = [st["x_wb"]] if st["x_wb"] else []
            xv = x_blk.rearrange("p (a d) -> p a d", a=2)
            t_xl = env.dma("gpsimd", lambda e_, blk=blk, rnd=rnd, xv=xv: e_.dma_start(
                out=xv[:], in_=x_in[blk * 512 + rnd * 256: blk * 512 + (rnd + 1) * 256, :]
                .rearrange("(a p) d -> p a d", p=128)), waits=xw, sem="d_fxl")
            t_tr = {}
            t_add = {}
            for dt in (2 * rnd, 2 * rnd + 1):
                base = 2048 + (dt % 2) * 1024
                for doc in range(8):
                    trw = []
                    if dt == 2 * rnd and doc == 0:
                        if rnd == 0:
                            trw = [st["acc_toks"][b] for b in range(8)] + \
                                  [x for x in st["pair_user"] if x]
                        else:
                            trw = t_adds[-2:]
                    t_tr[(dt, doc)] = env.step("tensor", lambda e_, dt=dt, doc=doc, base=base:
                        e_.transpose(
                            ps[:, base + doc * 128: base + (doc + 1) * 128],
                            acc[:, doc * 512 + dt * 128: doc * 512 + (dt + 1) * 128],
                            env.consts["idf"][:]),
                        waits=trw, inc="s_te" if doc == 7 else None)
            for li, dt in enumerate((2 * rnd, 2 * rnd + 1)):
                base = 2048 + (dt % 2) * 1024
                t_add[dt] = env.step("vector", lambda e_, li=li, base=base:
                    e_.scalar_tensor_tensor(
                        out=x_blk[:, li * 1024:(li + 1) * 1024], in0=ps[:, base: base + 1024],
                        scalar=res_scale, in1=x_blk[:, li * 1024:(li + 1) * 1024],
                        op0=ALU.mult, op1=ALU.add),
                    waits=[t_tr[(dt, 7)], t_xl], inc="s_ve")
                t_adds.append(t_add[dt])
            t_wb = env.dma("gpsimd", lambda e_, blk=blk, rnd=rnd, xv=xv: e_.dma_start(
                out=x_out[blk * 512 + rnd * 256: blk * 512 + (rnd + 1) * 256, :]
                .rearrange("(a p) d -> p a d", p=128),
                in_=xv[:]), waits=[t_add[2 * rnd], t_add[2 * rnd + 1]], sem="d_xw")
            st["x_wb"] = t_wb
        for pr in range(2):
            st["pair_user"][pr] = t_adds[-1]
        fin.append(st["x_wb"])

    for g in range(NG):
        if cbc_src is not None and g == 0:
            emit_cbc(0, extra_waits=list(waits))
        emit_gate(g, first_waits=waits)
        t_cbc_next = emit_cbc(g + 1) if (cbc_src is not None and g + 1 < NG) else None
        if g >= 1:
            emit_down(g - 1, t_cbc_next)
            if g % E == 0:
                emit_tail(g // E - 1)
        emit_wdq(g, 0, extra=list(waits) if g == 0 else ())
        emit_wdq(g, 1)
    emit_down(NG - 1, None)
    emit_tail(NB - 1)
    return fin


def emit_ffn_swiglu(env, h_fm, x_in, x_out, wgT, wuT, wdT, HID, gate_scale, res_scale,
                    waits=()):
    return emit_ffn_core(env, h_fm, x_in, x_out, 1,
                         lambda e, p2: wgT[p2 * 128:(p2 + 1) * 128, :],
                         lambda e, p2: wuT[p2 * 128:(p2 + 1) * 128, :],
                         lambda e: wdT, HID, gate_scale, res_scale, waits=waits)



def emit_moe_sublayer(env, x_in, x_out, c_wgu, c_wd0, routerT_hbm, nw_hbm, alpha_hbm,
                      normed_fm, combine_tm, waits=()):
    nc = env.nc
    with env.scope():
        nw_bc = env.sb("m_nwbc", [128, D], F32)
        alpha_col = env.sb("m_alpha", [128, 1], F32)
        router_sb = env.sb("m_router", [128, NDC * 8], BF16)
        nw_row = env.sb("m_nwrow", [1, D], F32)
        al_row = env.sb("m_alrow", [1, 2], F32)
        nrm = [env.sb("m_nrm", [128, D], F32) for _ in range(2)]
        eq1 = env.sb("m_eq1", [128, 8], F32)
        lm = env.sb("m_lm", [128, 8], F32)
        eq2 = env.sb("m_eq2", [128, 8], F32)
        rcol = env.sb("m_rcol", [128, 4], F32)
        ps_r = env.psum("m_psr", [128, 8], F32)
        ps_b = env.psum("m_psb", [128, 512], F32)

        rv = router_sb.rearrange("p (c e) -> p c e", c=NDC)
        t_rt = env.dma("sync", lambda e: e.dma_start(
            out=rv[:], in_=routerT_hbm.rearrange("(c p) e -> p c e", p=128)), waits=waits,
            sem="d_m0r")
        t_nw = env.dma("sync", lambda e: e.dma_start(out=nw_row[:], in_=nw_hbm[:]),
                       sem="d_m0n")
        t_al = env.dma("sync", lambda e: e.dma_start(out=al_row[0:1, 0:1], in_=alpha_hbm[:]),
                       sem="d_m0a")
        t_bc1e = None
        for half in range(2):
            bw = [t_nw] + list(env.consts["tok"]) if half == 0 else [t_bc1e]
            t_bc1 = env.step("tensor", lambda e, half=half: e.matmul(
                ps_b[:], env.consts["ones_f"][:], nw_row[0:1, half * 512:(half + 1) * 512],
                start=True, stop=True), waits=bw, inc="s_te")
            t_bc1e = env.step("vector", lambda e, half=half: e.tensor_copy(
                out=nw_bc[:, half * 512:(half + 1) * 512], in_=ps_b[:]),
                waits=[t_bc1], inc="s_ve")
        t_nwbc = t_bc1e
        t_alc = env.step("vector", lambda e: e.tensor_scalar(
            out=al_row[0:1, 1:2], in0=al_row[0:1, 0:1], scalar1=2.0, scalar2=-0.5,
            op0=ALU.min, op1=ALU.max), waits=[t_al], inc="s_ve")
        t_alh = env.step("vector", lambda e: e.tensor_scalar(
            out=al_row[0:1, 0:1], in0=al_row[0:1, 1:2], scalar1=0.5, scalar2=1.0,
            op0=ALU.mult, op1=ALU.mult), waits=[t_alc], inc="s_ve")
        t_amm = env.step("tensor", lambda e: e.matmul(
            ps_b[:, 0:1], env.consts["ones_f"][:], al_row[0:1, 0:1], start=True, stop=True),
            waits=[t_alh, t_nwbc], inc="s_te")
        t_acol = env.step("vector", lambda e: e.tensor_copy(
            out=alpha_col[:], in_=ps_b[:, 0:1]), waits=[t_amm], inc="s_ve")

        state = {"router_prev": t_acol, "wb": []}

        def apply_fn(tt, par, xt, lnb, c, toks):
            t_r = toks["t_r"]
            r = c[:, 6:7]; m = c[:, 2:3]
            rq = c[:, 7:8]; rr = c[:, 8:9]; srr = c[:, 9:10]; rrms = c[:, 10:11]
            s_n = c[:, 11:12]; s_h = c[:, 12:13]
            t1 = env.step("vector", lambda e, r=r, rq=rq: e.tensor_scalar(
                out=rq, in0=r, scalar1=r, scalar2=-1e-5, op0=ALU.mult, op1=ALU.mult),
                waits=[t_r], inc="s_ve")
            t2 = env.step("vector", lambda e, rq=rq, rr=rr: e.tensor_scalar(
                out=rr, in0=rq, scalar1=1.0 + 1e-6, scalar2=1.0, op0=ALU.add, op1=ALU.mult),
                waits=[t1], inc="s_ve")
            t3 = env.step("scalar", lambda e, rr=rr, srr=srr: e.activation(
                out=srr, in_=rr, func=AF.Sqrt), waits=[t2], inc="s_se")
            t4 = env.step("vector", lambda e, srr=srr, rrms=rrms: e.reciprocal(
                out=rrms, in_=srr), waits=[t3], inc="s_ve")
            t5 = env.step("vector", lambda e, r=r, rrms=rrms, s_n=s_n: e.tensor_scalar(
                out=s_n, in0=r, scalar1=rrms, scalar2=1.0, op0=ALU.mult, op1=ALU.mult),
                waits=[t4], inc="s_ve")
            t6 = env.step("vector", lambda e, r=r, s_h=s_h: e.tensor_scalar(
                out=s_h, in0=r, scalar1=0.5, scalar2=1.0, op0=ALU.mult, op1=ALU.mult),
                waits=[t_r], inc="s_ve")
            t_nr = env.step("vector", lambda e, par=par, m=m, s_n=s_n: e.tensor_scalar(
                out=nrm[par][:], in0=xt[:], scalar1=m, scalar2=s_n,
                op0=ALU.subtract, op1=ALU.mult), waits=[t5], inc="s_ve")
            t_nb = env.step("vector", lambda e, par=par: e.tensor_tensor(
                out=lnb[:], in0=nrm[par][:], in1=nw_bc[:], op=ALU.mult),
                waits=[t_nr, t_nwbc], inc="s_ve")
            t_t1 = env.step("vector", lambda e, par=par, m=m, s_h=s_h: e.tensor_scalar(
                out=nrm[par][:], in0=xt[:], scalar1=m, scalar2=s_h,
                op0=ALU.subtract, op1=ALU.mult), waits=[t6, t_nb], inc="s_ve")
            t_xu = env.step("vector", lambda e, par=par: e.tensor_tensor(
                out=xt[:], in0=xt[:], in1=nrm[par][:], op=ALU.add),
                waits=[t_t1], inc="s_ve")
            t_wb = env.dma("sync", lambda e, tt=tt: e.dma_start(
                out=x_out[tt * 128:(tt + 1) * 128, :], in_=xt[:]), waits=[t_xu])
            state["wb"].append(t_wb)
            return t_nb

        hv = normed_fm.rearrange("p (c t) -> p c t", c=NDC)

        def post_fn(tt, par, t_ev):
            prev = state["router_prev"]
            t_mm = None
            for cch in range(NDC):
                t_mm = env.step("tensor", lambda e, cch=cch, tt=tt: e.matmul(
                    ps_r[:], hv[:, cch, tt * 128:(tt + 1) * 128],
                    router_sb[:, cch * 8:(cch + 1) * 8],
                    start=(cch == 0), stop=(cch == NDC - 1)),
                    waits=[t_ev, t_rt, prev] if cch == 0 else [],
                    inc="s_te" if cch == NDC - 1 else None)
            rm1 = rcol[:, 0:1]; rm2 = rcol[:, 1:2]; w1 = rcol[:, 2:3]; w2 = rcol[:, 3:4]
            t_m1 = env.step("vector", lambda e: e.reduce_max(out=rm1, in_=ps_r[:], axis=AX),
                            waits=[t_mm], inc="s_ve")
            t_e1 = env.step("vector", lambda e: e.tensor_scalar(
                out=eq1[:], in0=ps_r[:], scalar1=rm1, scalar2=1.0,
                op0=ALU.is_equal, op1=ALU.mult), waits=[t_m1], inc="s_ve")
            t_lm = env.step("vector", lambda e: e.scalar_tensor_tensor(
                out=lm[:], in0=eq1[:], scalar=-1e30, in1=ps_r[:], op0=ALU.mult, op1=ALU.add),
                waits=[t_e1], inc="s_ve")
            t_m2 = env.step("vector", lambda e: e.reduce_max(out=rm2, in_=lm[:], axis=AX),
                            waits=[t_lm], inc="s_ve")
            t_d = env.step("vector", lambda e: e.tensor_tensor(
                out=w1, in0=rm1, in1=rm2, op=ALU.subtract), waits=[t_m2], inc="s_ve")
            t_w1 = env.step("scalar", lambda e: e.activation(
                out=w1, in_=w1, func=AF.Sigmoid), waits=[t_d], inc="s_se")
            t_w2 = env.step("vector", lambda e: e.tensor_scalar(
                out=w2, in0=w1, scalar1=-1.0, scalar2=1.0, op0=ALU.mult, op1=ALU.add),
                waits=[t_w1], inc="s_ve")
            t_e2 = env.step("vector", lambda e: e.tensor_scalar(
                out=eq2[:], in0=lm[:], scalar1=rm2, scalar2=1.0,
                op0=ALU.is_equal, op1=ALU.mult), waits=[t_m2], inc="s_ve")
            cslot = combine_tm[:, tt * 8:(tt + 1) * 8]
            t_c1 = env.step("vector", lambda e, cslot=cslot: e.tensor_scalar(
                out=cslot, in0=eq1[:], scalar1=w1, scalar2=1.0, op0=ALU.mult, op1=ALU.mult),
                waits=[t_w1], inc="s_ve")
            t_c2 = env.step("vector", lambda e, cslot=cslot: e.scalar_tensor_tensor(
                out=cslot, in0=eq2[:], scalar=w2, in1=cslot, op0=ALU.mult, op1=ALU.add),
                waits=[t_c1, t_e2, t_w2], inc="s_ve")
            t_c3 = env.step("vector", lambda e, cslot=cslot: e.tensor_scalar(
                out=cslot, in0=cslot, scalar1=alpha_col[:, 0:1], scalar2=1.0,
                op0=ALU.mult, op1=ALU.mult), waits=[t_c2, t_acol], inc="s_ve")
            state["router_prev"] = t_c3

        emit_lnT(env, x_in, normed_fm, waits=env.consts["tok"],
                 apply_fn=apply_fn, post_fn=post_fn)
    env.fence()
    with env.scope():
        fin = emit_ffn_core(
            env, normed_fm, x_out, x_out, 8,
            lambda e, p2: c_wgu[((e * 2 + 0) * 11 + p2) * 128: ((e * 2 + 0) * 11 + p2 + 1) * 128, :],
            lambda e, p2: c_wgu[((e * 2 + 1) * 11 + p2) * 128: ((e * 2 + 1) * 11 + p2 + 1) * 128, :],
            lambda e: c_wd0[e * 128:(e + 1) * 128, :],
            HID=2816, gate_scale=1.0, res_scale=1.0,
            waits=[], cbc_src=combine_tm)
    return fin



def emit_proj_residual(env, attn_all, w_sb, bo_bc, x_in, x_out, waits=()):
    acc = env.sb("pr_acc", [128, 4096], F32)
    x_blk = env.sb("pr_xblk", [128, 4096], F32)
    ps = env.psum("pr_ps", [128, 4096], F32)
    av = attn_all.rearrange("p (c t) -> p c t", c=NDC)
    wv = w_sb.rearrange("p (c j) -> p c j", c=NDC)
    fin = []
    prev_bank_user = [None] * 8
    prev_x_wb = None
    for blk in range(T // 512):
        t_mm = {}
        for doc in range(8):
            mmw = list(waits) if blk == 0 and doc == 0 else []
            if prev_bank_user[doc]:
                mmw.append(prev_bank_user[doc])
            t = None
            for c in range(NDC):
                t = env.step("tensor", lambda e, c=c, doc=doc, blk=blk: e.matmul(
                    ps[:, doc * 512:(doc + 1) * 512],
                    wv[:, c, doc * 128:(doc + 1) * 128],
                    av[:, c, blk * 512:(blk + 1) * 512],
                    start=(c == 0), stop=(c == NDC - 1)),
                    waits=mmw if c == 0 else [], inc="s_te" if c == NDC - 1 else None)
            t_mm[doc] = t
        t_ev = {}
        for doc in range(8):
            t_ev[doc] = env.step("vector", lambda e, doc=doc: e.tensor_copy(
                out=acc[:, doc * 512:(doc + 1) * 512], in_=ps[:, doc * 512:(doc + 1) * 512]),
                waits=[t_mm[doc]], inc="s_ve")
        xw = [prev_x_wb] if prev_x_wb else []
        xv = x_blk.rearrange("p (a d) -> p a d", a=4)
        t_xl = env.dma("sync", lambda e, blk=blk, xv=xv: e.dma_start(
            out=xv[:], in_=x_in[blk * 512:(blk + 1) * 512, :]
            .rearrange("(a p) d -> p a d", p=128)), waits=xw, sem="d_pxl")
        t_tr = {}
        for dt in range(4):
            for doc in range(8):
                trw = [t_ev[b] for b in range(8)] if (dt == 0 and doc == 0) else []
                t_tr[(dt, doc)] = env.step("tensor", lambda e, dt=dt, doc=doc: e.transpose(
                    ps[:, dt * 1024 + doc * 128: dt * 1024 + (doc + 1) * 128],
                    acc[:, doc * 512 + dt * 128: doc * 512 + (dt + 1) * 128],
                    env.consts["idf"][:]),
                    waits=trw, inc="s_te" if doc == 7 else None)
        t_add = {}
        for dt in range(4):
            t_add[dt] = env.step("vector", lambda e, dt=dt: e.scalar_tensor_tensor(
                out=x_blk[:, dt * 1024:(dt + 1) * 1024], in0=ps[:, dt * 1024:(dt + 1) * 1024],
                scalar=1.0, in1=x_blk[:, dt * 1024:(dt + 1) * 1024],
                op0=ALU.mult, op1=ALU.add),
                waits=[t_tr[(dt, 7)], t_xl], inc="s_ve")
            if bo_bc is not None:
                t_add[dt] = env.step("vector", lambda e, dt=dt: e.tensor_tensor(
                    out=x_blk[:, dt * 1024:(dt + 1) * 1024],
                    in0=x_blk[:, dt * 1024:(dt + 1) * 1024], in1=bo_bc[:],
                    op=ALU.add), waits=[t_add[dt]], inc="s_ve")
        for b in range(8):
            prev_bank_user[b] = t_add[3]
        t_wb = env.dma("sync", lambda e, blk=blk, xv=xv: e.dma_start(
            out=x_out[blk * 512:(blk + 1) * 512, :].rearrange("(a p) d -> p a d", p=128),
            in_=xv[:]), waits=[t_add[dt] for dt in range(4)])
        prev_x_wb = t_wb
        fin.append(t_wb)
    return fin


def emit_attn0(env, h_fm, x_in, x_out, c_wq, c_wk, c_wv, c_wo, bo_hbm, waits=()):
    nc = env.nc
    NH, DH = 16, 64
    PADT = 128
    attn_all = env.sb("a0_attn", [128, NDC * T], BF16)
    with env.scope():
        kpad = [env.sb("a0_kpad", [64, T + 2 * PADT], BF16) for _ in range(2)]
        vtm = [env.sb("a0_vtm", [128, (NTT + 2) * DH], BF16) for _ in range(2)]
        qh = [env.sb("a0_qh", [64, T], BF16) for _ in range(2)]
        wslc = [env.sb("a0_wslc", [128, 3 * NDC * DH], BF16) for _ in range(2)]
        vstg = env.sb("a0_vstg", [64, 512], BF16)
        mask = env.sb("a0_mask", [128, 384], F32)
        scf = [env.sb("a0_scf", [128, 384], F32) for _ in range(3)]
        p16 = [env.sb("a0_p16", [128, 384], BF16) for _ in range(3)]
        ptb = [env.sb("a0_ptb", [128, 384], BF16) for _ in range(2)]
        atmp = [env.sb("a0_atmp", [128, DH], BF16) for _ in range(2)]
        colb = [env.sb("a0_col", [128, 4], F32) for _ in range(3)]
        ps_p = env.psum("a0_psp", [64, 512], F32)
        ps_vt = env.psum("a0_psvt", [128, 2 * DH], BF16)
        ps_s = env.psum("a0_pss", [128, 3 * 512], F32)
        ps_pt = env.psum("a0_pspt", [128, 2 * 512], BF16)
        ps_av = env.psum("a0_psav", [128, 2 * DH], F32)
        ps_o = env.psum("a0_pso", [64, 2 * 128], BF16)

        t_i0 = env.step("gpsimd", lambda e: e.iota(mask[:], [[1, 384]], channel_multiplier=-1,
                        allow_small_or_imprecise_dtypes=True), waits=waits, inc="s_po")
        t_i1 = env.step("vector", lambda e: e.tensor_scalar(
            out=scf[0][:], in0=mask[:], scalar1=96.0, scalar2=1.0,
            op0=ALU.is_ge, op1=ALU.mult), waits=[t_i0], inc="s_ve")
        t_i1b = env.step("vector", lambda e: e.tensor_scalar(
            out=scf[1][:], in0=mask[:], scalar1=159.0, scalar2=1.0,
            op0=ALU.is_le, op1=ALU.mult), waits=[t_i0], inc="s_ve")
        t_i2 = env.step("vector", lambda e: e.tensor_tensor(
            out=mask[:], in0=scf[0][:], in1=scf[1][:], op=ALU.mult),
            waits=[t_i1, t_i1b], inc="s_ve")
        t_msk = env.step("vector", lambda e: e.tensor_scalar(
            out=mask[:], in0=mask[:], scalar1=1.0, scalar2=1e30,
            op0=ALU.subtract, op1=ALU.mult), waits=[t_i2], inc="s_ve")

        fin_heads = []
        guards = {"kv": [None, None], "score_done": [None, None], "pss": [None] * 3,
                  "p16": [None] * 3, "ptb": [None, None], "psav": [None, None],
                  "atmp": [None, None], "scf": [t_msk] * 3, "pso": [None, None],
                  "vstg": None, "psp": None, "psvt": [None, None],
                  "attn_done": [None, None]}
        hv = h_fm.rearrange("p (c t) -> p c t", c=NDC)
        hst = {}

        def start_head(h):
            par = h % 2
            ww = [guards["kv"][par]] if guards["kv"][par] else []
            if h < 2:
                ww += list(waits)
            t_lw = env.dma("sync", lambda e, h=h, par=par: e.dma_start(
                out=wslc[par][:, 0:512], in_=c_wq[h * 128:(h + 1) * 128, :]),
                waits=ww, sem=f"d_aq{par}")
            t_lw2 = env.dma("scalar", lambda e, h=h, par=par: e.dma_start(
                out=wslc[par][:, 512:1024], in_=c_wk[h * 128:(h + 1) * 128, :]),
                waits=ww, sem=f"d_ak{par}")
            t_lw3 = env.dma("gpsimd", lambda e, h=h, par=par: e.dma_start(
                out=wslc[par][:, 1024:1536], in_=c_wv[h * 128:(h + 1) * 128, :]),
                waits=ww, sem=f"d_av{par}")
            t_z1 = env.step("vector", lambda e, par=par: e.memset(
                kpad[par][:, 0:PADT], 0.0), waits=ww, inc="s_ve")
            t_z2 = env.step("vector", lambda e, par=par: e.memset(
                kpad[par][:, PADT + T:], 0.0), waits=[t_z1], inc="s_ve")
            t_z3 = env.step("vector", lambda e, par=par: e.memset(
                vtm[par][:, 0:DH], 0.0), waits=[t_z2], inc="s_ve")
            t_z4 = env.step("vector", lambda e, par=par: e.memset(
                vtm[par][:, (NTT + 1) * DH:], 0.0), waits=[t_z3], inc="s_ve")
            hst[h] = dict(lw=[t_lw, t_lw2, t_lw3], kv_last=t_z4)

        def emit_group(h, st, which):
            par = h % 2
            wv3 = wslc[par].rearrange("p (s c j) -> p s c j", s=3, c=NDC)
            mmw = list(hst[h]["lw"])
            if guards["psp"]:
                mmw.append(guards["psp"])
            t_pm = None
            for c in range(NDC):
                t_pm = env.step("tensor", lambda e, c=c, which=which, st=st, wv3=wv3:
                    e.matmul(ps_p[:],
                             wv3[:, which, c, :],
                             hv[:, c, st * 512:(st + 1) * 512],
                             start=(c == 0), stop=(c == NDC - 1)),
                    waits=mmw if c == 0 else [],
                    inc="s_te" if c == NDC - 1 else None)
            if which == 0:
                t_e = env.step("scalar", lambda e, st=st, par=par: e.activation(
                    out=qh[par][:, st * 512:(st + 1) * 512], in_=ps_p[:], func=AF.Copy),
                    waits=[t_pm], inc="s_se")
            elif which == 1:
                t_e = env.step("scalar", lambda e, st=st, par=par: e.activation(
                    out=kpad[par][:, PADT + st * 512: PADT + (st + 1) * 512],
                    in_=ps_p[:], func=AF.Copy), waits=[t_pm], inc="s_se")
            else:
                vw = [t_pm] + ([guards["vstg"]] if guards["vstg"] else [])
                t_vc = env.step("vector", lambda e: e.tensor_copy(
                    out=vstg[:], in_=ps_p[:]), waits=vw, inc="s_ve")
                for j in range(4):
                    tw = [t_vc] + ([guards["psvt"][0]] if guards["psvt"][0] else [])
                    t_vt = env.step("tensor", lambda e, j=j: e.transpose(
                        ps_vt[:, 0:DH],
                        vstg[:, j * 128:(j + 1) * 128], env.consts["idb"][0:64, 0:64]),
                        waits=tw, inc="s_te")
                    kt = 1 + st * 4 + j
                    t_ve = env.step("vector", lambda e, kt=kt, par=par: e.tensor_copy(
                        out=vtm[par][:, kt * DH:(kt + 1) * DH],
                        in_=ps_vt[:, 0:DH]),
                        waits=[t_vt], inc="s_ve")
                    guards["psvt"][0] = t_ve
                guards["vstg"] = t_ve
                t_e = t_ve
            guards["psp"] = t_e
            hst[h]["kv_last"] = t_e

        start_head(0)
        for st in range(8):
            for which in range(3):
                emit_group(0, st, which)

        for h in range(NH):
            par = h % 2
            if h + 1 < NH:
                start_head(h + 1)
                pend = [(st, w) for st in range(8) for w in range(3)]
            else:
                pend = []
            t_kv_last = hst[h]["kv_last"]
            t_attn_last = None
            for qt in range(NTT):
                qpar = qt % 2
                q3 = qt % 3
                p_scf = p_p16 = p_pss = p_colb = q3
                p_ptb = p_pspt = qpar
                smw = [t_kv_last]
                if guards["pss"][p_pss]:
                    smw.append(guards["pss"][p_pss])
                t_sc = env.step("tensor", lambda e, qt=qt, par=par, qpar=p_pss: e.matmul(
                    ps_s[:, qpar * 512: qpar * 512 + 384],
                    qh[par][:, qt * 128:(qt + 1) * 128],
                    kpad[par][:, qt * 128: qt * 128 + 384],
                    start=True, stop=True), waits=smw, inc="s_te")
                if pend:
                    emit_group(h + 1, *pend.pop(0))
                mw = [t_sc, guards["scf"][p_scf]] if guards["scf"][p_scf] else [t_sc]
                t_ms = env.step("vector", lambda e, qs=p_scf, qp=p_pss: e.scalar_tensor_tensor(
                    out=scf[qs][:], in0=ps_s[:, qp * 512: qp * 512 + 384],
                    scalar=0.125, in1=mask[:], op0=ALU.mult, op1=ALU.add),
                    waits=mw, inc="s_ve")
                guards["pss"][p_pss] = t_ms
                c4 = colb[p_colb]
                t_rm = env.step("vector", lambda e, qs=p_scf, c4=c4: e.reduce_max(
                    out=c4[:, 0:1], in_=scf[qs][:], axis=AX, negate=True),
                    waits=[t_ms], inc="s_ve")
                pw = [t_rm] + ([guards["p16"][p_p16]] if guards["p16"][p_p16] else [])
                t_ex = env.step("scalar", lambda e, qs=p_scf, qx=p_p16, c4=c4: e.activation(
                    out=p16[qx][:], in_=scf[qs][:], func=AF.Exp, bias=c4[:, 0:1],
                    accum_out=c4[:, 2:3]), waits=pw, inc="s_se")
                guards["scf"][p_scf] = t_ex
                t_rc = env.step("vector", lambda e, c4=c4: e.reciprocal(
                    out=c4[:, 3:4], in_=c4[:, 2:3]), waits=[t_ex], inc="s_ve")
                t_ptr = None
                for j in range(3):
                    tw = [t_ex]
                    if j == 0 and guards["ptb"][p_ptb]:
                        tw.append(guards["ptb"][p_ptb])
                    t_ptr = env.step("tensor", lambda e, j=j, qx=p_p16, qz=p_pspt: e.transpose(
                        ps_pt[:, qz * 512 + j * 128: qz * 512 + (j + 1) * 128],
                        p16[qx][:, j * 128:(j + 1) * 128], env.consts["idb"][:]),
                        waits=tw, inc="s_te" if j == 2 else None)
                guards["p16"][p_p16] = t_ptr
                t_pte = env.step("scalar", lambda e, qb=p_ptb, qz=p_pspt: e.activation(
                    out=ptb[qb][:], in_=ps_pt[:, qz * 512: qz * 512 + 384], func=AF.Copy),
                    waits=[t_ptr], inc="s_se")
                t_av = None
                for j in range(3):
                    avw = [t_pte]
                    if j == 0 and guards["psav"][qpar]:
                        avw.append(guards["psav"][qpar])
                    t_av = env.step("tensor", lambda e, j=j, qt=qt, par=par, qb=p_ptb, qpar=qpar:
                        e.matmul(ps_av[:, qpar * DH:(qpar + 1) * DH],
                                 ptb[qb][:, j * 128:(j + 1) * 128],
                                 vtm[par][:, (qt + j) * DH:(qt + j + 1) * DH],
                                 start=(j == 0), stop=(j == 2)),
                        waits=avw, inc="s_te" if j == 2 else None)
                guards["ptb"][p_ptb] = t_av
                nw_ = [t_av, t_rc] + ([guards["atmp"][qpar]] if guards["atmp"][qpar] else [])
                t_no = env.step("vector", lambda e, qpar=qpar, c4=c4: e.tensor_scalar(
                    out=atmp[qpar][:], in0=ps_av[:, qpar * DH:(qpar + 1) * DH],
                    scalar1=c4[:, 3:4], scalar2=1.0, op0=ALU.mult, op1=ALU.mult),
                    waits=nw_, inc="s_ve")
                guards["psav"][qpar] = t_no
                tw = [t_no]
                if guards["pso"][qpar]:
                    tw.append(guards["pso"][qpar])
                t_otr = env.step("tensor", lambda e, qpar=qpar: e.transpose(
                    ps_o[:, qpar * 128:(qpar + 1) * 128], atmp[qpar][:],
                    env.consts["idb"][:]), waits=tw, inc="s_te")
                guards["atmp"][qpar] = t_otr
                cch = (h * DH) // 128
                off = (h * DH) % 128
                t_oev = env.step("vector", lambda e, qpar=qpar, qt=qt, cch=cch, off=off:
                    e.tensor_copy(
                        out=attn_all[off:off + DH, cch * T + qt * 128: cch * T + (qt + 1) * 128],
                        in_=ps_o[0:DH, qpar * 128:(qpar + 1) * 128]),
                    waits=[t_otr], inc="s_ve")
                guards["pso"][qpar] = t_oev
                t_attn_last = t_oev
            guards["kv"][par] = t_attn_last
            fin_heads.append(t_attn_last)
    env.fence()
    with env.scope():
        wo_sb = env.sb("a0_wo", [128, NDC * D], BF16)
        bo_row = env.sb("a0_borow", [1, D], F32)
        bo_bc = env.sb("a0_bobc", [128, D], F32)
        with env.scope():
            ps_b = env.psum("a0_psb", [128, 512], F32)
            wov = wo_sb.rearrange("p (c j) -> p c j", c=NDC)
            t_lwo = env.dma("sync", lambda e: e.dma_start(
                out=wov[:], in_=c_wo.rearrange("(c p) j -> p c j", p=128)), sem="d_wo")
            t_lbo = env.dma("sync", lambda e: e.dma_start(out=bo_row[:], in_=bo_hbm[:]),
                            sem="d_bo")
            t_bce = None
            for half in range(2):
                bw = [t_lbo] if half == 0 else [t_bce]
                t_bcm = env.step("tensor", lambda e, half=half: e.matmul(
                    ps_b[:], env.consts["ones_f"][:], bo_row[0:1, half * 512:(half + 1) * 512],
                    start=True, stop=True), waits=bw, inc="s_te")
                t_bce = env.step("vector", lambda e, half=half: e.tensor_copy(
                    out=bo_bc[:, half * 512:(half + 1) * 512], in_=ps_b[:]),
                    waits=[t_bcm], inc="s_ve")
        env.fence()
        fin = emit_proj_residual(env, attn_all, wo_sb, bo_bc, x_in, x_out,
                                 waits=[t_lwo, t_bce])
    return fin


def emit_attn1(env, h_fm, x_in, x_out, c_wqP, c_wkP, c_wvP, c_wproj, waits=()):
    nc = env.nc
    NH, DH = 16, 64
    attn_all = env.sb("a1_attn", [128, NDC * T], BF16)
    with env.scope():
        kh = [env.sb("a1_kh", [64, T], BF16) for _ in range(2)]
        vtm = [env.sb("a1_vtm", [128, NTT * 65], BF16) for _ in range(2)]
        qh = [env.sb("a1_qh", [64, T], BF16) for _ in range(2)]
        wslc = [env.sb("a1_wslc", [128, 3 * NDC * DH], BF16) for _ in range(2)]
        vstg = env.sb("a1_vstg", [64, 512], BF16)
        sq = env.sb("a1_sq", [64, T], BF16)
        ptile = [env.sb("a1_pt", [128, 1024], BF16) for _ in range(2)]
        cmask = env.sb("a1_cmask", [128, 4 * 512], BF16)
        mskf = env.sb("a1_mskf", [128, 512], F32)
        nmax = env.sb("a1_nmax", [1, 24], F32)
        rrow = env.sb("a1_rrow", [1, 512], F32)
        rb_sb = env.sb("a1_rb", [64, 512], F32)
        cneg = env.sb("a1_cneg", [128, 1], F32)
        ps_p = env.psum("a1_psp", [64, 512], F32)
        ps_vt = env.psum("a1_psvt", [128, 2 * DH], BF16)
        ps_s = env.psum("a1_pss", [128, 4 * 512], F32)
        ps_av = env.psum("a1_psav", [65, 512], F32)
        ps_rb = env.psum("a1_psrb", [128, 512], F32)
        ps_n = ps_rb

        t_i0 = env.step("gpsimd", lambda e: e.iota(mskf[:], [[1, 512]], channel_multiplier=-1,
                        allow_small_or_imprecise_dtypes=True), waits=waits, inc="s_po")
        t_cm = None
        for r in range(4):
            t_c1 = env.step("vector", lambda e, r=r: e.tensor_scalar(
                out=cmask[:, r * 512:(r + 1) * 512], in0=mskf[:], scalar1=float(128 * r),
                scalar2=1.0, op0=ALU.is_ge, op1=ALU.mult), waits=[t_i0], inc="s_ve")
            t_cm = t_c1
        t_vo = env.step("vector", lambda e: e.memset(
            vtm[0].rearrange("p (k c) -> p k c", c=65)[:, :, 64:65], 1.0), inc="s_ve")
        t_vo2 = env.step("vector", lambda e: e.memset(
            vtm[1].rearrange("p (k c) -> p k c", c=65)[:, :, 64:65], 1.0), inc="s_ve")

        hv = h_fm.rearrange("p (c t) -> p c t", c=NDC)
        fin_heads = []
        guards = {"kv": [t_vo, t_vo2], "psp": None, "psvt": [None, None], "vstg": None,
                  "sq": None, "psn": None, "ptile": [None, None], "psav": None,
                  "pss": [None, None], "rrow": None, "rb": None, "cneg": None,
                  "nmax": None}
        hst = {}

        def start_head(h):
            par = h % 2
            ww = [guards["kv"][par]]
            if h < 2:
                ww += list(waits)
            t_lw = env.dma("sync", lambda e, h=h, par=par: e.dma_start(
                out=wslc[par][:, 0:512], in_=c_wqP[h * 128:(h + 1) * 128, :]),
                waits=ww, sem=f"d_aq{par}")
            t_lw2 = env.dma("scalar", lambda e, h=h, par=par: e.dma_start(
                out=wslc[par][:, 512:1024], in_=c_wkP[h * 128:(h + 1) * 128, :]),
                waits=ww, sem=f"d_ak{par}")
            t_lw3 = env.dma("gpsimd", lambda e, h=h, par=par: e.dma_start(
                out=wslc[par][:, 1024:1536], in_=c_wvP[h * 128:(h + 1) * 128, :]),
                waits=ww, sem=f"d_av{par}")
            hst[h] = dict(lw=[t_lw, t_lw2, t_lw3], kv_last=None)

        def emit_group(h, st, which):
            par = h % 2
            wv3 = wslc[par].rearrange("p (s c j) -> p s c j", s=3, c=NDC)
            mmw = list(hst[h]["lw"])
            if guards["psp"]:
                mmw.append(guards["psp"])
            t_pm = None
            for c in range(NDC):
                t_pm = env.step("tensor", lambda e, c=c, which=which, st=st, wv3=wv3:
                    e.matmul(ps_p[:], wv3[:, which, c, :],
                             hv[:, c, st * 512:(st + 1) * 512],
                             start=(c == 0), stop=(c == NDC - 1)),
                    waits=mmw if c == 0 else [],
                    inc="s_te" if c == NDC - 1 else None)
            if which == 0:
                t_e = env.step("vector", lambda e, st=st, par=par: e.tensor_copy(
                    out=qh[par][:, st * 512:(st + 1) * 512], in_=ps_p[:]),
                    waits=[t_pm], inc="s_ve")
            elif which == 1:
                t_e = env.step("vector", lambda e, st=st, par=par: e.tensor_copy(
                    out=kh[par][:, st * 512:(st + 1) * 512], in_=ps_p[:]),
                    waits=[t_pm], inc="s_ve")
            else:
                vw = [t_pm] + ([guards["vstg"]] if guards["vstg"] else [])
                t_vc = env.step("vector", lambda e: e.tensor_copy(
                    out=vstg[:], in_=ps_p[:]), waits=vw, inc="s_ve")
                for j in range(4):
                    tw = [t_vc] + ([guards["psvt"][0]] if guards["psvt"][0] else [])
                    t_vt = env.step("tensor", lambda e, j=j: e.transpose(
                        ps_vt[:, 0:DH], vstg[:, j * 128:(j + 1) * 128],
                        env.consts["idb"][0:64, 0:64]), waits=tw, inc="s_te")
                    kt = st * 4 + j
                    t_ve = env.step("vector", lambda e, kt=kt, par=par: e.tensor_copy(
                        out=vtm[par][:, kt * 65: kt * 65 + DH],
                        in_=ps_vt[:, 0:DH]),
                        waits=[t_vt], inc="s_ve")
                    guards["psvt"][0] = t_ve
                guards["vstg"] = t_ve
                t_e = t_ve
            guards["psp"] = t_e
            hst[h]["kv_last"] = t_e

        start_head(0)
        for st in range(8):
            for which in range(3):
                emit_group(0, st, which)

        for h in range(NH):
            par = h % 2
            t_kv_last = hst[h]["kv_last"]
            t_nm_last = None
            for which in range(2):
                src_b = qh[par] if which == 0 else kh[par]
                sw = [t_kv_last] + ([guards["sq"]] if guards["sq"] else [])
                t_sqr = env.step("scalar", lambda e, src_b=src_b: e.activation(
                    out=sq[:], in_=src_b[:], func=AF.Square), waits=sw, inc="s_se")
                for j in range(8):
                    nw_ = [t_sqr]
                    if guards["psn"]:
                        nw_.append(guards["psn"])
                    if j == 0 and which == 0 and guards["rb"]:
                        nw_.append(guards["rb"])
                    t_nmm = env.step("tensor", lambda e, j=j: e.matmul(
                        ps_n[0:1, :], env.consts["ocb"][0:64, 0:1], sq[:, j * 512:(j + 1) * 512],
                        start=True, stop=True), waits=nw_, inc="s_te")
                    nmw = [t_nmm] + ([guards["nmax"]] if guards["nmax"] and j == 0 and which == 0 else [])
                    t_nr = env.step("vector", lambda e, j=j, which=which: e.reduce_max(
                        out=nmax[0:1, which * 8 + j: which * 8 + j + 1], in_=ps_n[0:1, :],
                        axis=AX), waits=nmw, inc="s_ve")
                    guards["psn"] = t_nr
                    t_nm_last = t_nr
                guards["sq"] = t_nm_last
            t_mq = env.step("vector", lambda e: e.reduce_max(
                out=nmax[0:1, 16:17], in_=nmax[0:1, 0:8], axis=AX), waits=[t_nm_last], inc="s_ve")
            t_mk = env.step("vector", lambda e: e.reduce_max(
                out=nmax[0:1, 17:18], in_=nmax[0:1, 8:16], axis=AX), waits=[t_mq], inc="s_ve")
            t_c2 = env.step("vector", lambda e: e.tensor_tensor(
                out=nmax[0:1, 18:19], in0=nmax[0:1, 16:17], in1=nmax[0:1, 17:18],
                op=ALU.mult), waits=[t_mk], inc="s_ve")
            t_cs = env.step("scalar", lambda e: e.activation(
                out=nmax[0:1, 19:20], in_=nmax[0:1, 18:19], func=AF.Sqrt),
                waits=[t_c2], inc="s_se")
            t_cn = env.step("vector", lambda e: e.tensor_scalar(
                out=nmax[0:1, 20:21], in0=nmax[0:1, 19:20], scalar1=-0.125, scalar2=1.0,
                op0=ALU.mult, op1=ALU.mult), waits=[t_cs], inc="s_ve")
            cbw = [t_cn] + ([guards["cneg"]] if guards["cneg"] else [])
            t_cb = env.step("tensor", lambda e: e.matmul(
                ps_rb[:, 0:1], env.consts["ones_f"][:], nmax[0:1, 20:21],
                start=True, stop=True), waits=cbw, inc="s_te")
            guards["nmax"] = t_cb
            t_cv = env.step("vector", lambda e: e.tensor_copy(
                out=cneg[:], in_=ps_rb[:, 0:1]), waits=[t_cb], inc="s_ve")
            if h + 1 < NH:
                start_head(h + 1)
                pend = [(st, w) for st in range(8) for w in range(3)]
            else:
                pend = []
            gctr = 0
            sctok = [None, None]

            def emit_S(qb, mpair, t_kv_last=t_kv_last, par=par):
                sp2 = mpair % 2
                smw = [t_kv_last]
                if guards["pss"][sp2]:
                    smw.append(guards["pss"][sp2])
                t_sm = None
                for sub in range(2):
                    kc = 2 * mpair + sub
                    t_sm = env.step("tensor", lambda e, kc=kc, qb=qb, par=par, sp2=sp2, sub=sub:
                        e.matmul(ps_s[:, sp2 * 1024 + sub * 512: sp2 * 1024 + (sub + 1) * 512],
                                 kh[par][:, kc * 128:(kc + 1) * 128],
                                 qh[par][:, qb * 512:(qb + 1) * 512],
                                 start=True, stop=True),
                        waits=smw if sub == 0 else [], inc="s_te" if sub == 1 else None)
                sctok[sp2] = t_sm

            t_attn_last = None
            emit_S(0, 0)
            for qb in range(8):
                nkc = 4 * (qb + 1)
                t_av = None
                for mpair in range(nkc // 2):
                    sp2 = mpair % 2
                    if mpair + 1 < nkc // 2:
                        emit_S(qb, mpair + 1)
                    elif qb + 1 < 8:
                        emit_S(qb + 1, 0)
                    if pend and gctr % 3 == 0:
                        emit_group(h + 1, *pend.pop(0))
                    gctr += 1
                    t_sm = sctok[sp2]
                    pw = [t_sm, t_cv] + ([guards["ptile"][sp2]] if guards["ptile"][sp2] else [])
                    t_ex = env.step("scalar", lambda e, sp2=sp2: e.activation(
                        out=ptile[sp2][:], in_=ps_s[:, sp2 * 1024:(sp2 + 1) * 1024],
                        func=AF.Exp, scale=0.125, bias=cneg[:, 0:1]),
                        waits=pw, inc="s_se")
                    guards["pss"][sp2] = t_ex
                    for sub in range(2):
                        kc = 2 * mpair + sub
                        r = kc - 4 * qb
                        t_in = t_ex
                        if r >= 0:
                            t_in = env.step("vector", lambda e, sp2=sp2, sub=sub, r=r: e.tensor_tensor(
                                out=ptile[sp2][:, sub * 512:(sub + 1) * 512],
                                in0=ptile[sp2][:, sub * 512:(sub + 1) * 512],
                                in1=cmask[:, r * 512:(r + 1) * 512], op=ALU.mult),
                                waits=[t_ex, t_cm], inc="s_ve")
                        avw = [t_in]
                        if kc == 0 and guards["psav"]:
                            avw.append(guards["psav"])
                        t_av = env.step("tensor", lambda e, kc=kc, par=par, sp2=sp2, sub=sub, nkc=nkc:
                            e.matmul(ps_av[:], vtm[par][:, kc * 65:(kc + 1) * 65],
                                     ptile[sp2][:, sub * 512:(sub + 1) * 512],
                                     start=(kc == 0), stop=(kc == nkc - 1)),
                            waits=avw, inc="s_te" if sub == 1 else None)
                    guards["ptile"][sp2] = t_av
                dw = [t_av] + ([guards["rrow"]] if guards["rrow"] else [])
                t_l1 = env.step("scalar", lambda e: e.activation(
                    out=rrow[:], in_=ps_av[64:65, :], func=AF.Ln), waits=dw, inc="s_se")
                t_l2 = env.step("scalar", lambda e: e.activation(
                    out=rrow[:], in_=rrow[:], func=AF.Exp, scale=-1.0),
                    waits=[t_l1], inc="s_se")
                rbw = [t_l2, t_cv] + ([guards["rb"]] if guards["rb"] else [])
                t_rb = env.step("tensor", lambda e: e.matmul(
                    ps_rb[:], env.consts["ones_f"][:], rrow[:],
                    start=True, stop=True), waits=rbw, inc="s_te")
                t_rv = env.step("vector", lambda e: e.tensor_copy(
                    out=rb_sb[:], in_=ps_rb[0:64, :]), waits=[t_rb], inc="s_ve")
                guards["cneg"] = t_rv
                guards["rrow"] = t_rv
                cch = (h * DH) // 128
                off = (h * DH) % 128
                t_oev = env.step("vector", lambda e, qb=qb, cch=cch, off=off: e.tensor_tensor(
                    out=attn_all[off:off + DH, cch * T + qb * 512: cch * T + (qb + 1) * 512],
                    in0=ps_av[0:DH, :], in1=rb_sb[:], op=ALU.mult),
                    waits=[t_rv], inc="s_ve")
                guards["psav"] = t_oev
                guards["rb"] = t_oev
                t_attn_last = t_oev
            guards["kv"][par] = t_attn_last
            fin_heads.append(t_attn_last)
    env.fence()
    with env.scope():
        wp_sb = env.sb("a1_wp", [128, NDC * D], BF16)
        wpv = wp_sb.rearrange("p (c j) -> p c j", c=NDC)
        t_lwp = env.dma("sync", lambda e: e.dma_start(
            out=wpv[:], in_=c_wproj.rearrange("(c p) j -> p c j", p=128)), sem="d_wo")
        fin = emit_proj_residual(env, attn_all, wp_sb, None, x_in, x_out, waits=[t_lwp])
    return fin



def _pre_lhsT(wT, np2):
    a = wT.reshape(8, 128, np2, 2, 128).transpose(2, 1, 0, 3, 4)
    return np.ascontiguousarray(a).reshape(np2 * 128, 2048)


def _pre_down(dw_e, nhc):
    a = dw_e.reshape(4, 2, 128, nhc, 128).transpose(4, 0, 1, 3, 2)
    return np.ascontiguousarray(a).reshape(128, 4 * 2 * nhc * 128)


def _pre_head(wT):
    a = wT.reshape(8, 128, 16, 64).transpose(2, 1, 0, 3)
    return np.ascontiguousarray(a).reshape(2048, 512)


def _prep(inputs):
    g = {}
    f32 = np.float32
    gu_t = np.asarray(inputs["L0_ffn_gate_up_weight"], f32).transpose(0, 2, 1)
    g["wguP"] = np.ascontiguousarray(np.concatenate(
        [_pre_lhsT(np.ascontiguousarray(gu_t[e, :, half * 2816:(half + 1) * 2816]), 11)
         for e in range(8) for half in range(2)], axis=0))
    dw = np.asarray(inputs["L0_ffn_down_weight"], f32)
    g["wdP"] = np.ascontiguousarray(np.concatenate(
        [_pre_down(dw[e], 22) for e in range(8)], axis=0))
    g["routerT"] = np.ascontiguousarray(np.asarray(inputs["L0_ffn_router_weight"], f32).T)
    g["nw0"] = np.asarray(inputs["L0_ffn_norm_weight"], f32).reshape(1, D)
    g["alpha"] = np.asarray(inputs["L0_ffn_residual_alpha"], f32).reshape(1, 1)
    g["wqP"] = _pre_head(np.ascontiguousarray(np.asarray(inputs["L0_attn_Wq"], f32).T))
    g["wkP"] = _pre_head(np.ascontiguousarray(np.asarray(inputs["L0_attn_Wk"], f32).T))
    g["wvP"] = _pre_head(np.ascontiguousarray(np.asarray(inputs["L0_attn_Wv"], f32).T))
    g["wo0T"] = np.ascontiguousarray(np.asarray(inputs["L0_attn_Wo"], f32).T)
    g["bo0"] = np.asarray(inputs["L0_attn_bo"], f32).reshape(1, D)
    wqkvT = np.ascontiguousarray(np.asarray(inputs["L1_attn_w_qkv"], f32).T)
    g["q1P"] = _pre_head(np.ascontiguousarray(wqkvT[:, 0:D]))
    g["k1P"] = _pre_head(np.ascontiguousarray(wqkvT[:, D:2 * D]))
    g["v1P"] = _pre_head(np.ascontiguousarray(wqkvT[:, 2 * D:3 * D]))
    g["wprojT"] = np.ascontiguousarray(np.asarray(inputs["L1_attn_w_proj"], f32).T)
    g["wg1P"] = _pre_lhsT(np.ascontiguousarray(np.asarray(inputs["L1_ffn_w_gate"], np.int32).T), 12)
    g["wu1P"] = _pre_lhsT(np.ascontiguousarray(np.asarray(inputs["L1_ffn_w_up"], np.int32).T), 12)
    g["wd1P"] = _pre_down(np.asarray(inputs["L1_ffn_w_down"], np.int32), 24)
    xs = [np.ascontiguousarray(np.asarray(inputs["L0_attn_x"][i], f32)) for i in range(8)]
    return g, xs


def build_full(dbg=False):
    nc = bass.Bass("TRN2", target_bir_lowering=False, debug=False)
    env = Env(nc)
    f32in = {}
    for name, shape in [
        ("x0", [T, D]), ("wguP", [8 * 2 * 11 * 128, 2048]), ("wdP", [8 * 128, 22 * 1024]),
        ("routerT", [D, 8]), ("nw0", [1, D]), ("alpha", [1, 1]),
        ("wqP", [2048, 512]), ("wkP", [2048, 512]), ("wvP", [2048, 512]),
        ("wo0T", [D, D]), ("bo0", [1, D]),
        ("q1P", [2048, 512]), ("k1P", [2048, 512]), ("v1P", [2048, 512]),
        ("wprojT", [D, D]),
    ]:
        f32in[name] = nc.dram_tensor(name, shape, F32, kind="ExternalInput").ap()
    i32in = {}
    for name, shape in [("wg1P", [12 * 128, 2048]), ("wu1P", [12 * 128, 2048]),
                        ("wd1P", [128, 24 * 1024])]:
        i32in[name] = nc.dram_tensor(name, shape, I32, kind="ExternalInput").ap()
    y = nc.dram_tensor("y", [T, D], F32, kind="ExternalOutput").ap()
    xs = nc.dram_tensor("xs", [T, D], F32).ap()
    dbgo = [nc.dram_tensor(f"dbg{i}", [T, D], F32, kind="ExternalOutput").ap()
            for i in range(5)] if dbg else None
    def dump(i):
        if dbg:
            env.dma("gpsimd", lambda e, i=i: e.dma_start(out=dbgo[i][:], in_=xs[:]),
                    waits=env.join_all(), sem="d_dbg")
    c = {}
    for name in ("routerT", "wguP", "wdP", "wqP", "wkP", "wvP", "wo0T", "q1P", "k1P",
                 "v1P", "wprojT"):
        c[name] = nc.dram_tensor("c_" + name, list(f32in[name].shape), BF16).ap()
    for name in ("wg1P", "wu1P", "wd1P"):
        c[name] = nc.dram_tensor("c_" + name, list(i32in[name].shape), BF16).ap()
    with env.es:
        env.consts = emit_consts(env)
        tconv = emit_conv(env, [(f32in[n], c[n]) for n in
                                ("routerT", "wguP", "wdP", "wqP", "wkP", "wvP", "wo0T",
                                 "q1P", "k1P", "v1P", "wprojT")]
                          + [(i32in[n], c[n]) for n in ("wg1P", "wu1P", "wd1P")])
        with env.scope():
            normed_fm = env.sb("normed_fm", [128, NDC * T], BF16)
            combine_tm = env.sb("combine_tm", [128, NTT * 8], F32)
            emit_moe_sublayer(env, f32in["x0"], xs, c["wguP"], c["wdP"],
                              c["routerT"], f32in["nw0"], f32in["alpha"],
                              normed_fm, combine_tm, waits=[tconv[0], tconv[-1]])
        dump(0)
        env.fence()
        with env.scope():
            h_fm = env.sb("h_fm_a0", [128, NDC * T], BF16)
            with env.scope():
                emit_lnT(env, xs, h_fm, waits=env.consts["tok"])
            env.fence()
            emit_attn0(env, h_fm, xs, xs, c["wqP"], c["wkP"], c["wvP"], c["wo0T"],
                       f32in["bo0"], waits=[])
        dump(1)
        env.fence()
        with env.scope():
            normed_fm = env.sb("normed_fm2", [128, NDC * T], BF16)
            combine_tm = env.sb("combine_tm2", [128, NTT * 8], F32)
            emit_moe_sublayer(env, xs, xs, c["wguP"], c["wdP"],
                              c["routerT"], f32in["nw0"], f32in["alpha"],
                              normed_fm, combine_tm, waits=[])
        dump(2)
        env.fence()
        with env.scope():
            h_fm = env.sb("h_fm_f1", [128, NDC * T], BF16)
            with env.scope():
                emit_lnT(env, xs, h_fm, waits=[])
            env.fence()
            emit_ffn_swiglu(env, h_fm, xs, xs, c["wg1P"], c["wu1P"], c["wd1P"],
                            HID=3072, gate_scale=1.0 / 127.0,
                            res_scale=0.5 / (127.0 * 127.0), waits=[])
        dump(3)
        env.fence()
        with env.scope():
            h_fm = env.sb("h_fm_a1", [128, NDC * T], BF16)
            with env.scope():
                emit_lnT(env, xs, h_fm, waits=[])
            env.fence()
            emit_attn1(env, h_fm, xs, xs, c["q1P"], c["k1P"], c["v1P"], c["wprojT"],
                       waits=[])
        dump(4)
        env.fence()
        with env.scope():
            h_fm = env.sb("h_fm_f2", [128, NDC * T], BF16)
            with env.scope():
                emit_lnT(env, xs, h_fm, waits=[])
            env.fence()
            fin = emit_ffn_swiglu(env, h_fm, xs, y, c["wg1P"], c["wu1P"], c["wd1P"],
                                  HID=3072, gate_scale=1.0 / 127.0,
                                  res_scale=0.5 / (127.0 * 127.0), waits=[])
        env.wait_only("gpsimd", fin + env.join_all())
        env.emit()
    names = list(f32in) + list(i32in)
    return nc, names


_FULL_CACHE = []


def kernel_fused(**inputs):
    g, xs_np = _prep(inputs)
    if not _FULL_CACHE:
        _FULL_CACHE.append(build_full())
    nc, innames = _FULL_CACHE[0]
    in_maps = []
    for i in range(8):
        m = {k: g[k] for k in innames if k != "x0"}
        m["x0"] = xs_np[i]
        in_maps.append(m)
    res = run_bass_kernel_spmd(nc, in_maps, core_ids=list(range(8)))
    return np.stack([np.asarray(r["y"]) for r in res.results], axis=0).astype(np.float32)


_PROG_CACHE = {}


def _get_prog(name):
    if name not in _PROG_CACHE:
        _PROG_CACHE[name] = {
            "moe": build_moe_only, "attn0": build_attn0_only,
            "ffn1": build_ffn1_only, "attn1": build_attn1_only,
        }[name]()
    return _PROG_CACHE[name]


def _launch(name, g, xs):
    nc, innames = _get_prog(name)
    in_maps = []
    for i in range(8):
        m = {k: g[k] for k in innames if k != "x0"}
        m["x0"] = xs[i]
        in_maps.append(m)
    res = run_bass_kernel_spmd(nc, in_maps, core_ids=list(range(8)))
    return [np.asarray(r["y"]) for r in res.results]


def kernel(**inputs):
    return kernel_fused(**inputs)


def build_moe_only():
    nc = bass.Bass("TRN2", target_bir_lowering=False, debug=False)
    env = Env(nc)
    x0 = nc.dram_tensor("x0", [T, D], F32, kind="ExternalInput").ap()
    y = nc.dram_tensor("y", [T, D], F32, kind="ExternalOutput").ap()
    wgu = nc.dram_tensor("wgu0T", [8 * D, 5632], F32, kind="ExternalInput").ap()
    wd0 = nc.dram_tensor("wd0T", [8 * 2816, D], F32, kind="ExternalInput").ap()
    router = nc.dram_tensor("routerT", [D, 8], F32, kind="ExternalInput").ap()
    nw = nc.dram_tensor("nw0", [1, D], F32, kind="ExternalInput").ap()
    alpha = nc.dram_tensor("alpha", [1, 1], F32, kind="ExternalInput").ap()
    c_wgu = nc.dram_tensor("c_wgu", [8 * D, 5632], BF16).ap()
    c_wd0 = nc.dram_tensor("c_wd0", [8 * 2816, D], BF16).ap()
    with env.es:
        env.consts = emit_consts(env)
        tconv = emit_conv(env, [(wgu, c_wgu), (wd0, c_wd0)])
        normed_fm = env.sb("normed_fm", [128, NDC * T], BF16)
        combine_tm = env.sb("combine_tm", [128, NTT * 8], F32)
        fin = emit_moe_sublayer(env, x0, y, c_wgu, c_wd0, router, nw, alpha,
                                normed_fm, combine_tm, waits=[tconv[-1]])
        env.wait_only("gpsimd", fin)
        env.emit()
    return nc, ["x0", "wgu0T", "wd0T", "routerT", "nw0", "alpha"]


def build_attn0_only():
    nc = bass.Bass("TRN2", target_bir_lowering=False, debug=False)
    env = Env(nc)
    x0 = nc.dram_tensor("x0", [T, D], F32, kind="ExternalInput").ap()
    y = nc.dram_tensor("y", [T, D], F32, kind="ExternalOutput").ap()
    wq = nc.dram_tensor("wqP", [2048, 512], F32, kind="ExternalInput").ap()
    wk = nc.dram_tensor("wkP", [2048, 512], F32, kind="ExternalInput").ap()
    wv = nc.dram_tensor("wvP", [2048, 512], F32, kind="ExternalInput").ap()
    wo = nc.dram_tensor("wo0T", [D, D], F32, kind="ExternalInput").ap()
    bo = nc.dram_tensor("bo0", [1, D], F32, kind="ExternalInput").ap()
    cs = {n: nc.dram_tensor("c_" + n, list(ap.shape), BF16).ap()
          for n, ap in (("wqP", wq), ("wkP", wk), ("wvP", wv), ("wo0T", wo))}
    with env.es:
        env.consts = emit_consts(env)
        tconv = emit_conv(env, [(wq, cs["wqP"]), (wk, cs["wkP"]), (wv, cs["wvP"]),
                                (wo, cs["wo0T"])])
        h_fm = env.sb("h_fm", [128, NDC * T], BF16)
        with env.scope():
            tln = emit_lnT(env, x0, h_fm, waits=env.consts["tok"])
        env.fence()
        with env.scope():
            fin = emit_attn0(env, h_fm, x0, y, cs["wqP"], cs["wkP"], cs["wvP"],
                             cs["wo0T"], bo, waits=[tconv[-1]])
        env.wait_only("gpsimd", fin)
        env.emit()
    return nc, ["x0", "wqP", "wkP", "wvP", "wo0T", "bo0"]


def build_attn1_only():
    nc = bass.Bass("TRN2", target_bir_lowering=False, debug=False)
    env = Env(nc)
    x0 = nc.dram_tensor("x0", [T, D], F32, kind="ExternalInput").ap()
    y = nc.dram_tensor("y", [T, D], F32, kind="ExternalOutput").ap()
    qP = nc.dram_tensor("q1P", [2048, 512], F32, kind="ExternalInput").ap()
    kP = nc.dram_tensor("k1P", [2048, 512], F32, kind="ExternalInput").ap()
    vP = nc.dram_tensor("v1P", [2048, 512], F32, kind="ExternalInput").ap()
    wproj = nc.dram_tensor("wprojT", [D, D], F32, kind="ExternalInput").ap()
    cs = {n: nc.dram_tensor("c_" + n, list(ap.shape), BF16).ap()
          for n, ap in (("q1P", qP), ("k1P", kP), ("v1P", vP), ("wprojT", wproj))}
    with env.es:
        env.consts = emit_consts(env)
        tconv = emit_conv(env, [(qP, cs["q1P"]), (kP, cs["k1P"]), (vP, cs["v1P"]),
                                (wproj, cs["wprojT"])])
        h_fm = env.sb("h_fm", [128, NDC * T], BF16)
        with env.scope():
            tln = emit_lnT(env, x0, h_fm, waits=env.consts["tok"])
        env.fence()
        with env.scope():
            fin = emit_attn1(env, h_fm, x0, y, cs["q1P"], cs["k1P"], cs["v1P"],
                             cs["wprojT"], waits=[tconv[-1]])
        env.wait_only("gpsimd", fin)
        env.emit()
    return nc, ["x0", "q1P", "k1P", "v1P", "wprojT"]


def build_ffn1_only():
    nc = bass.Bass("TRN2", target_bir_lowering=False, debug=False)
    env = Env(nc)
    x0 = nc.dram_tensor("x0", [T, D], F32, kind="ExternalInput").ap()
    y = nc.dram_tensor("y", [T, D], F32, kind="ExternalOutput").ap()
    wg1 = nc.dram_tensor("wg1P", [12 * 128, 2048], I32, kind="ExternalInput").ap()
    wu1 = nc.dram_tensor("wu1P", [12 * 128, 2048], I32, kind="ExternalInput").ap()
    wd1 = nc.dram_tensor("wd1P", [128, 24576], I32, kind="ExternalInput").ap()
    cg = nc.dram_tensor("c_wg1P", [12 * 128, 2048], BF16).ap()
    cu = nc.dram_tensor("c_wu1P", [12 * 128, 2048], BF16).ap()
    cd = nc.dram_tensor("c_wd1P", [128, 24576], BF16).ap()
    with env.es:
        env.consts = emit_consts(env)
        tconv = emit_conv(env, [(wg1, cg), (wu1, cu), (wd1, cd)])
        h_fm = env.sb("h_fm", [128, NDC * T], BF16)
        with env.scope():
            tln = emit_lnT(env, x0, h_fm, waits=env.consts["tok"])
        env.fence()
        with env.scope():
            tf = emit_ffn_swiglu(env, h_fm, x0, y, cg, cu, cd, HID=3072,
                                 gate_scale=1.0 / 127.0, res_scale=0.5 / (127.0 * 127.0),
                                 waits=[tconv[-1]])
        env.wait_only("gpsimd", tf)
        env.emit()
    return nc, ["x0", "wg1P", "wu1P", "wd1P"]


